# version 06
# baseline (speedup 1.0000x reference)
import numpy as np

import concourse.bass as bass
import concourse.mybir as mybir
from concourse.bass_utils import run_bass_kernel_spmd

F32 = mybir.dt.float32
BF16 = mybir.dt.bfloat16
I32 = mybir.dt.int32
AF = mybir.ActivationFunctionType
ALU = mybir.AluOpType
AX = mybir.AxisListType

D = 1024
HEADS = 16
DH = 64
M = 256
DFF = 4096
NCH = 8
EPS_K = 1e-4
LN_EPS = 1e-5
NORM = DH ** -0.25
LN2 = float(np.log(2.0))

QUEUES = ("sd", "gp", "pe", "act", "dve")
ENG_OF = {"sd": "sync", "gp": "gpsimd", "pe": "tensor", "act": "scalar", "dve": "vector"}


SEMS = ("sd", "gp", "gpd", "cc", "pe", "act", "dve")


class Prog:
    def __init__(self):
        self.ops = []
        self.count = {s: 0 for s in SEMS}
        self.bufs = {}

    def buf(self, name):
        if name not in self.bufs:
            self.bufs[name] = {"w": [], "r": []}
        return self.bufs[name]

    def op(self, q, emit, reads=(), writes=(), extra=(), dma=False, cc=False):
        waits = [w for w in extra if w is not None]
        for b in reads:
            waits += self.buf(b)["w"]
        for b in writes:
            waits += self.buf(b)["r"] + self.buf(b)["w"]
        if cc:
            sem, step = "cc", 1
        elif dma:
            sem, step = "D_" + (writes[0] if writes else "misc"), 16
        else:
            sem, step = q, 1
        if sem not in self.count:
            self.count[sem] = 0
        self.count[sem] += step
        tick = (sem, self.count[sem])
        uniq = {}
        for (ws, wv) in waits:
            if ws == sem and wv >= tick[1]:
                raise RuntimeError(f"self-deadlock {sem} {wv} {tick}")
            uniq[ws] = max(uniq.get(ws, 0), wv)
        self.ops.append((q, emit, [(k, v) for k, v in uniq.items()], tick, step))
        for b in reads:
            self.buf(b)["r"].append(tick)
        for b in writes:
            self.bufs[b] = {"w": [tick], "r": []}
        return tick

    def merge(self, dst, srcs):
        d = self.buf(dst)
        for s in srcs:
            b = self.buf(s)
            d["w"] = d["w"] + b["w"]
            d["r"] = d["r"] + b["r"]

    def emit(self, blk, sems):
        needed = set()
        for _, _, waits, _, _ in self.ops:
            needed.update(waits)
        per_q = {q: [] for q in QUEUES}
        semq = {}
        for (q, emit_fn, waits, tick, step) in self.ops:
            per_q[q].append((emit_fn, waits, tick, step))
            assert semq.setdefault(tick[0], q) == q, f"sem {tick[0]} on two queues"
        last_inc = {s: 0 for s in self.count}

        def make_body(q):
            ops = per_q[q]

            def body(eng):
                waited = {s: 0 for s in self.count}
                for (emit_fn, waits, tick, step) in ops:
                    for (ws, wv) in waits:
                        if wv > waited[ws]:
                            eng.wait_ge(sems[ws], wv)
                            waited[ws] = wv
                    inst = emit_fn(eng)
                    if tick in needed or step == 16 or tick[0] == "cc":
                        assert inst is not None, f"ticketed op returned None on {q}"
                        inst.then_inc(sems[tick[0]], tick[1] - last_inc[tick[0]])
                        last_inc[tick[0]] = tick[1]
            return body

        for q in QUEUES:
            if per_q[q]:
                getattr(blk, ENG_OF[q])(make_body(q))


LAST_PROG = None


def build_program(TOK=4096, TILE=512, cut=None):
    NT = TOK // TILE
    TC = TILE // 128
    W = NCH * TILE
    NB = W // 32
    nc = bass.Bass("TRN2", target_bir_lowering=False, debug=False)

    def din(name, shape, dtype=F32):
        return nc.dram_tensor(name, list(shape), dtype, kind="ExternalInput").ap()

    xT = din("xT", [D, TOK])
    wqT = din("wqT", [D, D]); wkT = din("wkT", [D, D]); wvT = din("wvT", [D, D])
    woT = din("woT", [D, D])
    w1T = din("w1T", [D, DFF]); w2T = din("w2T", [DFF, D])
    wfn = din("wfn", [D, D])
    projTd = din("projTd", [DH, M])
    bcols = din("bcols", [128, 80])
    brows = din("brows", [8, 1024])
    dwb = din("dwb", [128, 256])
    dact = din("dact", [1, 1])
    cI = din("cI", [128, 128])
    cind = din("cind", [128, 128])
    par = din("par", [1, 2])
    gcols = din("gcols", [128, 32])

    outT = nc.dram_tensor("outT", [D, TOK], F32, kind="ExternalOutput").ap()

    xn_scr = nc.dram_tensor("xn_scr", [128, NT * W], BF16, kind="Internal").ap()
    g_scr = nc.dram_tensor("g_scr", [128, NT * W], F32, kind="Internal").ap()
    w1_scr = nc.dram_tensor("w1_scr", [128, 32 * D], BF16, kind="Internal").ap()
    w2_scr = nc.dram_tensor("w2_scr", [128, NCH * DFF], BF16, kind="Internal").ap()
    CCW = 2120
    cc_in = nc.dram_tensor("cc_in", [128, CCW], F32, kind="Internal").ap()
    cc_out = nc.dram_tensor("cc_out", [128, CCW], F32, kind="Internal").ap()

    B0 = 16512

    def S(name, shape, dtype, off):
        assert B0 + off + int(np.prod(shape[1:])) * mybir.dt.size(dtype) <= 229376
        return nc.alloc_sbuf_tensor_at(name, list(shape), dtype, offset=B0 + off)

    a_f1 = S("a_f1", [128, 4096], F32, 0)
    a_w2p = [S(f"a_w2p{i}", [128, 4096], BF16, i * 8192) for i in range(2)]
    a_f2 = S("a_f2", [128, 4096], F32, 16384)
    r_vsl = S("r_vsl", [1, 1024], F32, 16384)
    r_gch = S("r_gch", [1, 512], F32, 20480)
    r_vch = S("r_vch", [1, 512], F32, 22528)
    a_f3 = S("a_f3", [128, 4096], F32, 32768)
    a_xt = S("a_xt", [128, 4096], F32, 49152)
    a_w1p = [S(f"a_w1p{i}", [128, 1024], BF16, 49152 + i * 2048) for i in range(2)]
    a_out = [S("a_out0", [128, 512], F32, 53248)]
    a_xres = [S("a_xres0", [128, 512], F32, 55296)]
    a_ss = S("a_ss", [128, 512], F32, 57344)
    a_nm = S("a_nm", [65, 512], F32, 57344)
    r_ds0 = S("r_ds0", [1, 512], F32, 59392)
    a_b1 = [S(f"a_b1{i}", [128, 4096], BF16, 65536 + i * 8192) for i in range(2)]
    c_dwp = S("c_dwp", [128, 256], F32, 73728)
    c_dwn = S("c_dwn", [128, 256], F32, 74752)
    s_land = S("s_land", [8, 1024], F32, 75776)
    a_b2 = S("a_b2", [128, 4096], BF16, 81920)
    a_b3 = S("a_b3", [128, 4096], BF16, 90112)
    a_b4 = S("a_b4", [128, 4096], BF16, 98304)
    a_i1 = S("a_i1", [128, 2048], I32, 98304)
    a_big = S("a_big", [128, 16384], BF16, 106496)
    a_wqh = S("a_wqh", [128, 4096], F32, 106496)
    c_wqt = S("c_wqt", [128, 8192], BF16, 122880)
    c_wk = S("c_wk", [128, 8192], BF16, 139264)
    c_wq = S("c_wq", [128, 8192], BF16, 139264)
    c_wv = S("c_wv", [128, 8192], BF16, 155648)
    c_wo = S("c_wo", [128, 8192], BF16, 155648)
    s_ctx = S("s_ctx", [128, 2048], F32, 172032)
    s_xst = S("s_xst", [128, 4096], BF16, 172032)
    s_att = S("s_att", [128, 4096], BF16, 172032)
    a_g = S("a_g", [128, 2048], F32, 172032)
    s_caug = S("s_caug", [128, 2080], BF16, 180224)
    s_ds = S("s_ds", [16, 512], F32, 184384)
    s_sm = S("s_sm", [128, 1664], F32, 186432)
    r_ds1 = S("r_ds1", [1, 512], F32, 61440)
    r_one = S("r_one", [1, 128], F32, 193088)
    r_neg1 = S("r_neg1", [1, 128], F32, 193600)
    r_bv = S("r_bv", [1, 1024], F32, 194112)
    r_g2b2 = S("r_g2b2", [2, 1024], F32, 198208)
    r_g3b3 = S("r_g3b3", [2, 1024], F32, 202304)
    r_lnA = S("r_lnA", [1, 512], F32, 206400)
    r_lnB = S("r_lnB", [2, 512], F32, 208448)
    c_pj = S("c_pj", [128, 256], BF16, 210496)
    c_Ib = S("c_Ib", [128, 128], BF16, 211008)
    c_If = S("c_If", [128, 128], F32, 211264)
    c_indb = S("c_indb", [128, 128], BF16, 211776)
    c_bcols = S("c_bcols", [128, 80], F32, 212032)
    s_ecs = S("s_ecs", [65, 16], F32, 212352)
    c_oc = S("c_oc", [128, 1], BF16, 212416)
    c_ocf = S("c_ocf", [128, 1], F32, 212448)
    c_gcols = S("c_gcols", [128, 32], F32, 212480)
    c_dmask = S("c_dmask", [128, 512], BF16, 63488)
    r_db = [S(f"r_db{i}", [1, 512], BF16, 59392 + i * 2048) for i in range(2)]
    r_nb = S("r_nb", [1, 128], BF16, 64512)

    SM_MU, SM_VAR = 0, 512
    C_RMQ, C_RM, C_S0, C_VS = 1024, 1088, 1104, 1136
    C_D2P, C_D2N, C_NS0, C_GH, C_ST = 1144, 1145, 1146, 1147, 1148
    C_TMP = 1152
    C_KS = 1184
    C_PAR = 1536
    C_G2E = 1540
    R_G16, R_G2, R_S0B = 0, 16, 40
    s_q = s_sm[:, 1216:1280]

    nc.bump_sbuf(B0 + 212608 - nc.sbuf_base)

    psum = {n: nc.alloc_psum_tensor(n, [128, 512], F32)
            for n in ("pA", "pB", "pC", "pD", "p0", "p1", "p2", "p3")}
    pA, pB, pC, pD = psum["pA"], psum["pB"], psum["pC"], psum["pD"]
    pc = [psum[f"p{i}"] for i in range(4)]

    P = Prog()
    P.marks = {}

    def ld(dst, src, reads=(), writes=(), extra=(), q="sd"):
        return P.op(q, lambda eng, d=dst, s=src: eng.dma_start(out=d, in_=s),
                    reads=reads, writes=writes, extra=extra, dma=True)

    def mm(out_ap, l, r, reads=(), writes=(), extra=(), start=True, stop=True, tp=None):
        def emit(eng):
            kw = {"tile_position": tp} if tp is not None else {}
            return eng.matmul(out_ap, l, r, start=start, stop=stop,
                              skip_group_check=True, **kw)
        return P.op("pe", emit, reads=reads, writes=writes, extra=extra)

    def mmg(out_ap, pairs, reads=(), writes=(), extra=(), start=True, stop=True, tp=None):
        def emit(eng):
            inst = None
            n = len(pairs)
            kw = {"tile_position": tp} if tp is not None else {}
            for i, (l, r) in enumerate(pairs):
                inst = eng.matmul(out_ap, l, r, start=(start and i == 0),
                                  stop=(stop and i == n - 1), skip_group_check=True, **kw)
            return inst
        return P.op("pe", emit, reads=reads, writes=writes, extra=extra)

    def rl(*aps):
        return [f"rl_{ap.tensor.name}" for ap in aps
                if type(ap.tensor).__name__ == "PSumTensorHandle"]

    def act(fn, out, in_, reads=(), writes=(), extra=(), **kw):
        return P.op("act", lambda eng: eng.activation(out, in_, fn, **kw),
                    reads=reads, writes=list(writes) + rl(in_), extra=extra)

    def acopy(out, in_, reads=(), writes=(), extra=(), scale=None):
        if scale is None:
            return P.op("act", lambda eng: eng.copy(out, in_),
                        reads=reads, writes=list(writes) + rl(in_), extra=extra)
        return P.op("act", lambda eng: eng.activation(out, in_, AF.Copy, scale=scale),
                    reads=reads, writes=list(writes) + rl(in_), extra=extra)

    def ts(out, in_, s1, s2, op0, op1=None, reads=(), writes=(), extra=(), q="dve"):
        if op1 is None:
            return P.op(q, lambda eng: eng.tensor_scalar(out, in_, s1, None, op0=op0),
                        reads=reads, writes=list(writes) + rl(in_), extra=extra)
        return P.op(q, lambda eng: eng.tensor_scalar(out, in_, s1, s2, op0=op0, op1=op1),
                    reads=reads, writes=list(writes) + rl(in_), extra=extra)

    def tt(out, a, b, alu, reads=(), writes=(), extra=(), q="dve"):
        return P.op(q, lambda eng: eng.tensor_tensor(out, a, b, op=alu),
                    reads=reads, writes=list(writes) + rl(a, b), extra=extra)

    def stt(out, in0, scalar, in1, op0, op1, reads=(), writes=(), extra=(), q="dve"):
        return P.op(q, lambda eng: eng.scalar_tensor_tensor(out, in0, scalar, in1,
                                                            op0=op0, op1=op1),
                    reads=reads, writes=list(writes) + rl(in0, in1), extra=extra)

    def vcopy(out, in_, reads=(), writes=(), extra=(), q="dve"):
        return P.op(q, lambda eng: eng.tensor_copy(out, in_),
                    reads=reads, writes=list(writes) + rl(in_), extra=extra)

    def vred(out, in_, alu, reads=(), writes=(), extra=(), absval=None):
        return P.op("dve", lambda eng: eng.tensor_reduce(out, in_, axis=AX.X, op=alu,
                                                         apply_absolute_value=absval),
                    reads=reads, writes=list(writes) + rl(in_), extra=extra)

    def vtrans(out, in_, reads=(), writes=(), extra=()):
        return P.op("dve", lambda eng: eng.transpose(out, in_),
                    reads=reads, writes=list(writes) + rl(in_), extra=extra)

    def memset(ap, val, reads=(), writes=(), extra=(), q="dve"):
        return P.op(q, lambda eng: eng.memset(ap, val), reads=reads, writes=writes, extra=extra)

    def recipv(out, in_, reads=(), writes=(), extra=()):
        return P.op("dve", lambda eng: eng.reciprocal(out, in_),
                    reads=reads, writes=list(writes) + rl(in_), extra=extra)

    def petr(out, in_, ident, reads=(), writes=(), extra=()):
        return P.op("pe", lambda eng: eng.transpose(out, in_, ident),
                    reads=reads, writes=writes, extra=extra)

    smc = lambda c0, n=1: s_sm[:, c0:c0 + n]
    smr = lambda c0, n: s_sm[0:1, c0:c0 + n]

    def newton_rsqrt(dst, src, tmp, rbufs, wbuf):
        ts(dst.bitcast(I32), src.bitcast(I32), 0x7F800000, 23,
           ALU.bitwise_and, ALU.logical_shift_right, reads=rbufs, writes=[wbuf])
        ts(dst.bitcast(I32), dst.bitcast(I32), -1, 380, ALU.mult, ALU.add,
           reads=[wbuf], writes=[wbuf])
        ts(dst.bitcast(I32), dst.bitcast(I32), 1, 23,
           ALU.logical_shift_right, ALU.logical_shift_left, reads=[wbuf], writes=[wbuf])
        for _ in range(6):
            tt(tmp, dst, dst, ALU.mult, reads=[wbuf], writes=["rsq_tmp"])
            tt(tmp, tmp, src, ALU.mult, reads=["rsq_tmp"] + rbufs, writes=["rsq_tmp"])
            ts(tmp, tmp, -0.5, 1.5, ALU.mult, ALU.add, reads=["rsq_tmp"], writes=["rsq_tmp"])
            tt(dst, dst, tmp, ALU.mult, reads=[wbuf, "rsq_tmp"], writes=[wbuf])

    def fp4_round(tin, iout, tmpf, rbuf, ibuf, obuf):
        ts(iout, tin.bitcast(I32), 0x7F800000, 23,
           ALU.bitwise_and, ALU.logical_shift_right, reads=[rbuf], writes=[ibuf])
        ts(iout, iout, 127, 0, ALU.subtract, ALU.max, reads=[ibuf], writes=[ibuf])
        ts(iout, iout, 2, None, ALU.min, reads=[ibuf], writes=[ibuf])
        ts(iout, iout, 149, None, ALU.add, reads=[ibuf], writes=[ibuf])
        ts(iout, iout, 23, 0x400000, ALU.logical_shift_left, ALU.bitwise_or,
           reads=[ibuf], writes=[ibuf])
        tt(tmpf, tin, iout.bitcast(F32), ALU.add, reads=[rbuf, ibuf], writes=[obuf])
        tt(tmpf, tmpf, iout.bitcast(F32), ALU.subtract, reads=[obuf, ibuf], writes=[obuf])

    def tanh_2pow(dst_p, dst_n, src, t1, t2, t3, rbufs, wp, wn, tb):
        ts(t1, src, 0.5, None, ALU.mult, reads=rbufs, writes=[tb])
        tt(t2, t1, t1, ALU.mult, reads=[tb], writes=[tb])
        ts(t3, t2, -17.0 / 315.0, 2.0 / 15.0, ALU.mult, ALU.add, reads=[tb], writes=[tb])
        tt(t3, t3, t2, ALU.mult, reads=[tb], writes=[tb])
        ts(t3, t3, -1.0 / 3.0, None, ALU.add, reads=[tb], writes=[tb])
        tt(t3, t3, t2, ALU.mult, reads=[tb], writes=[tb])
        ts(t3, t3, 1.0, None, ALU.add, reads=[tb], writes=[tb])
        tt(dst_p, t3, t1, ALU.mult, reads=[tb], writes=[wp])
        for _ in range(3):
            tt(t2, dst_p, dst_p, ALU.mult, reads=[wp], writes=[tb])
            ts(t2, t2, 1.0, None, ALU.add, reads=[tb], writes=[tb])
            recipv(t3, t2, reads=[tb], writes=[tb])
            tt(dst_p, dst_p, t3, ALU.mult, reads=[wp, tb], writes=[wp])
            ts(dst_p, dst_p, 2.0, None, ALU.mult, reads=[wp], writes=[wp])
        ts(t1, dst_p, 0.5 * LN2, None, ALU.mult, reads=[wp], writes=[tb])

        def expw(dst, sgn, wd):
            ts(dst, t1, sgn / 9.0, 1.0, ALU.mult, ALU.add, reads=[tb], writes=[wd])
            for kk in range(8, 0, -1):
                tt(t2, dst, t1, ALU.mult, reads=[wd, tb], writes=[tb + "x"])
                ts(dst, t2, sgn / kk, 1.0, ALU.mult, ALU.add, reads=[tb + "x"], writes=[wd])
        expw(dst_p, 1.0, wp)
        expw(dst_n, -1.0, wn)
    def BS(c0, c1):
        return [f"B{i}" for i in range(c0 // 256, (c1 + 255) // 256)]

    P.op("act", lambda eng: eng.preload_act_table(AF.Silu))
    ld(c_If[:], cI[:], writes=["cIf"])
    ld(c_Ib[:], cI[:], writes=["cIb"], q="gp")
    ld(c_indb[:], cind[:], writes=["cind"], q="gp")
    ld(c_bcols[:], bcols[:], writes=["bcols"])
    ld(c_gcols[:], gcols[:], writes=["gcols"])
    ld(c_dwp[:], dwb[:], writes=["dwraw"])
    ld(smr(512, 1), dact[:], writes=["dact"])
    ld(smr(C_PAR, 2), par[:], writes=["par"])
    ld(s_sm[0:64, 0:256], projTd[:], writes=["pjf"])
    ld(s_land[:], brows[:], writes=["rows8"])
    ld(r_bv[:], s_land[0:1, :], reads=["rows8"], writes=["r_bv"])
    ld(r_g2b2[0:1, :], s_land[1:2, :], reads=["rows8"], writes=["r_g2"])
    ld(r_g2b2[1:2, :], s_land[2:3, :], reads=["rows8"], writes=["r_b2"])
    ld(r_g3b3[0:1, :], s_land[3:4, :], reads=["rows8"], writes=["r_g3"])
    ld(r_g3b3[1:2, :], s_land[4:5, :], reads=["rows8"], writes=["r_b3"])
    ld(r_neg1[:], s_land[5:6, 0:128], reads=["rows8"], writes=["r_n1"])
    ld(r_one[:], s_land[6:7, 0:128], reads=["rows8"], writes=["r_1"])
    memset(c_oc[:], 1.0, writes=["oc"])
    memset(c_ocf[:], 1.0, writes=["ocf"])
    ld(r_lnB[1:2, :], brows[7:8, 0:512], writes=["lnB1"])

    ts(s_sm[0:64, 0:256], s_sm[0:64, 0:256], NORM, None, ALU.mult,
       reads=["pjf"], writes=["pjf"])
    acopy(c_pj[0:64, :], s_sm[0:64, 0:256], reads=["pjf"], writes=["pj_lo"])
    ld(c_pj[64:128, :], c_pj[0:64, :], reads=["pj_lo"], writes=["pj_hi"])
    tt(s_sm[0:64, 0:256], c_pj[0:64, :], c_pj[0:64, :], ALU.mult,
       reads=["pj_lo", "pjf"], writes=["pjsq"])
    mm(pD[0:1, 0:256], c_ocf[0:64, :], s_sm[0:64, 0:256],
       reads=["pjsq", "ocf", "bD"], writes=["bD"])
    ts(smr(520, 256), pD[0:1, 0:256], 0.5, None, ALU.mult, reads=["bD"], writes=["s0t"])
    vred(smr(R_S0B, 1), smr(520, 256), ALU.max, reads=["s0t"], writes=["stab0"])
    mm(pD[0:16, 0:1], r_one[0:1, 0:16], smr(R_S0B, 1),
       reads=["stab0", "r_1", "bD"], writes=["bD"])
    acopy(smc(C_NS0)[0:16, :], pD[0:16, 0:1], reads=["bD"], writes=["ns0"], scale=-1.0)

    P.merge("sctmp", ["s0t"])
    tanh_2pow(smr(530, 1), smr(531, 1), smr(512, 1), smr(532, 1), smr(533, 1), smr(534, 1),
              ["dact"], "d2pr", "d2nr", "sctmp")
    mm(pD[:, 0:2], r_one[:], smr(530, 2), reads=["d2pr", "d2nr", "r_1", "bD"], writes=["bD"])
    acopy(smc(C_D2P, 2), pD[:, 0:2], reads=["bD"], writes=["d2c"])

    mm(pD[:, 2:4], r_one[:], smr(C_PAR, 2), reads=["par", "r_1", "bD"], writes=["bD"])
    acopy(smc(1280, 2), pD[:, 2:4], reads=["bD"], writes=["parc"])

    tanh_2pow(c_dwp[:], c_dwn[:], c_dwp[:],
              a_f1[:, 0:256], a_f1[:, 256:512], a_f1[:, 512:768],
              ["dwraw"], "dwp", "dwn", "f1t")

    ld(c_wk[:].rearrange("p (c o) -> p c o", o=1024), wkT[:].rearrange("(c p) o -> p c o", p=128), writes=["wk"], q="gp")
    ld(c_wv[:].rearrange("p (c o) -> p c o", o=1024), wvT[:].rearrange("(c p) o -> p c o", p=128), writes=["wv"], q="gp")

    for half in range(2):
        ld(a_wqh[:].rearrange("p (c o) -> p c o", o=1024),
           wfn[:].rearrange("(c p) o -> p c o", p=128)[:, half * 4:(half + 1) * 4, :],
           writes=BS(0, 8192), q="gp")
        vred(s_sm[:, 768:896], a_wqh[:].rearrange("p (a b) -> p a b", b=32), ALU.max,
             absval=True, reads=BS(0, 8192), writes=["wbm"])
        ts(a_i1[:, 0:128], smc(768, 128).bitcast(I32), 0x7F800000, 23,
           ALU.bitwise_and, ALU.logical_shift_right, reads=["wbm", "i1t"], writes=["we8"])
        ts(a_f1[:, 768:896], smc(768, 128), 0.0, None, ALU.is_equal,
           reads=["wbm", "f1t"], writes=["wiz"])
        ts(a_i1[:, 128:256], a_i1[:, 0:128], 2, None, ALU.subtract, reads=["we8"], writes=["wsb"])
        ts(a_i1[:, 128:256], a_i1[:, 128:256], 23, None, ALU.logical_shift_left,
           reads=["wsb"], writes=["wsb"])
        ts(smc(896, 128), a_f1[:, 768:896], -1.0, 1.0, ALU.mult, ALU.add,
           reads=["wiz"], writes=["wsc"])
        tt(smc(896, 128), smc(896, 128), a_i1[:, 128:256].bitcast(F32), ALU.mult,
           reads=["wsc", "wsb"], writes=["wsc"])
        tt(smc(896, 128), smc(896, 128), a_f1[:, 768:896], ALU.add,
           reads=["wsc", "wiz"], writes=["wsc"])
        tt(smc(896, 128), smc(896, 128), c_dwp[:, half * 128:half * 128 + 128],
           ALU.mult, reads=["wsc", "dwp"], writes=["wsc"])
        ts(a_i1[:, 128:256], a_i1[:, 0:128], -1, 256, ALU.mult, ALU.add,
           reads=["we8", "wsc"], writes=["wsb"])
        ts(a_i1[:, 128:256], a_i1[:, 128:256], 23, None, ALU.logical_shift_left,
           reads=["wsb"], writes=["wsb"])
        ts(a_f1[:, 896:1024], a_f1[:, 768:896], -1.0, 1.0, ALU.mult, ALU.add,
           reads=["wiz"], writes=["wiv"])
        tt(a_f1[:, 896:1024], a_f1[:, 896:1024], a_i1[:, 128:256].bitcast(F32), ALU.mult,
           reads=["wiv", "wsb"], writes=["wiv"])
        tt(a_f1[:, 896:1024], a_f1[:, 896:1024], a_f1[:, 768:896], ALU.add,
           reads=["wiv", "wiz"], writes=["wiv"])
        tt(a_f1[:, 896:1024], a_f1[:, 896:1024],
           c_dwn[:, half * 128:half * 128 + 128], ALU.mult,
           reads=["wiv", "dwn"], writes=["wiv"])
        for ch in range(2):
            sl = slice(ch * 2048, (ch + 1) * 2048)
            bsl = slice(ch * 64, (ch + 1) * 64)
            tt(a_f2[:, sl].rearrange("p (a b) -> p a b", b=32),
               a_wqh[:, sl].rearrange("p (a b) -> p a b", b=32),
               a_f1[:, 896:1024][:, bsl].unsqueeze(2).broadcast_to([128, 64, 32]),
               ALU.mult, reads=BS(ch * 4096, (ch + 1) * 4096) + ["wiv", "f2t"], writes=["wxs"])
            ts(a_f2[:, sl], a_f2[:, sl], 6.0, -6.0, ALU.min, ALU.max,
               reads=["wxs"], writes=["wxs"])
            fp4_round(a_f2[:, sl], a_i1[:, 0:2048], a_f3[:, sl], "wxs", "wi1", "wr4")
            tt(a_b2[:, sl].rearrange("p (a b) -> p a b", b=32),
               a_f3[:, sl].rearrange("p (a b) -> p a b", b=32),
               smc(896, 128)[:, bsl].unsqueeze(2).broadcast_to([128, 64, 32]),
               ALU.mult, reads=["wr4", "wsc", "wqn"], writes=["wqn"])
        for ocl in range(4):
            oc = half * 4 + ocl
            for icn in range(NCH):
                w_in = a_b2[:, ocl * 1024 + icn * 128: ocl * 1024 + icn * 128 + 128]
                petr(pC[:, 0:64].bitcast(BF16), w_in, c_Ib[:],
                     reads=["wqn", "cIb", "bC"], writes=["bC"])
                dstc = icn * 1024 + oc * 128
                acopy(c_wqt[:, dstc:dstc + 128], pC[:, 0:64].bitcast(BF16),
                      reads=["bC"], writes=BS(8192 + dstc, 8192 + dstc + 128))
    WQT = BS(8192, 16384)

    for cj in range(NCH):
        ld(a_b3[:, 0:4096], w1T[cj * 128:(cj + 1) * 128, :], writes=["b3"], q="gp")
        ld(w1_scr[:].rearrange("p (f c o) -> p f c o", f=32, c=NCH)[:, :, cj, :],
           a_b3[:, 0:4096].rearrange("p (f o) -> p f o", o=128),
           reads=["b3"], writes=["w1s"] if cj == NCH - 1 else [])
    P.merge("b4", ["we8", "wsb", "wi1"])
    for fc in range(32):
        ld(a_b4[:, 0:1024], w2T[fc * 128:(fc + 1) * 128, :], writes=["b4"], q="gp")
        ld(w2_scr[:].rearrange("p (c f o) -> p c f o", c=NCH, f=32)[:, :, fc, :],
           a_b4[:, 0:1024].rearrange("p (c o) -> p c o", o=128),
           reads=["b4"], writes=["w2s"] if fc == 31 else [])
    P.marks["ph0"] = len(P.ops)
    for t in range(NT):
        ld(a_xt[:].rearrange("p (c n) -> p c n", c=NCH),
           xT[:, t * TILE:(t + 1) * TILE].rearrange("(c p) n -> p c n", p=128),
           writes=["xt"])
        acopy(a_b3[:, 0:W], a_xt[:], reads=["xt"], writes=["b3"])
        tt(a_b4[:, 0:W], a_b3[:, 0:W], a_b3[:, 0:W], ALU.mult,
           reads=["b3", "i1t"], writes=["b4"], q="gp")
        mmg(pC[0:1, 0:TILE], [(c_oc[:], a_b3[:, ci * TILE:(ci + 1) * TILE]) for ci in range(NCH)],
            reads=["b3", "oc", "bC"], writes=["bC"])
        mmg(pD[0:1, 0:TILE], [(c_oc[:], a_b4[:, ci * TILE:(ci + 1) * TILE]) for ci in range(NCH)],
            reads=["b4", "bD"], writes=["bD"])
        mu, var = smr(SM_MU, TILE), smr(SM_VAR, TILE)
        ts(mu, pC[0:1, 0:TILE], 1.0 / D, None, ALU.mult, reads=["bC"], writes=["mu", "dact", "s0t", "d2pr", "d2nr", "sctmp", "stab0", "ghr", "g2r"])
        ts(var, pD[0:1, 0:TILE], 1.0 / D, None, ALU.mult, reads=["bD"], writes=["var", "s0t", "wbm", "bm", "sc", "wsc"])
        tt(r_lnA[:], mu, mu, ALU.mult, reads=["mu", "lnA"], writes=["lnA"])
        tt(var, var, r_lnA[:], ALU.subtract, reads=["var", "lnA"], writes=["var"])
        ts(var, var, LN_EPS, None, ALU.add, reads=["var"], writes=["var"])
        newton_rsqrt(r_lnA[:], var, r_lnB[0:1, :], ["var", "lnB0"], "lnA")
        tt(r_lnB[0:1, :], mu, r_lnA[:], ALU.mult, reads=["mu", "lnA", "lnB0"], writes=["lnB0"])
        ts(r_lnB[0:1, :], r_lnB[0:1, :], -1.0, None, ALU.mult, reads=["lnB0"], writes=["lnB0"])
        mm(pC[:, 0:TILE], r_one[:], r_lnA[:], reads=["lnA", "r_1", "bC"], writes=["bC"])
        mm(pD[:, 0:TILE], r_one[:], r_lnB[0:1, :], reads=["lnB0", "r_1", "bD"], writes=["bD"])
        for ci in range(NCH):
            tt(a_f1[:, ci * TILE:(ci + 1) * TILE], a_xt[:, ci * TILE:(ci + 1) * TILE],
               pC[:, 0:TILE], ALU.mult, reads=["xt", "bC", "f1t"], writes=["f1t"])
            tt(a_f1[:, ci * TILE:(ci + 1) * TILE], a_f1[:, ci * TILE:(ci + 1) * TILE],
               pD[:, 0:TILE], ALU.add, reads=["bD", "f1t"], writes=["f1t"])
        acopy(s_xst[:, 0:W], a_f1[:, 0:W], reads=["f1t", "xstg"], writes=["xstg"])
        ld(xn_scr[:, t * W:(t + 1) * W], s_xst[:, 0:W], reads=["xstg"], writes=[f"xnscr{t}"])
        for ci in range(NCH):
            vtrans(a_f2[:, ci * TILE:(ci + 1) * TILE], a_f1[:, ci * TILE:(ci + 1) * TILE],
                   reads=["f1t", "f2t"], writes=["f2t"])
        vred(s_sm[:, 768:768 + NB], a_f2[:, 0:W].rearrange("p (a b) -> p a b", b=32),
             ALU.max, absval=True, reads=["f2t", "wbm"], writes=["bm"])
        ts(a_i1[:, 0:NB], smc(768, NB).bitcast(I32), 0x7F800000, 23,
           ALU.bitwise_and, ALU.logical_shift_right, reads=["bm"], writes=["e8", "b4"])
        ts(a_f3[:, 0:NB], smc(768, NB), 0.0, None, ALU.is_equal,
           reads=["bm"], writes=["iz", "f3t"])
        ts(a_i1[:, NB:2 * NB], a_i1[:, 0:NB], 2, None, ALU.subtract, reads=["e8"], writes=["sb"])
        ts(a_i1[:, NB:2 * NB], a_i1[:, NB:2 * NB], 23, None, ALU.logical_shift_left,
           reads=["sb"], writes=["sb"])
        ts(smc(896, NB), a_f3[:, 0:NB], -1.0, 1.0, ALU.mult, ALU.add,
           reads=["iz", "wsc"], writes=["sc"])
        tt(smc(896, NB), smc(896, NB), a_i1[:, NB:2 * NB].bitcast(F32), ALU.mult,
           reads=["sc", "sb"], writes=["sc"])
        tt(smc(896, NB), smc(896, NB), a_f3[:, 0:NB], ALU.add, reads=["sc", "iz"], writes=["sc"])
        tt(smc(896, NB), smc(896, NB), smc(C_D2P).broadcast_to([128, NB]), ALU.mult,
           reads=["sc", "d2c"], writes=["sc"])
        ts(a_i1[:, NB:2 * NB], a_i1[:, 0:NB], -1, 256, ALU.mult, ALU.add,
           reads=["e8", "sc"], writes=["sb"])
        ts(a_i1[:, NB:2 * NB], a_i1[:, NB:2 * NB], 23, None, ALU.logical_shift_left,
           reads=["sb"], writes=["sb"])
        ts(a_f3[:, NB:2 * NB], a_f3[:, 0:NB], -1.0, 1.0, ALU.mult, ALU.add,
           reads=["iz"], writes=["iv"])
        tt(a_f3[:, NB:2 * NB], a_f3[:, NB:2 * NB], a_i1[:, NB:2 * NB].bitcast(F32), ALU.mult,
           reads=["iv", "sb"], writes=["iv"])
        tt(a_f3[:, NB:2 * NB], a_f3[:, NB:2 * NB], a_f3[:, 0:NB], ALU.add,
           reads=["iv", "iz"], writes=["iv"])
        tt(a_f3[:, NB:2 * NB], a_f3[:, NB:2 * NB], smc(C_D2N).broadcast_to([128, NB]),
           ALU.mult, reads=["iv", "d2c"], writes=["iv"])
        vcopy(a_f2[:, 0:W].rearrange("p (a b) -> p a b", b=32),
              a_f3[:, NB:2 * NB].unsqueeze(2).broadcast_to([128, NB, 32]),
              reads=["iv", "f2t"], writes=["f2t"])
        for ci in range(NCH):
            vtrans(a_f3[:, ci * TILE:(ci + 1) * TILE], a_f2[:, ci * TILE:(ci + 1) * TILE],
                   reads=["f2t", "iv", "iz", "f3t"], writes=["f3t"])
        tt(a_f2[:, 0:W], a_f1[:, 0:W], a_f3[:, 0:W], ALU.mult,
           reads=["f1t", "f3t", "f2t"], writes=["f2t"])
        ts(a_f2[:, 0:W], a_f2[:, 0:W], 6.0, -6.0, ALU.min, ALU.max,
           reads=["f2t"], writes=["f2t"], q="gp")
        vcopy(a_f3[:, 0:W].rearrange("p (a b) -> p a b", b=32),
              smc(896, NB).unsqueeze(2).broadcast_to([128, NB, 32]),
              reads=["sc", "f3t"], writes=["f3t"])
        for ci in range(NCH):
            vtrans(a_f1[:, ci * TILE:(ci + 1) * TILE], a_f3[:, ci * TILE:(ci + 1) * TILE],
                   reads=["f3t", "f1t"], writes=["f1t"])
        for ch in range(2):
            sl = slice(ch * 2048, (ch + 1) * 2048)
            fp4_round(a_f2[:, sl], a_i1[:, 0:2048], a_f3[:, sl], "f2t", "i1t", "f3t")
        tt(a_b2[:, 0:W], a_f3[:, 0:W], a_f1[:, 0:W], ALU.mult,
           reads=["f3t", "f1t", "b2"], writes=["b2"])
        P.merge("b4", ["e8", "sb", "i1t"])
        for ci in range(NCH):
            pairs = [(c_wqt[:, cj * 1024 + ci * 128: cj * 1024 + ci * 128 + 128],
                      a_b2[:, cj * TILE:(cj + 1) * TILE]) for cj in range(NCH)]
            pm = pA if ci % 2 == 0 else pB
            bn = "bA" if ci % 2 == 0 else "bB"
            mmg(pm[:, 0:TILE], pairs, reads=["b2", bn] + WQT, writes=[bn])
            act(AF.Silu, a_f3[:, ci * TILE:(ci + 1) * TILE], pm[:, 0:TILE],
                reads=[bn, "bcols", "f3t"], writes=["f3t"],
                bias=c_bcols[:, 5 * NCH + ci:5 * NCH + ci + 1])
        ld(g_scr[:, t * W:(t + 1) * W], a_f3[:, 0:W], reads=["f3t"], writes=[f"gscr{t}"])

    P.marks["1a"] = len(P.ops)
    P.op("act", lambda eng: eng.preload_act_table(AF.Exp))
    memset(s_ctx[:], 0.0, writes=["ctxa", "xstg"])
    P.merge("b1_1", ["dwp", "dwn", "dwraw", "rows8", "vst"])
    memset(smc(C_RM, 16), -1e30, writes=["rm"])
    memset(smc(C_S0, 32), 0.0, writes=["s0a"])
    memset(smc(C_VS, 8), 0.0, writes=["vsa"])
    for i in range(4):
        memset(pc[i][:, 0:512], 0.0, writes=[f"bc{i}"])
    for t in range(NT):
        bi = t % 2
        xnb = a_b1[bi]
        ld(xnb[:, 0:W], xn_scr[:, t * W:(t + 1) * W],
           reads=[f"xnscr{t}"], writes=[f"b1_{bi}"])
        for ci in range(NCH):
            pairs = [(c_wk[:, cj * 1024 + ci * 128: cj * 1024 + ci * 128 + 128],
                      xnb[:, cj * TILE:(cj + 1) * TILE]) for cj in range(NCH)]
            pm = pA if ci % 2 == 0 else pB
            bn = "bA" if ci % 2 == 0 else "bB"
            mmg(pm[:, 0:TILE], pairs, reads=[f"b1_{bi}", "wk", bn], writes=[bn])
            act(AF.Identity, a_b3[:, ci * TILE:(ci + 1) * TILE], pm[:, 0:TILE],
                reads=[bn, "bcols", "b3"], writes=["b3"],
                bias=c_bcols[:, 1 * NCH + ci:1 * NCH + ci + 1])
        for tc in range(TC):
            for half in range(2):
                pairs = [(xnb[:, cj * TILE + tc * 128: cj * TILE + tc * 128 + 128],
                          c_wv[:, cj * 1024 + half * 512: cj * 1024 + (half + 1) * 512])
                         for cj in range(NCH)]
                pm = pA if (tc * 2 + half) % 2 == 0 else pB
                bn = "bA" if (tc * 2 + half) % 2 == 0 else "bB"
                mmg(pm[:, 0:512], pairs, reads=[f"b1_{bi}", "wv", bn], writes=[bn], stop=False)
                mm(pm[:, 0:512], r_one[:], r_bv[0:1, half * 512:(half + 1) * 512],
                   reads=["r_1", "r_bv"], writes=[bn], start=False)
                acopy(a_b4[:, tc * 1024 + half * 512: tc * 1024 + (half + 1) * 512],
                      pm[:, 0:512], reads=[bn, "b4"], writes=["b4"])
        tt(a_b2[:, 0:W], a_b3[:, 0:W], a_b3[:, 0:W], ALU.mult,
           reads=["b3", "b2"], writes=["b2"])
        mmg(pD[0:16, 0:TILE],
            [(c_indb[:, ci * 16:(ci + 1) * 16], a_b2[:, ci * TILE:(ci + 1) * TILE])
             for ci in range(NCH)], reads=["b2", "cind", "bD"], writes=["bD"])
        act(AF.Identity, s_ds[0:16, 0:TILE], pD[0:16, 0:TILE], reads=["bD", "ns0", "ds"],
            writes=["ds"], scale=-1.0, bias=smc(C_NS0)[0:16, :])
        memset(pD[:, 0:40], 0.0, reads=["bD"], writes=["bD"])
        for tc in range(TC):
            petr(pC[:, 0:16], s_ds[0:16, tc * 128:(tc + 1) * 128], c_If[0:16, 0:16],
                 reads=["ds", "cIf", "bC"], writes=["bC"])
            acopy(s_q[:, tc * 16:(tc + 1) * 16], pC[:, 0:16], reads=["bC", "sq"], writes=["sq"])
        def emit_xp(h, tc):
            ci = h // 2
            r0 = (h % 2) * 64
            pj = c_pj[0:64, :] if r0 == 0 else c_pj[64:128, :]
            pm = pA if (h * TC + tc) % 2 == 0 else pB
            bn = "bA" if (h * TC + tc) % 2 == 0 else "bB"
            lhsT = a_b3[r0:r0 + 64, ci * TILE + tc * 128: ci * TILE + tc * 128 + 128]
            mm(pm[0:128, 0:M], lhsT, pj, reads=["b3", "pj_lo", "pj_hi", bn], writes=[bn])
            nslot = tc * HEADS + h
            dst = a_big[:, nslot * M:(nslot + 1) * M]
            act(AF.Exp, dst, pm[0:128, 0:M], reads=[bn, "sq"],
                writes=[f"B{nslot}"], bias=s_q[:, tc * 16 + h: tc * 16 + h + 1])
            vred(smc(C_RMQ + tc * 16 + h), pm[0:128, 0:M], ALU.max,
                 reads=[bn, "rmq"], writes=["rmq"] if (h == HEADS - 1 and tc == TC - 1) else [])

        def emit_ctx(h, tc):
            nslot = tc * HEADS + h
            for mc in range(2):
                col = h * 128 + mc * 64
                pcb = pc[col // 512]
                off = col % 512
                mm(pcb[:, off:off + 64],
                   a_big[:, nslot * M + mc * 128: nslot * M + (mc + 1) * 128],
                   a_b4[:, tc * 1024 + h * 64: tc * 1024 + (h + 1) * 64],
                   reads=[f"B{nslot}", "b4", f"bc{col // 512}"],
                   writes=[f"bc{col // 512}"], start=False, stop=(tc == TC - 1))
                mm(pD[:, mc * 16 + h: mc * 16 + h + 1],
                   a_big[:, nslot * M + mc * 128: nslot * M + (mc + 1) * 128],
                   c_oc[:], reads=[f"B{nslot}", "oc", "bD"],
                   writes=["bD"], start=False, stop=(tc == TC - 1))

        hts = [(h, tc) for h in range(HEADS) for tc in range(TC)]
        for i, (h, tc) in enumerate(hts):
            emit_xp(h, tc)
            if i > 0:
                emit_ctx(*hts[i - 1])
        emit_ctx(*hts[-1])
        for cj in range(NCH):
            for tc in range(TC):
                mm(pD[:, 32 + cj:32 + cj + 1],
                   a_b4[:, tc * 1024 + cj * 128: tc * 1024 + (cj + 1) * 128],
                   c_oc[:], reads=["b4", "oc", "bD"],
                   writes=["bD"], start=False, stop=(tc == TC - 1))
        for tc in range(TC):
            tt(smc(C_RM, 16), smc(C_RM, 16), smc(C_RMQ + tc * 16, 16), ALU.max,
               reads=["rm", "rmq"], writes=["rm"])
        for i in range(4):
            tt(s_ctx[:, i * 512:(i + 1) * 512], s_ctx[:, i * 512:(i + 1) * 512],
               pc[i][:, 0:512], ALU.add, reads=["ctxa", f"bc{i}"], writes=["ctxa"])
        tt(smc(C_S0, 32), smc(C_S0, 32), pD[:, 0:32], ALU.add,
           reads=["s0a", "bD"], writes=["s0a"])
        tt(smc(C_VS, 8), smc(C_VS, 8), pD[:, 32:40], ALU.add,
           reads=["vsa", "bD"], writes=["vsa"])
        if t < NT - 1:
            for i in range(4):
                memset(pc[i][:, 0:512], 0.0, reads=["ctxa"], writes=[f"bc{i}"])

    P.marks["1b"] = len(P.ops)
    ld(cc_in[:, 0:2048], s_ctx[:], reads=["ctxa"], writes=["cci0"])
    ld(cc_in[:, 2048:2080], smc(C_S0, 32), reads=["s0a"], writes=["cci1"])
    ld(cc_in[:, 2080:2088], smc(C_VS, 8), reads=["vsa"], writes=["cci2"])
    ts(smc(C_TMP, 16), smc(C_RM, 16), smc(1280, 1), None, ALU.mult,
       reads=["rm", "parc"], writes=["slA"])
    ld(cc_in[:, 2088:2104], smc(C_TMP, 16), reads=["slA"], writes=["cci3"])
    ts(smc(C_TMP, 16), smc(C_RM, 16), smc(1281, 1), None, ALU.mult,
       reads=["rm", "parc", "cci3", "slA"], writes=["slB"])
    ld(cc_in[:, 2104:2120], smc(C_TMP, 16), reads=["slB"], writes=["cci4"])
    P.op("gp", lambda eng: eng.collective_compute(
        "AllReduce", ALU.add, replica_groups=[[0, 1], [2, 3], [4, 5], [6, 7]],
        ins=[cc_in[:]], outs=[cc_out[:]]),
        reads=["cci0", "cci1", "cci2", "cci3", "cci4"], writes=["cco"], cc=True)
    ld(s_ctx[:], cc_out[:, 0:2048], reads=["cco", "ctxa"], writes=["ctxa"])
    ld(smc(C_S0, 32), cc_out[:, 2048:2080], reads=["cco", "s0a"], writes=["s0a"])
    ld(smc(C_VS, 8), cc_out[:, 2080:2088], reads=["cco", "vsa"], writes=["vsa"])
    ld(smc(C_TMP, 16), cc_out[:, 2088:2104], reads=["cco", "slB"], writes=["slA"])
    ld(smc(C_RM, 16), cc_out[:, 2104:2120], reads=["cco", "rm"], writes=["slB2"])
    tt(smc(C_TMP, 16), smc(C_TMP, 16), smc(C_RM, 16), ALU.max,
       reads=["slA", "slB2"], writes=["mx"])
    petr(pD[0:16, 0:128], smc(C_TMP, 16), c_If[:], reads=["mx", "cIf", "bD"], writes=["bD"])
    vred(smc(C_ST)[0:16, :], pD[0:16, 0:128], ALU.max, reads=["bD"], writes=["stab"])
    ts(smc(C_TMP)[0:16, :], smc(C_NS0)[0:16, :], -0.5, None, ALU.mult,
       reads=["ns0", "mx"], writes=["hb"])
    act(AF.Exp, smc(C_GH)[0:16, :], smc(C_ST)[0:16, :], reads=["stab", "hb"],
        writes=["ghc"], bias=smc(C_TMP)[0:16, :], scale=-0.5)
    petr(pD[0:1, 0:16], smc(C_GH)[0:16, :], c_If[0:16, 0:16],
         reads=["ghc", "cIf", "bD"], writes=["bD"])
    acopy(smr(R_G16, 16), pD[0:1, 0:16], reads=["bD"], writes=["ghr"])
    petr(pD[0:8, 0:128], smc(C_VS, 8), c_If[:], reads=["vsa", "cIf", "bD"], writes=["bD"])
    acopy(s_land[0:8, 0:128], pD[0:8, 0:128], reads=["bD"], writes=["vst", "rows8"])
    ld(r_vsl[:], s_land[0:8, 0:128], reads=["vst"], writes=["vsl", "f2t"])
    for i in range(4):
        vcopy(r_gch[:].rearrange("o (h r) -> o h r", r=128),
              smr(R_G16 + 4 * i, 4).unsqueeze(2).broadcast_to([1, 4, 128]),
              reads=["ghr", "gch"], writes=["gch"])
        vcopy(r_vch[:].rearrange("o (h m e) -> o h m e", m=2, e=64),
              r_vsl[0:1, 4 * i * 64:(4 * i + 4) * 64].rearrange("o (h e) -> o h e", e=64)
              .unsqueeze(2).broadcast_to([1, 4, 2, 64]),
              reads=["vsl", "vchb"], writes=["vchb"])
        ts(r_vch[:], r_vch[:], EPS_K, None, ALU.mult, reads=["vchb"], writes=["vchb"])
        mm(pC[:, 0:512], r_one[:], r_gch[:], reads=["gch", "r_1", "bC"], writes=["bC"])
        mm(pD[:, 0:512], r_one[:], r_vch[:], reads=["vchb", "r_1", "bD"], writes=["bD"])
        tt(s_ctx[:, i * 512:(i + 1) * 512], s_ctx[:, i * 512:(i + 1) * 512], pC[:, 0:512],
           ALU.mult, reads=["ctxa", "bC"], writes=["ctxa"])
        tt(s_ctx[:, i * 512:(i + 1) * 512], s_ctx[:, i * 512:(i + 1) * 512], pC[:, 0:512],
           ALU.mult, reads=["ctxa", "bC"], writes=["ctxa"])
        tt(s_ctx[:, i * 512:(i + 1) * 512], s_ctx[:, i * 512:(i + 1) * 512], pD[:, 0:512],
           ALU.add, reads=["ctxa", "bD"], writes=["ctxa"])
    P.merge("f2t", ["vsl", "gch", "vchb"])
    tt(smr(R_G2, 16), smr(R_G16, 16), smr(R_G16, 16), ALU.mult, reads=["ghr"], writes=["g2r"])
    vcopy(smr(C_G2E, 32).rearrange("o (m h) -> o m h", m=2),
          smr(R_G2, 16).unsqueeze(1).broadcast_to([1, 2, 16]), reads=["g2r"], writes=["g2e"])
    mm(pC[:, 0:32], r_one[:], smr(C_G2E, 32), reads=["g2e", "r_1", "bC"], writes=["bC"])
    tt(smc(C_KS, 32), smc(C_S0, 32), pC[:, 0:32], ALU.mult, reads=["s0a", "bC"], writes=["ks"])
    ts(smc(C_KS, 32), smc(C_KS, 32), EPS_K * 2 * TOK, None, ALU.add, reads=["ks"], writes=["ks"])
    vcopy(s_caug[:, :].rearrange("p (g c) -> p g c", c=65)[:, :, 0:64],
          s_ctx[:, 0:2048].rearrange("p (g e) -> p g e", e=64),
          reads=["ctxa"], writes=["caug"])
    vcopy(s_caug[:, :].rearrange("p (h m c) -> p h m c", h=16, m=2)[:, :, :, 64:65],
          smc(C_KS, 32).rearrange("p (m h) -> p h m", m=2).unsqueeze(3),
          reads=["ks", "caug"], writes=["caug"])
    for h in range(HEADS):
        for mc in range(2):
            mm(pD[0:65, h:h + 1], s_caug[:, (h * 2 + mc) * 65:(h * 2 + mc + 1) * 65],
               c_oc[:], reads=["caug", "oc", "bD"],
               writes=["bD"], start=(h == 0 and mc == 0), stop=(h == HEADS - 1 and mc == 1))
    act(AF.Copy, s_ecs[:], pD[0:65, 0:16], reads=["bD"], writes=["ecs"], scale=EPS_K)
    petr(pC[0:16, 0:1], s_ecs[64:65, 0:16], c_If[64:65, 64:65],
         reads=["ecs", "cIf", "bC"], writes=["bC"])
    acopy(smc(1600)[0:16, :], pC[0:16, 0:1], reads=["bC"], writes=["e64"])
    ld(r_nb[:], brows[5:6, 0:128], reads=["xt"], writes=["r_nb"], q="gp")
    memset(c_dmask[:], 0.0, reads=["xt"], writes=["dmask"])
    for g in range(32):
        vcopy(c_dmask[:, 16 * g + g // 2: 16 * g + g // 2 + 1],
              s_caug[:, g * 65 + 64: g * 65 + 65], reads=["caug", "dmask"],
              writes=["dmask"] if g == 31 else [])
    ld(c_wq[:].rearrange("p (c o) -> p c o", o=1024), wqT[:].rearrange("(c p) o -> p c o", p=128),
       reads=["wk", "b3"], writes=["wq"], q="gp")
    ld(c_wo[:].rearrange("p (c o) -> p c o", o=1024), woT[:].rearrange("(c p) o -> p c o", p=128),
       reads=["wv", "b4"], writes=["wo"], q="gp")

    P.marks["cc"] = len(P.ops)
    P.merge("b1_1", ["vst", "vsl", "rows8"])
    memset(a_xt[0:1, 0:1], 0.0,
           writes=["xt", "w1p0", "w1p1", "xr0", "xr1", "out0", "out1", "ssb", "nm",
                   "att", "ctxa", "dsl0", "dsl1", "dwp", "dwn", "gld"])
    dsslot = [r_ds0, r_ds1]
    for t in range(NT):
        bi = t % 2
        xnb = a_b1[bi]
        ld(xnb[:, 0:W], xn_scr[:, t * W:(t + 1) * W],
           reads=[f"xnscr{t}"], writes=[f"b1_{bi}"])
        for ci in range(NCH):
            pairs = [(c_wq[:, cj * 1024 + ci * 128: cj * 1024 + ci * 128 + 128],
                      xnb[:, cj * TILE:(cj + 1) * TILE]) for cj in range(NCH)]
            pm = pA if ci % 2 == 0 else pB
            bn = "bA" if ci % 2 == 0 else "bB"
            mmg(pm[:, 0:TILE], pairs, reads=[f"b1_{bi}", "wq", bn], writes=[bn])
            act(AF.Identity, a_b2[:, ci * TILE:(ci + 1) * TILE], pm[:, 0:TILE],
                reads=[bn, "bcols", "b2"], writes=["b2"],
                bias=c_bcols[:, 0 * NCH + ci:0 * NCH + ci + 1])
        tt(a_b4[:, 0:W], a_b2[:, 0:W], a_b2[:, 0:W], ALU.mult,
           reads=["b2", "b4"], writes=["b4"])
        mmg(pC[0:16, 0:TILE],
            [(c_indb[:, ci * 16:(ci + 1) * 16], a_b4[:, ci * TILE:(ci + 1) * TILE])
             for ci in range(NCH)], reads=["b4", "cind", "bC"], writes=["bC"])
        acopy(s_ds[0:16, 0:TILE], pC[0:16, 0:TILE], reads=["bC", "ds"], writes=["ds"])
        for h in range(HEADS):
            ci = h // 2
            r0 = (h % 2) * 64
            pj = c_pj[0:64, :] if r0 == 0 else c_pj[64:128, :]
            for tc in range(TC):
                pm = pA if (h * TC + tc) % 2 == 0 else pB
                bn = "bA" if (h * TC + tc) % 2 == 0 else "bB"
                lhsT = a_b2[r0:r0 + 64, ci * TILE + tc * 128: ci * TILE + tc * 128 + 128]
                mm(pm[0:128, 0:M], lhsT, pj, reads=["b2", "pj_lo", "pj_hi", bn], writes=[bn])
                vred(smc(C_RMQ + tc * 16 + h), pm[0:128, 0:M], ALU.max,
                     reads=[bn, "rmq"], writes=["rmq"] if (h == HEADS - 1 and tc == TC - 1) else [])
        for tc in range(TC):
            petr(pC[0:16, 128:256], smc(C_RMQ + tc * 16, 16), c_If[:],
                 reads=["rmq", "cIf", "bC"], writes=["bC"])
            tt(s_ds[0:16, tc * 128:(tc + 1) * 128], s_ds[0:16, tc * 128:(tc + 1) * 128],
               pC[0:16, 128:256], ALU.add, reads=["ds", "bC"], writes=["ds"])
        for h in range(HEADS):
            ci = h // 2
            r0 = (h % 2) * 64
            pj = c_pj[0:64, :] if r0 == 0 else c_pj[64:128, :]
            sl = f"dsl{h % 2}"
            ld(r_db[h % 2][:], s_ds[h:h + 1, 0:TILE], reads=["ds", sl], writes=[sl],
               q="gp")
            for mc in range(2):
                pm = pA if mc == 0 else pB
                bn = "bA" if mc == 0 else "bB"
                eslot = 2 * (h * 2 + mc)
                mm(pm[:, 0:TILE], pj[:, mc * 128:(mc + 1) * 128],
                   a_b2[r0:r0 + 64, ci * TILE: ci * TILE + TILE],
                   reads=["b2", "pj_lo", "pj_hi", bn], writes=[bn], stop=False)
                mm(pm[:, 0:TILE], r_nb[:], r_db[h % 2][:],
                   reads=["r_nb", sl], writes=[bn], start=False)
                act(AF.Exp, a_big[:, (h * 2 + mc) * TILE:(h * 2 + mc + 1) * TILE],
                    pm[:, 0:TILE], reads=[bn],
                    writes=[f"B{eslot}", f"B{eslot + 1}"])
        memset(pc[0][0:16, 0:TILE], 0.0, reads=["bc0"], writes=["bc0"])
        for g in range(32):
            mm(pc[0][0:16, 0:TILE], c_dmask[:, 16 * g:16 * g + 16],
               a_big[:, g * TILE:(g + 1) * TILE],
               reads=["dmask", f"B{2 * g}", f"B{2 * g + 1}", "bc0"], writes=["bc0"],
               start=False, stop=(g == 31))
        ts(s_ds[0:16, 0:TILE], pc[0][0:16, 0:TILE], smc(1600)[0:16, :], None,
           ALU.add, reads=["e64", "bc0", "ds"], writes=["ds"])
        recipv(s_ds[0:16, 0:TILE], s_ds[0:16, 0:TILE], reads=["ds"], writes=["ds"])
        for h in range(HEADS):
            bn = "bC" if h % 2 == 0 else "bD"
            pm = pC if h % 2 == 0 else pD
            for mc in range(2):
                eslot = 2 * (h * 2 + mc)
                mm(pm[0:65, 0:TILE], s_caug[:, (h * 2 + mc) * 65:(h * 2 + mc + 1) * 65],
                   a_big[:, (h * 2 + mc) * TILE:(h * 2 + mc + 1) * TILE],
                   reads=["caug", f"B{eslot}", f"B{eslot + 1}", bn], writes=[bn],
                   start=(mc == 0), stop=(mc == 1))
            act(AF.Identity, a_nm[:], pm[0:65, 0:TILE], reads=[bn, "ecs"],
                writes=["nm"] + (["ssb"] if h == 0 else []), bias=s_ecs[:, h:h + 1])
            sl = f"dsl{h % 2}"
            ld(dsslot[h % 2][:], s_ds[h:h + 1, 0:TILE], reads=["ds", sl], writes=[sl],
               q="gp")
            pm2 = pD if h % 2 == 0 else pC
            bn2 = "bD" if h % 2 == 0 else "bC"
            mm(pm2[0:64, 0:TILE], r_one[0:1, 0:64], dsslot[h % 2][:],
               reads=["r_1", sl, bn2], writes=[bn2])
            tt(s_att[(h % 2) * 64:(h % 2) * 64 + 64, (h // 2) * TILE:(h // 2 + 1) * TILE],
               a_nm[0:64, :], pm2[0:64, 0:TILE], ALU.mult,
               reads=["nm", bn2, "att"], writes=["att"])
        for ci in range(NCH):
            pairs = [(c_wo[:, cj * 1024 + ci * 128: cj * 1024 + ci * 128 + 128],
                      s_att[:, cj * TILE:(cj + 1) * TILE]) for cj in range(NCH)]
            pm = pA if ci % 2 == 0 else pB
            bn = "bA" if ci % 2 == 0 else "bB"
            mmg(pm[:, 0:TILE], pairs, reads=["att", "wo", bn], writes=[bn])
            stt(a_f1[:, ci * TILE:(ci + 1) * TILE], pm[:, 0:TILE],
                c_bcols[:, 3 * NCH + ci:3 * NCH + ci + 1], xnb[:, ci * TILE:(ci + 1) * TILE],
                ALU.add, ALU.add, reads=[bn, "bcols", f"b1_{bi}", "f1t", "w2pb0", "w2pb1"],
                writes=["f1t"])
        acopy(a_big[:, 0:W], a_f1[:, 0:W], reads=["f1t"] + BS(0, W), writes=BS(0, W))
        tt(a_big[:, W:2 * W], a_big[:, 0:W], a_big[:, 0:W], ALU.mult,
           reads=BS(0, W) + BS(W, 2 * W), writes=BS(W, 2 * W))
        mmg(pC[0:1, 0:TILE], [(c_oc[:], a_big[:, ci * TILE:(ci + 1) * TILE]) for ci in range(NCH)],
            reads=BS(0, W) + ["oc", "bC"], writes=["bC"])
        mmg(pD[0:1, 0:TILE], [(c_oc[:], a_big[:, W + ci * TILE:W + (ci + 1) * TILE])
                                for ci in range(NCH)],
            reads=BS(W, 2 * W) + ["bD"], writes=["bD"])

        def ln_smalls():
            mu, var = smr(SM_MU, TILE), smr(SM_VAR, TILE)
            ts(mu, pC[0:1, 0:TILE], 1.0 / D, None, ALU.mult, reads=["bC"], writes=["mu", "dact", "s0t", "d2pr", "d2nr", "sctmp", "stab0", "ghr", "g2r"])
            ts(var, pD[0:1, 0:TILE], 1.0 / D, None, ALU.mult, reads=["bD"], writes=["var", "s0t", "wbm", "bm", "sc", "wsc"])
            tt(r_lnA[:], mu, mu, ALU.mult, reads=["mu", "lnA"], writes=["lnA"])
            tt(var, var, r_lnA[:], ALU.subtract, reads=["var", "lnA"], writes=["var"])
            ts(var, var, LN_EPS, None, ALU.add, reads=["var"], writes=["var"])
            newton_rsqrt(r_lnA[:], var, r_lnB[0:1, :], ["var", "lnB0"], "lnA")
            tt(r_lnB[0:1, :], mu, r_lnA[:], ALU.mult, reads=["mu", "lnA", "lnB0"], writes=["lnB0"])
            ts(r_lnB[0:1, :], r_lnB[0:1, :], -1.0, None, ALU.mult, reads=["lnB0"], writes=["lnB0"])
        ln_smalls()
        mm(pC[:, 0:TILE], r_one[:], r_lnA[:], reads=["lnA", "r_1", "bC"], writes=["bC"])
        mm(pD[:, 0:TILE], r_one[:], r_lnB[0:1, :], reads=["lnB0", "r_1", "bD"], writes=["bD"])
        for ci in range(NCH):
            tt(a_f2[:, ci * TILE:(ci + 1) * TILE], a_f1[:, ci * TILE:(ci + 1) * TILE],
               pC[:, 0:TILE], ALU.mult, reads=["f1t", "bC", "f2t"], writes=["f2t"])
            tt(a_f2[:, ci * TILE:(ci + 1) * TILE], a_f2[:, ci * TILE:(ci + 1) * TILE],
               pD[:, 0:TILE], ALU.add, reads=["bD", "f2t"], writes=["f2t"])
            ts(a_f2[:, ci * TILE:(ci + 1) * TILE], a_f2[:, ci * TILE:(ci + 1) * TILE],
               c_gcols[:, ci:ci + 1], c_gcols[:, 8 + ci:9 + ci], ALU.mult, ALU.add,
               reads=["gcols", "f2t"], writes=["f2t"], q="gp")
        acopy(a_b3[:, 0:W], a_f2[:, 0:W], reads=["f2t", "b3"], writes=["b3"])
        for fo in range(32):
            pb = fo % 2
            ld(a_w1p[pb][:], w1_scr[:, fo * D:(fo + 1) * D],
               reads=["w1s"], writes=[f"w1p{pb}"])
            pairs = [(a_w1p[pb][:, cj * 128:(cj + 1) * 128],
                      a_b3[:, cj * TILE:(cj + 1) * TILE]) for cj in range(NCH)]
            pm = pA if fo % 2 == 0 else pB
            bn = "bA" if fo % 2 == 0 else "bB"
            mmg(pm[:, 0:TILE], pairs, reads=[f"w1p{pb}", "b3", bn], writes=[bn])
            act(AF.Relu, a_big[:, fo * TILE:(fo + 1) * TILE], pm[:, 0:TILE],
                reads=[bn, "bcols"] + BS(fo * TILE, (fo + 1) * TILE),
                writes=BS(fo * TILE, (fo + 1) * TILE),
                bias=c_bcols[:, 6 * NCH + fo:6 * NCH + fo + 1])
        for ci in range(NCH):
            pb = ci % 2
            ld(a_w2p[pb][:], w2_scr[:, ci * DFF:(ci + 1) * DFF],
               reads=["w2s"], writes=[f"w2pb{pb}", "f1t"])
            pairs = [(a_w2p[pb][:, fo * 128:(fo + 1) * 128],
                      a_big[:, fo * TILE:(fo + 1) * TILE]) for fo in range(32)]
            pm = pA if ci % 2 == 0 else pB
            bn = "bA" if ci % 2 == 0 else "bB"
            mmg(pm[:, 0:TILE], pairs,
                reads=[f"w2pb{pb}", bn] + BS(0, 16384), writes=[bn])
            stt(a_f3[:, ci * TILE:(ci + 1) * TILE], pm[:, 0:TILE],
                c_bcols[:, 4 * NCH + ci:4 * NCH + ci + 1], a_f2[:, ci * TILE:(ci + 1) * TILE],
                ALU.add, ALU.add, reads=[bn, "bcols", "f2t", "f3t"], writes=["f3t"])
        acopy(a_big[:, 0:W], a_f3[:, 0:W], reads=["f3t"] + BS(0, W), writes=BS(0, W))
        tt(a_big[:, W:2 * W], a_big[:, 0:W], a_big[:, 0:W], ALU.mult,
           reads=BS(0, W) + BS(W, 2 * W), writes=BS(W, 2 * W))
        mmg(pC[0:1, 0:TILE], [(c_oc[:], a_big[:, ci * TILE:(ci + 1) * TILE]) for ci in range(NCH)],
            reads=BS(0, W) + ["oc", "bC"], writes=["bC"])
        mmg(pD[0:1, 0:TILE], [(c_oc[:], a_big[:, W + ci * TILE:W + (ci + 1) * TILE])
                                for ci in range(NCH)],
            reads=BS(W, 2 * W) + ["bD"], writes=["bD"])
        ln_smalls()
        mm(pC[:, 0:TILE], r_one[:], r_lnA[:], reads=["lnA", "r_1", "bC"], writes=["bC"])
        mm(pD[:, 0:TILE], r_one[:], r_lnB[0:1, :], reads=["lnB0", "r_1", "bD"], writes=["bD"])
        ld(a_g[:], g_scr[:, t * W:t * W + 2048], reads=[f"gscr{t}"],
           writes=["att"])
        for ci in range(NCH):
            if ci == 4:
                ld(a_g[:], g_scr[:, t * W + 2048:t * W + 4096],
                   reads=[f"gscr{t}"], writes=["att"])
            ob = 0
            ld(a_xres[ob][:], xT[ci * 128:(ci + 1) * 128, t * TILE:(t + 1) * TILE],
               reads=[f"xr{ob}"], writes=[f"xr{ob}"])
            tt(a_ss[:], a_f3[:, ci * TILE:(ci + 1) * TILE], pC[:, 0:TILE], ALU.mult,
               reads=["f3t", "bC"], writes=["ssb"] + (["nm"] if ci == 0 else []))
            tt(a_ss[:], a_ss[:], pD[:, 0:TILE], ALU.add, reads=["ssb", "bD"], writes=["ssb"])
            ts(a_ss[:], a_ss[:], c_gcols[:, 16 + ci:17 + ci], c_gcols[:, 24 + ci:25 + ci],
               ALU.mult, ALU.add, reads=["gcols", "ssb"], writes=["ssb"], q="gp")
            tt(a_out[ob][:], a_ss[:], a_g[:, (ci % 4) * 512:(ci % 4 + 1) * 512], ALU.mult,
               reads=["ssb", "att", f"out{ob}"], writes=[f"out{ob}"])
            tt(a_out[ob][:], a_out[ob][:], a_xres[ob][:], ALU.add,
               reads=[f"out{ob}", f"xr{ob}"], writes=[f"out{ob}"])
            ld(outT[ci * 128:(ci + 1) * 128, t * TILE:(t + 1) * TILE], a_out[ob][:],
               reads=[f"out{ob}"], writes=["ost"])

    P.marks["end"] = len(P.ops)
    if cut is not None:
        P.ops = P.ops[:cut]
        cnt = {s: 0 for s in P.count}
        for (_q, _e, _w, _tick, _s) in P.ops:
            cnt[_tick[0]] = max(cnt.get(_tick[0], 0), _tick[1])
        P.count = cnt
        P.bufs = {}
        ld(outT[0:128, 0:TILE], a_f1[:, 0:TILE], writes=["fin"])
        P.op("dve", lambda eng: eng.engine_nop(), reads=["fin"])
    else:
        P.op("dve", lambda eng: eng.engine_nop(), reads=["ost"])
    globals()["LAST_PROG"] = P

    from contextlib import ExitStack
    with ExitStack() as _ctx:
        sems = {s: _ctx.enter_context(nc.semaphore("sem_" + s)) for s in P.count}
        blk = _ctx.enter_context(nc.Block())
        P.emit(blk, sems)
    return nc


def _host_layout(inputs, TOK=4096):
    c = np.ascontiguousarray
    f = np.float32
    video = np.asarray(inputs["L0_attn_video"], f)
    B, N, _ = video.shape
    base = {
        "wqT": c(np.asarray(inputs["L0_attn_Wq"], f).T),
        "wkT": c(np.asarray(inputs["L0_attn_Wk"], f).T),
        "wvT": c(np.asarray(inputs["L0_attn_Wv"], f).T),
        "woT": c(np.asarray(inputs["L0_attn_Wo"], f).T),
        "w1T": c(np.asarray(inputs["L0_attn_W1"], f).T),
        "w2T": c(np.asarray(inputs["L0_attn_W2"], f).T),
        "wfn": c(np.asarray(inputs["L0_ffn_weight"], f)),
        "projTd": c(np.asarray(inputs["L0_attn_proj"], f).T),
        "dact": np.asarray(inputs["L0_ffn_delta_act"], f).reshape(1, 1),
        "cI": np.eye(128, dtype=f),
    }
    bc = np.zeros((128, 80), f)
    for j, k in enumerate(["L0_attn_bq", "L0_attn_bk", "L0_attn_bv", "L0_attn_bo",
                           "L0_attn_b2", "L0_ffn_bias"]):
        bc[:, j * 8:(j + 1) * 8] = np.asarray(inputs[k], f).reshape(8, 128).T
    bc[:, 48:80] = np.asarray(inputs["L0_attn_b1"], f).reshape(32, 128).T
    base["bcols"] = bc
    br = np.zeros((8, 1024), f)
    br[0] = np.asarray(inputs["L0_attn_bv"], f)
    br[1] = np.asarray(inputs["L0_attn_g2"], f)
    br[2] = np.asarray(inputs["L0_attn_be2"], f)
    br[3] = np.asarray(inputs["L0_attn_g3"], f)
    br[4] = np.asarray(inputs["L0_attn_be3"], f)
    br[5, 0:128] = -1.0
    br[6, 0:128] = 1.0
    br[7, :] = 1.0
    base["brows"] = br
    gc = np.zeros((128, 32), f)
    for j, k in enumerate(["L0_attn_g2", "L0_attn_be2", "L0_attn_g3", "L0_attn_be3"]):
        gc[:, j * 8:(j + 1) * 8] = np.asarray(inputs[k], f).reshape(8, 128).T
    base["gcols"] = gc
    dw = np.asarray(inputs["L0_ffn_delta_w"], f)
    base["dwb"] = c(dw.reshape(8, 128, 32).transpose(1, 0, 2).reshape(128, 256))
    ind = np.zeros((128, 128), f)
    for ci in range(8):
        for r in range(128):
            ind[r, ci * 16 + 2 * ci + r // 64] = 0.0
    ind[:] = 0.0
    for ci in range(8):
        for r in range(128):
            h = 2 * ci + r // 64
            ind[r, ci * 16 + h] = 0.0625
    base["cind"] = ind

    maps = []
    for core in range(8):
        b, half = core // 2, core % 2
        im = dict(base)
        im["xT"] = c(video[b, half * TOK:(half + 1) * TOK, :].T)
        im["par"] = np.array([[1.0, 0.0]] if half == 0 else [[0.0, 1.0]], f)
        maps.append(im)
    return maps


def kernel(**inputs):
    TOK = 4096
    nc = build_program(TOK=TOK, TILE=512)
    in_maps = _host_layout(inputs, TOK)
    res = run_bass_kernel_spmd(nc, in_maps, core_ids=list(range(8)))
    video = np.asarray(inputs["L0_attn_video"])
    B, N, Dm = video.shape
    out = np.empty((B, N, Dm), np.float32)
    for core in range(8):
        b, half = core // 2, core % 2
        out[b, half * TOK:(half + 1) * TOK, :] = res.results[core]["outT"].T
    return out


# version 07
# speedup vs baseline: 1.0069x; 1.0069x over previous
import numpy as np

import concourse.bass as bass
import concourse.mybir as mybir
from concourse.bass_utils import run_bass_kernel_spmd

F32 = mybir.dt.float32
BF16 = mybir.dt.bfloat16
I32 = mybir.dt.int32
AF = mybir.ActivationFunctionType
ALU = mybir.AluOpType
AX = mybir.AxisListType

D = 1024
HEADS = 16
DH = 64
M = 256
DFF = 4096
NCH = 8
EPS_K = 1e-4
LN_EPS = 1e-5
NORM = DH ** -0.25
LN2 = float(np.log(2.0))

QUEUES = ("sd", "gp", "pe", "act", "dve")
ENG_OF = {"sd": "sync", "gp": "gpsimd", "pe": "tensor", "act": "scalar", "dve": "vector"}


SEMS = ("sd", "gp", "gpd", "cc", "pe", "act", "dve")


class Prog:
    def __init__(self):
        self.ops = []
        self.count = {s: 0 for s in SEMS}
        self.bufs = {}

    def buf(self, name):
        if name not in self.bufs:
            self.bufs[name] = {"w": [], "r": []}
        return self.bufs[name]

    def op(self, q, emit, reads=(), writes=(), extra=(), dma=False, cc=False):
        waits = [w for w in extra if w is not None]
        for b in reads:
            waits += self.buf(b)["w"]
        for b in writes:
            waits += self.buf(b)["r"] + self.buf(b)["w"]
        if cc:
            sem, step = "cc", 1
        elif dma:
            sem, step = "D_" + (writes[0] if writes else "misc"), 16
        else:
            sem, step = q, 1
        if sem not in self.count:
            self.count[sem] = 0
        self.count[sem] += step
        tick = (sem, self.count[sem])
        uniq = {}
        for (ws, wv) in waits:
            if ws == sem and wv >= tick[1]:
                raise RuntimeError(f"self-deadlock {sem} {wv} {tick}")
            uniq[ws] = max(uniq.get(ws, 0), wv)
        self.ops.append((q, emit, [(k, v) for k, v in uniq.items()], tick, step))
        for b in reads:
            self.buf(b)["r"].append(tick)
        for b in writes:
            self.bufs[b] = {"w": [tick], "r": []}
        return tick

    def merge(self, dst, srcs):
        d = self.buf(dst)
        for s in srcs:
            b = self.buf(s)
            d["w"] = d["w"] + b["w"]
            d["r"] = d["r"] + b["r"]

    def emit(self, blk, sems):
        needed = set()
        for _, _, waits, _, _ in self.ops:
            needed.update(waits)
        per_q = {q: [] for q in QUEUES}
        semq = {}
        for (q, emit_fn, waits, tick, step) in self.ops:
            per_q[q].append((emit_fn, waits, tick, step))
            assert semq.setdefault(tick[0], q) == q, f"sem {tick[0]} on two queues"
        last_inc = {s: 0 for s in self.count}

        def make_body(q):
            ops = per_q[q]

            def body(eng):
                waited = {s: 0 for s in self.count}
                for (emit_fn, waits, tick, step) in ops:
                    for (ws, wv) in waits:
                        if wv > waited[ws]:
                            eng.wait_ge(sems[ws], wv)
                            waited[ws] = wv
                    inst = emit_fn(eng)
                    if tick in needed or step == 16 or tick[0] == "cc":
                        assert inst is not None, f"ticketed op returned None on {q}"
                        inst.then_inc(sems[tick[0]], tick[1] - last_inc[tick[0]])
                        last_inc[tick[0]] = tick[1]
            return body

        for q in QUEUES:
            if per_q[q]:
                getattr(blk, ENG_OF[q])(make_body(q))


LAST_PROG = None


def build_program(TOK=4096, TILE=512, cut=None):
    NT = TOK // TILE
    TC = TILE // 128
    W = NCH * TILE
    NB = W // 32
    nc = bass.Bass("TRN2", target_bir_lowering=False, debug=False)

    def din(name, shape, dtype=F32):
        return nc.dram_tensor(name, list(shape), dtype, kind="ExternalInput").ap()

    xT = din("xT", [D, TOK])
    wqT = din("wqT", [D, D]); wkT = din("wkT", [D, D]); wvT = din("wvT", [D, D])
    woT = din("woT", [D, D])
    w1T = din("w1T", [D, DFF]); w2T = din("w2T", [DFF, D])
    wfn = din("wfn", [D, D])
    projTd = din("projTd", [DH, M])
    bcols = din("bcols", [128, 80])
    brows = din("brows", [8, 1024])
    dwb = din("dwb", [128, 256])
    dact = din("dact", [1, 1])
    cI = din("cI", [128, 128])
    cind = din("cind", [128, 128])
    par = din("par", [1, 2])
    gcols = din("gcols", [128, 32])

    outT = nc.dram_tensor("outT", [D, TOK], F32, kind="ExternalOutput").ap()

    xn_scr = nc.dram_tensor("xn_scr", [128, NT * W], BF16, kind="Internal").ap()
    g_scr = nc.dram_tensor("g_scr", [128, NT * W], F32, kind="Internal").ap()
    w1_scr = nc.dram_tensor("w1_scr", [128, 32 * D], BF16, kind="Internal").ap()
    w2_scr = nc.dram_tensor("w2_scr", [128, NCH * DFF], BF16, kind="Internal").ap()
    CCW = 2120
    cc_in = nc.dram_tensor("cc_in", [128, CCW], F32, kind="Internal").ap()
    cc_out = nc.dram_tensor("cc_out", [128, CCW], F32, kind="Internal").ap()

    B0 = 16512

    def S(name, shape, dtype, off):
        assert B0 + off + int(np.prod(shape[1:])) * mybir.dt.size(dtype) <= 229376
        return nc.alloc_sbuf_tensor_at(name, list(shape), dtype, offset=B0 + off)

    a_f1 = S("a_f1", [128, 4096], F32, 0)
    a_w2p = [S(f"a_w2p{i}", [128, 4096], BF16, i * 8192) for i in range(2)]
    a_f2 = S("a_f2", [128, 4096], F32, 16384)
    r_vsl = S("r_vsl", [1, 1024], F32, 16384)
    r_gch = S("r_gch", [1, 512], F32, 20480)
    r_vch = S("r_vch", [1, 512], F32, 22528)
    a_f3 = S("a_f3", [128, 4096], F32, 32768)
    a_xt = S("a_xt", [128, 4096], F32, 49152)
    a_w1p = [S(f"a_w1p{i}", [128, 1024], BF16, 49152 + i * 2048) for i in range(2)]
    a_out = [S("a_out0", [128, 512], F32, 53248)]
    a_xres = [S("a_xres0", [128, 512], F32, 55296)]
    a_ss = S("a_ss", [128, 512], F32, 57344)
    a_nm = S("a_nm", [65, 512], F32, 57344)
    r_ds0 = S("r_ds0", [1, 512], F32, 59392)
    a_b1 = [S(f"a_b1{i}", [128, 4096], BF16, 65536 + i * 8192) for i in range(2)]
    c_dwp = S("c_dwp", [128, 256], F32, 73728)
    c_dwn = S("c_dwn", [128, 256], F32, 74752)
    s_land = S("s_land", [8, 1024], F32, 75776)
    a_b2 = S("a_b2", [128, 4096], BF16, 81920)
    a_b3 = S("a_b3", [128, 4096], BF16, 90112)
    a_b4 = S("a_b4", [128, 4096], BF16, 98304)
    a_i1 = S("a_i1", [128, 2048], I32, 98304)
    a_big = S("a_big", [128, 16384], BF16, 106496)
    a_wqh = S("a_wqh", [128, 4096], F32, 106496)
    c_wqt = S("c_wqt", [128, 8192], BF16, 122880)
    c_wk = S("c_wk", [128, 8192], BF16, 139264)
    c_wq = S("c_wq", [128, 8192], BF16, 139264)
    c_wv = S("c_wv", [128, 8192], BF16, 155648)
    c_wo = S("c_wo", [128, 8192], BF16, 155648)
    s_ctx = S("s_ctx", [128, 2048], F32, 172032)
    s_xst = S("s_xst", [128, 4096], BF16, 172032)
    s_att = S("s_att", [128, 4096], BF16, 172032)
    a_g = S("a_g", [128, 2048], F32, 172032)
    s_caug = S("s_caug", [128, 2080], BF16, 180224)
    s_ds = S("s_ds", [16, 512], F32, 184384)
    s_sm = S("s_sm", [128, 1664], F32, 186432)
    r_ds1 = S("r_ds1", [1, 512], F32, 61440)
    r_one = S("r_one", [1, 128], F32, 193088)
    r_neg1 = S("r_neg1", [1, 128], F32, 193600)
    r_bv = S("r_bv", [1, 1024], F32, 194112)
    r_g2b2 = S("r_g2b2", [2, 1024], F32, 198208)
    r_g3b3 = S("r_g3b3", [2, 1024], F32, 202304)
    r_lnA = S("r_lnA", [1, 512], F32, 206400)
    r_lnB = S("r_lnB", [2, 512], F32, 208448)
    c_pj = S("c_pj", [128, 256], BF16, 210496)
    c_Ib = S("c_Ib", [128, 128], BF16, 211008)
    c_If = S("c_If", [128, 128], F32, 211264)
    c_indb = S("c_indb", [128, 128], BF16, 211776)
    c_bcols = S("c_bcols", [128, 80], F32, 212032)
    s_ecs = S("s_ecs", [65, 16], F32, 212352)
    c_oc = S("c_oc", [128, 1], BF16, 212416)
    c_ocf = S("c_ocf", [128, 1], F32, 212448)
    c_gcols = S("c_gcols", [128, 32], F32, 212480)
    c_dmask = S("c_dmask", [128, 512], BF16, 63488)
    r_db = [S(f"r_db{i}", [1, 512], BF16, 59392 + i * 2048) for i in range(2)]
    r_nb = S("r_nb", [1, 128], BF16, 64512)
    r_bvb = S("r_bvb", [1, 1024], BF16, 194112)
    r_1bv = S("r_1bv", [1, 128], BF16, 196160)

    SM_MU, SM_VAR = 0, 512
    C_RMQ, C_RM, C_S0, C_VS = 1024, 1088, 1104, 1136
    C_D2P, C_D2N, C_NS0, C_GH, C_ST = 1144, 1145, 1146, 1147, 1148
    C_TMP = 1152
    C_KS = 1184
    C_PAR = 1536
    C_G2E = 1540
    R_G16, R_G2, R_S0B = 0, 16, 40
    s_q = s_sm[:, 1216:1280]

    nc.bump_sbuf(B0 + 212608 - nc.sbuf_base)

    psum = {n: nc.alloc_psum_tensor(n, [128, 512], F32)
            for n in ("pA", "pB", "pC", "pD", "p0", "p1", "p2", "p3")}
    pA, pB, pC, pD = psum["pA"], psum["pB"], psum["pC"], psum["pD"]
    pc = [psum[f"p{i}"] for i in range(4)]

    P = Prog()
    P.marks = {}

    def ld(dst, src, reads=(), writes=(), extra=(), q="sd"):
        return P.op(q, lambda eng, d=dst, s=src: eng.dma_start(out=d, in_=s),
                    reads=reads, writes=writes, extra=extra, dma=True)

    def mm(out_ap, l, r, reads=(), writes=(), extra=(), start=True, stop=True, tp=None):
        def emit(eng):
            kw = {"tile_position": tp} if tp is not None else {}
            return eng.matmul(out_ap, l, r, start=start, stop=stop,
                              skip_group_check=True, **kw)
        return P.op("pe", emit, reads=reads, writes=writes, extra=extra)

    def mmg(out_ap, pairs, reads=(), writes=(), extra=(), start=True, stop=True, tp=None):
        def emit(eng):
            inst = None
            n = len(pairs)
            kw = {"tile_position": tp} if tp is not None else {}
            for i, (l, r) in enumerate(pairs):
                inst = eng.matmul(out_ap, l, r, start=(start and i == 0),
                                  stop=(stop and i == n - 1), skip_group_check=True, **kw)
            return inst
        return P.op("pe", emit, reads=reads, writes=writes, extra=extra)

    def rl(*aps):
        return [f"rl_{ap.tensor.name}" for ap in aps
                if type(ap.tensor).__name__ == "PSumTensorHandle"]

    def act(fn, out, in_, reads=(), writes=(), extra=(), **kw):
        return P.op("act", lambda eng: eng.activation(out, in_, fn, **kw),
                    reads=reads, writes=list(writes) + rl(in_), extra=extra)

    def acopy(out, in_, reads=(), writes=(), extra=(), scale=None):
        if scale is None:
            return P.op("act", lambda eng: eng.copy(out, in_),
                        reads=reads, writes=list(writes) + rl(in_), extra=extra)
        return P.op("act", lambda eng: eng.activation(out, in_, AF.Copy, scale=scale),
                    reads=reads, writes=list(writes) + rl(in_), extra=extra)

    def ts(out, in_, s1, s2, op0, op1=None, reads=(), writes=(), extra=(), q="dve"):
        if op1 is None:
            return P.op(q, lambda eng: eng.tensor_scalar(out, in_, s1, None, op0=op0),
                        reads=reads, writes=list(writes) + rl(in_), extra=extra)
        return P.op(q, lambda eng: eng.tensor_scalar(out, in_, s1, s2, op0=op0, op1=op1),
                    reads=reads, writes=list(writes) + rl(in_), extra=extra)

    def tt(out, a, b, alu, reads=(), writes=(), extra=(), q="dve"):
        return P.op(q, lambda eng: eng.tensor_tensor(out, a, b, op=alu),
                    reads=reads, writes=list(writes) + rl(a, b), extra=extra)

    def stt(out, in0, scalar, in1, op0, op1, reads=(), writes=(), extra=(), q="dve"):
        return P.op(q, lambda eng: eng.scalar_tensor_tensor(out, in0, scalar, in1,
                                                            op0=op0, op1=op1),
                    reads=reads, writes=list(writes) + rl(in0, in1), extra=extra)

    def vcopy(out, in_, reads=(), writes=(), extra=(), q="dve"):
        return P.op(q, lambda eng: eng.tensor_copy(out, in_),
                    reads=reads, writes=list(writes) + rl(in_), extra=extra)

    def vred(out, in_, alu, reads=(), writes=(), extra=(), absval=None):
        return P.op("dve", lambda eng: eng.tensor_reduce(out, in_, axis=AX.X, op=alu,
                                                         apply_absolute_value=absval),
                    reads=reads, writes=list(writes) + rl(in_), extra=extra)

    def vtrans(out, in_, reads=(), writes=(), extra=()):
        return P.op("dve", lambda eng: eng.transpose(out, in_),
                    reads=reads, writes=list(writes) + rl(in_), extra=extra)

    def memset(ap, val, reads=(), writes=(), extra=(), q="dve"):
        return P.op(q, lambda eng: eng.memset(ap, val), reads=reads, writes=writes, extra=extra)

    def recipv(out, in_, reads=(), writes=(), extra=()):
        return P.op("dve", lambda eng: eng.reciprocal(out, in_),
                    reads=reads, writes=list(writes) + rl(in_), extra=extra)

    def petr(out, in_, ident, reads=(), writes=(), extra=()):
        return P.op("pe", lambda eng: eng.transpose(out, in_, ident),
                    reads=reads, writes=writes, extra=extra)

    smc = lambda c0, n=1: s_sm[:, c0:c0 + n]
    smr = lambda c0, n: s_sm[0:1, c0:c0 + n]

    def newton_rsqrt(dst, src, tmp, rbufs, wbuf):
        ts(dst.bitcast(I32), src.bitcast(I32), 0x7F800000, 23,
           ALU.bitwise_and, ALU.logical_shift_right, reads=rbufs, writes=[wbuf])
        ts(dst.bitcast(I32), dst.bitcast(I32), -1, 380, ALU.mult, ALU.add,
           reads=[wbuf], writes=[wbuf])
        ts(dst.bitcast(I32), dst.bitcast(I32), 1, 23,
           ALU.logical_shift_right, ALU.logical_shift_left, reads=[wbuf], writes=[wbuf])
        for _ in range(6):
            tt(tmp, dst, dst, ALU.mult, reads=[wbuf], writes=["rsq_tmp"])
            tt(tmp, tmp, src, ALU.mult, reads=["rsq_tmp"] + rbufs, writes=["rsq_tmp"])
            ts(tmp, tmp, -0.5, 1.5, ALU.mult, ALU.add, reads=["rsq_tmp"], writes=["rsq_tmp"])
            tt(dst, dst, tmp, ALU.mult, reads=[wbuf, "rsq_tmp"], writes=[wbuf])

    def fp4_round(tin, iout, tmpf, rbuf, ibuf, obuf):
        ts(iout, tin.bitcast(I32), 0x7F800000, 23,
           ALU.bitwise_and, ALU.logical_shift_right, reads=[rbuf], writes=[ibuf])
        ts(iout, iout, 127, 0, ALU.subtract, ALU.max, reads=[ibuf], writes=[ibuf])
        ts(iout, iout, 2, None, ALU.min, reads=[ibuf], writes=[ibuf])
        ts(iout, iout, 149, None, ALU.add, reads=[ibuf], writes=[ibuf])
        ts(iout, iout, 23, 0x400000, ALU.logical_shift_left, ALU.bitwise_or,
           reads=[ibuf], writes=[ibuf])
        tt(tmpf, tin, iout.bitcast(F32), ALU.add, reads=[rbuf, ibuf], writes=[obuf])
        tt(tmpf, tmpf, iout.bitcast(F32), ALU.subtract, reads=[obuf, ibuf], writes=[obuf])

    def tanh_2pow(dst_p, dst_n, src, t1, t2, t3, rbufs, wp, wn, tb):
        ts(t1, src, 0.5, None, ALU.mult, reads=rbufs, writes=[tb])
        tt(t2, t1, t1, ALU.mult, reads=[tb], writes=[tb])
        ts(t3, t2, -17.0 / 315.0, 2.0 / 15.0, ALU.mult, ALU.add, reads=[tb], writes=[tb])
        tt(t3, t3, t2, ALU.mult, reads=[tb], writes=[tb])
        ts(t3, t3, -1.0 / 3.0, None, ALU.add, reads=[tb], writes=[tb])
        tt(t3, t3, t2, ALU.mult, reads=[tb], writes=[tb])
        ts(t3, t3, 1.0, None, ALU.add, reads=[tb], writes=[tb])
        tt(dst_p, t3, t1, ALU.mult, reads=[tb], writes=[wp])
        for _ in range(3):
            tt(t2, dst_p, dst_p, ALU.mult, reads=[wp], writes=[tb])
            ts(t2, t2, 1.0, None, ALU.add, reads=[tb], writes=[tb])
            recipv(t3, t2, reads=[tb], writes=[tb])
            tt(dst_p, dst_p, t3, ALU.mult, reads=[wp, tb], writes=[wp])
            ts(dst_p, dst_p, 2.0, None, ALU.mult, reads=[wp], writes=[wp])
        ts(t1, dst_p, 0.5 * LN2, None, ALU.mult, reads=[wp], writes=[tb])

        def expw(dst, sgn, wd):
            ts(dst, t1, sgn / 9.0, 1.0, ALU.mult, ALU.add, reads=[tb], writes=[wd])
            for kk in range(8, 0, -1):
                tt(t2, dst, t1, ALU.mult, reads=[wd, tb], writes=[tb + "x"])
                ts(dst, t2, sgn / kk, 1.0, ALU.mult, ALU.add, reads=[tb + "x"], writes=[wd])
        expw(dst_p, 1.0, wp)
        expw(dst_n, -1.0, wn)
    def BS(c0, c1):
        return [f"B{i}" for i in range(c0 // 256, (c1 + 255) // 256)]

    P.op("act", lambda eng: eng.preload_act_table(AF.Silu))
    ld(c_If[:], cI[:], writes=["cIf"])
    ld(c_Ib[:], cI[:], writes=["cIb"], q="gp")
    ld(c_indb[:], cind[:], writes=["cind"], q="gp")
    ld(c_bcols[:], bcols[:], writes=["bcols"])
    ld(c_gcols[:], gcols[:], writes=["gcols"])
    ld(c_dwp[:], dwb[:], writes=["dwraw"])
    ld(smr(512, 1), dact[:], writes=["dact"])
    ld(smr(C_PAR, 2), par[:], writes=["par"])
    ld(s_sm[0:64, 0:256], projTd[:], writes=["pjf"])
    ld(s_land[:], brows[:], writes=["rows8"])
    ld(r_bvb[:], s_land[0:1, :], reads=["rows8"], writes=["r_bvb"], q="gp")
    ld(r_1bv[:], s_land[6:7, 0:128], reads=["rows8"], writes=["r_1bv"], q="gp")
    ld(r_g2b2[0:1, :], s_land[1:2, :], reads=["rows8"], writes=["r_g2"])
    ld(r_g2b2[1:2, :], s_land[2:3, :], reads=["rows8"], writes=["r_b2"])
    ld(r_g3b3[0:1, :], s_land[3:4, :], reads=["rows8"], writes=["r_g3"])
    ld(r_g3b3[1:2, :], s_land[4:5, :], reads=["rows8"], writes=["r_b3"])
    ld(r_neg1[:], s_land[5:6, 0:128], reads=["rows8"], writes=["r_n1"])
    ld(r_one[:], s_land[6:7, 0:128], reads=["rows8"], writes=["r_1"])
    memset(c_oc[:], 1.0, writes=["oc"])
    memset(c_ocf[:], 1.0, writes=["ocf"])
    ld(r_lnB[1:2, :], brows[7:8, 0:512], writes=["lnB1"])

    ts(s_sm[0:64, 0:256], s_sm[0:64, 0:256], NORM, None, ALU.mult,
       reads=["pjf"], writes=["pjf"])
    acopy(c_pj[0:64, :], s_sm[0:64, 0:256], reads=["pjf"], writes=["pj_lo"])
    ld(c_pj[64:128, :], c_pj[0:64, :], reads=["pj_lo"], writes=["pj_hi"])
    tt(s_sm[0:64, 0:256], c_pj[0:64, :], c_pj[0:64, :], ALU.mult,
       reads=["pj_lo", "pjf"], writes=["pjsq"])
    mm(pD[0:1, 0:256], c_ocf[0:64, :], s_sm[0:64, 0:256],
       reads=["pjsq", "ocf", "bD"], writes=["bD"])
    ts(smr(520, 256), pD[0:1, 0:256], 0.5, None, ALU.mult, reads=["bD"], writes=["s0t"])
    vred(smr(R_S0B, 1), smr(520, 256), ALU.max, reads=["s0t"], writes=["stab0"])
    mm(pD[0:16, 0:1], r_one[0:1, 0:16], smr(R_S0B, 1),
       reads=["stab0", "r_1", "bD"], writes=["bD"])
    acopy(smc(C_NS0)[0:16, :], pD[0:16, 0:1], reads=["bD"], writes=["ns0"], scale=-1.0)

    P.merge("sctmp", ["s0t"])
    tanh_2pow(smr(530, 1), smr(531, 1), smr(512, 1), smr(532, 1), smr(533, 1), smr(534, 1),
              ["dact"], "d2pr", "d2nr", "sctmp")
    mm(pD[:, 0:2], r_one[:], smr(530, 2), reads=["d2pr", "d2nr", "r_1", "bD"], writes=["bD"])
    acopy(smc(C_D2P, 2), pD[:, 0:2], reads=["bD"], writes=["d2c"])

    mm(pD[:, 2:4], r_one[:], smr(C_PAR, 2), reads=["par", "r_1", "bD"], writes=["bD"])
    acopy(smc(1280, 2), pD[:, 2:4], reads=["bD"], writes=["parc"])

    tanh_2pow(c_dwp[:], c_dwn[:], c_dwp[:],
              a_f1[:, 0:256], a_f1[:, 256:512], a_f1[:, 512:768],
              ["dwraw"], "dwp", "dwn", "f1t")

    ld(c_wk[:].rearrange("p (c o) -> p c o", o=1024), wkT[:].rearrange("(c p) o -> p c o", p=128), writes=["wk"], q="gp")
    ld(c_wv[:].rearrange("p (c o) -> p c o", o=1024), wvT[:].rearrange("(c p) o -> p c o", p=128), writes=["wv"], q="gp")

    for half in range(2):
        ld(a_wqh[:].rearrange("p (c o) -> p c o", o=1024),
           wfn[:].rearrange("(c p) o -> p c o", p=128)[:, half * 4:(half + 1) * 4, :],
           writes=BS(0, 8192), q="gp")
        vred(s_sm[:, 768:896], a_wqh[:].rearrange("p (a b) -> p a b", b=32), ALU.max,
             absval=True, reads=BS(0, 8192), writes=["wbm"])
        ts(a_i1[:, 0:128], smc(768, 128).bitcast(I32), 0x7F800000, 23,
           ALU.bitwise_and, ALU.logical_shift_right, reads=["wbm", "i1t"], writes=["we8"])
        ts(a_f1[:, 768:896], smc(768, 128), 0.0, None, ALU.is_equal,
           reads=["wbm", "f1t"], writes=["wiz"])
        ts(a_i1[:, 128:256], a_i1[:, 0:128], 2, None, ALU.subtract, reads=["we8"], writes=["wsb"])
        ts(a_i1[:, 128:256], a_i1[:, 128:256], 23, None, ALU.logical_shift_left,
           reads=["wsb"], writes=["wsb"])
        ts(smc(896, 128), a_f1[:, 768:896], -1.0, 1.0, ALU.mult, ALU.add,
           reads=["wiz"], writes=["wsc"])
        tt(smc(896, 128), smc(896, 128), a_i1[:, 128:256].bitcast(F32), ALU.mult,
           reads=["wsc", "wsb"], writes=["wsc"])
        tt(smc(896, 128), smc(896, 128), a_f1[:, 768:896], ALU.add,
           reads=["wsc", "wiz"], writes=["wsc"])
        tt(smc(896, 128), smc(896, 128), c_dwp[:, half * 128:half * 128 + 128],
           ALU.mult, reads=["wsc", "dwp"], writes=["wsc"])
        ts(a_i1[:, 128:256], a_i1[:, 0:128], -1, 256, ALU.mult, ALU.add,
           reads=["we8", "wsc"], writes=["wsb"])
        ts(a_i1[:, 128:256], a_i1[:, 128:256], 23, None, ALU.logical_shift_left,
           reads=["wsb"], writes=["wsb"])
        ts(a_f1[:, 896:1024], a_f1[:, 768:896], -1.0, 1.0, ALU.mult, ALU.add,
           reads=["wiz"], writes=["wiv"])
        tt(a_f1[:, 896:1024], a_f1[:, 896:1024], a_i1[:, 128:256].bitcast(F32), ALU.mult,
           reads=["wiv", "wsb"], writes=["wiv"])
        tt(a_f1[:, 896:1024], a_f1[:, 896:1024], a_f1[:, 768:896], ALU.add,
           reads=["wiv", "wiz"], writes=["wiv"])
        tt(a_f1[:, 896:1024], a_f1[:, 896:1024],
           c_dwn[:, half * 128:half * 128 + 128], ALU.mult,
           reads=["wiv", "dwn"], writes=["wiv"])
        for ch in range(2):
            sl = slice(ch * 2048, (ch + 1) * 2048)
            bsl = slice(ch * 64, (ch + 1) * 64)
            tt(a_f2[:, sl].rearrange("p (a b) -> p a b", b=32),
               a_wqh[:, sl].rearrange("p (a b) -> p a b", b=32),
               a_f1[:, 896:1024][:, bsl].unsqueeze(2).broadcast_to([128, 64, 32]),
               ALU.mult, reads=BS(ch * 4096, (ch + 1) * 4096) + ["wiv", "f2t"], writes=["wxs"])
            ts(a_f2[:, sl], a_f2[:, sl], 6.0, -6.0, ALU.min, ALU.max,
               reads=["wxs"], writes=["wxs"])
            fp4_round(a_f2[:, sl], a_i1[:, 0:2048], a_f3[:, sl], "wxs", "wi1", "wr4")
            tt(a_b2[:, sl].rearrange("p (a b) -> p a b", b=32),
               a_f3[:, sl].rearrange("p (a b) -> p a b", b=32),
               smc(896, 128)[:, bsl].unsqueeze(2).broadcast_to([128, 64, 32]),
               ALU.mult, reads=["wr4", "wsc", "wqn"], writes=["wqn"])
        for ocl in range(4):
            oc = half * 4 + ocl
            for icn in range(NCH):
                w_in = a_b2[:, ocl * 1024 + icn * 128: ocl * 1024 + icn * 128 + 128]
                petr(pC[:, 0:64].bitcast(BF16), w_in, c_Ib[:],
                     reads=["wqn", "cIb", "bC"], writes=["bC"])
                dstc = icn * 1024 + oc * 128
                acopy(c_wqt[:, dstc:dstc + 128], pC[:, 0:64].bitcast(BF16),
                      reads=["bC"], writes=BS(8192 + dstc, 8192 + dstc + 128))
    WQT = BS(8192, 16384)

    for cj in range(NCH):
        ld(a_b3[:, 0:4096], w1T[cj * 128:(cj + 1) * 128, :], writes=["b3"], q="gp")
        ld(w1_scr[:].rearrange("p (f c o) -> p f c o", f=32, c=NCH)[:, :, cj, :],
           a_b3[:, 0:4096].rearrange("p (f o) -> p f o", o=128),
           reads=["b3"], writes=["w1s"] if cj == NCH - 1 else [])
    P.merge("b4", ["we8", "wsb", "wi1"])
    for fc in range(32):
        ld(a_b4[:, 0:1024], w2T[fc * 128:(fc + 1) * 128, :], writes=["b4"], q="gp")
        ld(w2_scr[:].rearrange("p (c f o) -> p c f o", c=NCH, f=32)[:, :, fc, :],
           a_b4[:, 0:1024].rearrange("p (c o) -> p c o", o=128),
           reads=["b4"], writes=["w2s"] if fc == 31 else [])
    P.marks["ph0"] = len(P.ops)
    for t in range(NT):
        ld(a_xt[:].rearrange("p (c n) -> p c n", c=NCH),
           xT[:, t * TILE:(t + 1) * TILE].rearrange("(c p) n -> p c n", p=128),
           writes=["xt"])
        acopy(a_b3[:, 0:W], a_xt[:], reads=["xt"], writes=["b3"])
        tt(a_b4[:, 0:W], a_b3[:, 0:W], a_b3[:, 0:W], ALU.mult,
           reads=["b3", "i1t"], writes=["b4"])
        mmg(pC[0:1, 0:TILE], [(c_oc[:], a_b3[:, ci * TILE:(ci + 1) * TILE]) for ci in range(NCH)],
            reads=["b3", "oc", "bC"], writes=["bC"])
        mmg(pD[0:1, 0:TILE], [(c_oc[:], a_b4[:, ci * TILE:(ci + 1) * TILE]) for ci in range(NCH)],
            reads=["b4", "bD"], writes=["bD"])
        mu, var = smr(SM_MU, TILE), smr(SM_VAR, TILE)
        ts(mu, pC[0:1, 0:TILE], 1.0 / D, None, ALU.mult, reads=["bC"], writes=["mu", "dact", "s0t", "d2pr", "d2nr", "sctmp", "stab0", "ghr", "g2r"])
        ts(var, pD[0:1, 0:TILE], 1.0 / D, None, ALU.mult, reads=["bD"], writes=["var", "s0t", "wbm", "bm", "sc", "wsc"])
        tt(r_lnA[:], mu, mu, ALU.mult, reads=["mu", "lnA"], writes=["lnA"])
        tt(var, var, r_lnA[:], ALU.subtract, reads=["var", "lnA"], writes=["var"])
        ts(var, var, LN_EPS, None, ALU.add, reads=["var"], writes=["var"])
        newton_rsqrt(r_lnA[:], var, r_lnB[0:1, :], ["var", "lnB0"], "lnA")
        tt(r_lnB[0:1, :], mu, r_lnA[:], ALU.mult, reads=["mu", "lnA", "lnB0"], writes=["lnB0"])
        ts(r_lnB[0:1, :], r_lnB[0:1, :], -1.0, None, ALU.mult, reads=["lnB0"], writes=["lnB0"])
        mm(pC[:, 0:TILE], r_one[:], r_lnA[:], reads=["lnA", "r_1", "bC"], writes=["bC"])
        mm(pD[:, 0:TILE], r_one[:], r_lnB[0:1, :], reads=["lnB0", "r_1", "bD"], writes=["bD"])
        for ci in range(NCH):
            tt(a_f1[:, ci * TILE:(ci + 1) * TILE], a_xt[:, ci * TILE:(ci + 1) * TILE],
               pC[:, 0:TILE], ALU.mult, reads=["xt", "bC", "f1t"], writes=["f1t"])
            tt(a_f1[:, ci * TILE:(ci + 1) * TILE], a_f1[:, ci * TILE:(ci + 1) * TILE],
               pD[:, 0:TILE], ALU.add, reads=["bD", "f1t"], writes=["f1t"])
        acopy(s_xst[:, 0:W], a_f1[:, 0:W], reads=["f1t", "xstg"], writes=["xstg"])
        ld(xn_scr[:, t * W:(t + 1) * W], s_xst[:, 0:W], reads=["xstg"], writes=[f"xnscr{t}"])
        for ci in range(NCH):
            vtrans(a_f2[:, ci * TILE:(ci + 1) * TILE], a_f1[:, ci * TILE:(ci + 1) * TILE],
                   reads=["f1t", "f2t"], writes=["f2t"])
        vred(s_sm[:, 768:768 + NB], a_f2[:, 0:W].rearrange("p (a b) -> p a b", b=32),
             ALU.max, absval=True, reads=["f2t", "wbm"], writes=["bm"])
        ts(a_i1[:, 0:NB], smc(768, NB).bitcast(I32), 0x7F800000, 23,
           ALU.bitwise_and, ALU.logical_shift_right, reads=["bm"], writes=["e8", "b4"])
        ts(a_f3[:, 0:NB], smc(768, NB), 0.0, None, ALU.is_equal,
           reads=["bm"], writes=["iz", "f3t"])
        ts(a_i1[:, NB:2 * NB], a_i1[:, 0:NB], 2, None, ALU.subtract, reads=["e8"], writes=["sb"])
        ts(a_i1[:, NB:2 * NB], a_i1[:, NB:2 * NB], 23, None, ALU.logical_shift_left,
           reads=["sb"], writes=["sb"])
        ts(smc(896, NB), a_f3[:, 0:NB], -1.0, 1.0, ALU.mult, ALU.add,
           reads=["iz", "wsc"], writes=["sc"])
        tt(smc(896, NB), smc(896, NB), a_i1[:, NB:2 * NB].bitcast(F32), ALU.mult,
           reads=["sc", "sb"], writes=["sc"])
        tt(smc(896, NB), smc(896, NB), a_f3[:, 0:NB], ALU.add, reads=["sc", "iz"], writes=["sc"])
        tt(smc(896, NB), smc(896, NB), smc(C_D2P).broadcast_to([128, NB]), ALU.mult,
           reads=["sc", "d2c"], writes=["sc"])
        ts(a_i1[:, NB:2 * NB], a_i1[:, 0:NB], -1, 256, ALU.mult, ALU.add,
           reads=["e8", "sc"], writes=["sb"])
        ts(a_i1[:, NB:2 * NB], a_i1[:, NB:2 * NB], 23, None, ALU.logical_shift_left,
           reads=["sb"], writes=["sb"])
        ts(a_f3[:, NB:2 * NB], a_f3[:, 0:NB], -1.0, 1.0, ALU.mult, ALU.add,
           reads=["iz"], writes=["iv"])
        tt(a_f3[:, NB:2 * NB], a_f3[:, NB:2 * NB], a_i1[:, NB:2 * NB].bitcast(F32), ALU.mult,
           reads=["iv", "sb"], writes=["iv"])
        tt(a_f3[:, NB:2 * NB], a_f3[:, NB:2 * NB], a_f3[:, 0:NB], ALU.add,
           reads=["iv", "iz"], writes=["iv"])
        tt(a_f3[:, NB:2 * NB], a_f3[:, NB:2 * NB], smc(C_D2N).broadcast_to([128, NB]),
           ALU.mult, reads=["iv", "d2c"], writes=["iv"])
        vcopy(a_f2[:, 0:W].rearrange("p (a b) -> p a b", b=32),
              a_f3[:, NB:2 * NB].unsqueeze(2).broadcast_to([128, NB, 32]),
              reads=["iv", "f2t"], writes=["f2t"])
        for ci in range(NCH):
            vtrans(a_f3[:, ci * TILE:(ci + 1) * TILE], a_f2[:, ci * TILE:(ci + 1) * TILE],
                   reads=["f2t", "iv", "iz", "f3t"], writes=["f3t"])
        tt(a_f2[:, 0:W], a_f1[:, 0:W], a_f3[:, 0:W], ALU.mult,
           reads=["f1t", "f3t", "f2t"], writes=["f2t"])
        ts(a_f2[:, 0:W], a_f2[:, 0:W], 6.0, -6.0, ALU.min, ALU.max,
           reads=["f2t"], writes=["f2t"])
        vcopy(a_f3[:, 0:W].rearrange("p (a b) -> p a b", b=32),
              smc(896, NB).unsqueeze(2).broadcast_to([128, NB, 32]),
              reads=["sc", "f3t"], writes=["f3t"])
        for ci in range(NCH):
            vtrans(a_f1[:, ci * TILE:(ci + 1) * TILE], a_f3[:, ci * TILE:(ci + 1) * TILE],
                   reads=["f3t", "f1t"], writes=["f1t"])
        for ch in range(2):
            sl = slice(ch * 2048, (ch + 1) * 2048)
            fp4_round(a_f2[:, sl], a_i1[:, 0:2048], a_f3[:, sl], "f2t", "i1t", "f3t")
        tt(a_b2[:, 0:W], a_f3[:, 0:W], a_f1[:, 0:W], ALU.mult,
           reads=["f3t", "f1t", "b2"], writes=["b2"])
        P.merge("b4", ["e8", "sb", "i1t"])
        for ci in range(NCH):
            pairs = [(c_wqt[:, cj * 1024 + ci * 128: cj * 1024 + ci * 128 + 128],
                      a_b2[:, cj * TILE:(cj + 1) * TILE]) for cj in range(NCH)]
            pm = pA if ci % 2 == 0 else pB
            bn = "bA" if ci % 2 == 0 else "bB"
            mmg(pm[:, 0:TILE], pairs, reads=["b2", bn] + WQT, writes=[bn])
            act(AF.Silu, a_f3[:, ci * TILE:(ci + 1) * TILE], pm[:, 0:TILE],
                reads=[bn, "bcols", "f3t"], writes=["f3t"],
                bias=c_bcols[:, 5 * NCH + ci:5 * NCH + ci + 1])
        ld(g_scr[:, t * W:(t + 1) * W], a_f3[:, 0:W], reads=["f3t"], writes=[f"gscr{t}"])

    P.marks["1a"] = len(P.ops)
    P.op("act", lambda eng: eng.preload_act_table(AF.Exp))
    memset(s_ctx[:], 0.0, writes=["ctxa", "xstg"])
    P.merge("b1_1", ["dwp", "dwn", "dwraw", "rows8", "vst"])
    memset(smc(C_RM, 16), -1e30, writes=["rm"])
    memset(smc(C_S0, 32), 0.0, writes=["s0a"])
    memset(smc(C_VS, 8), 0.0, writes=["vsa"])
    for i in range(4):
        memset(pc[i][:, 0:512], 0.0, writes=[f"bc{i}"])
    for t in range(NT):
        bi = t % 2
        xnb = a_b1[bi]
        ld(xnb[:, 0:W], xn_scr[:, t * W:(t + 1) * W],
           reads=[f"xnscr{t}"], writes=[f"b1_{bi}"])
        for ci in range(NCH):
            pairs = [(c_wk[:, cj * 1024 + ci * 128: cj * 1024 + ci * 128 + 128],
                      xnb[:, cj * TILE:(cj + 1) * TILE]) for cj in range(NCH)]
            pm = pA if ci % 2 == 0 else pB
            bn = "bA" if ci % 2 == 0 else "bB"
            mmg(pm[:, 0:TILE], pairs, reads=[f"b1_{bi}", "wk", bn], writes=[bn])
            act(AF.Identity, a_b3[:, ci * TILE:(ci + 1) * TILE], pm[:, 0:TILE],
                reads=[bn, "bcols", "b3"], writes=["b3"],
                bias=c_bcols[:, 1 * NCH + ci:1 * NCH + ci + 1])
        for tc in range(TC):
            for half in range(2):
                pairs = [(xnb[:, cj * TILE + tc * 128: cj * TILE + tc * 128 + 128],
                          c_wv[:, cj * 1024 + half * 512: cj * 1024 + (half + 1) * 512])
                         for cj in range(NCH)]
                pm = pA if (tc * 2 + half) % 2 == 0 else pB
                bn = "bA" if (tc * 2 + half) % 2 == 0 else "bB"
                mmg(pm[:, 0:512], pairs, reads=[f"b1_{bi}", "wv", bn], writes=[bn], stop=False)
                mm(pm[:, 0:512], r_1bv[:], r_bvb[0:1, half * 512:(half + 1) * 512],
                   reads=["r_1bv", "r_bvb"], writes=[bn], start=False)
                acopy(a_b4[:, tc * 1024 + half * 512: tc * 1024 + (half + 1) * 512],
                      pm[:, 0:512], reads=[bn, "b4"], writes=["b4"])
        tt(a_b2[:, 0:W], a_b3[:, 0:W], a_b3[:, 0:W], ALU.mult,
           reads=["b3", "b2"], writes=["b2"])
        mmg(pD[0:16, 0:TILE],
            [(c_indb[:, ci * 16:(ci + 1) * 16], a_b2[:, ci * TILE:(ci + 1) * TILE])
             for ci in range(NCH)], reads=["b2", "cind", "bD"], writes=["bD"])
        act(AF.Identity, s_ds[0:16, 0:TILE], pD[0:16, 0:TILE], reads=["bD", "ns0", "ds"],
            writes=["ds"], scale=-1.0, bias=smc(C_NS0)[0:16, :])
        memset(pD[:, 0:40], 0.0, reads=["bD"], writes=["bD"])
        for tc in range(TC):
            petr(pC[:, 0:16], s_ds[0:16, tc * 128:(tc + 1) * 128], c_If[0:16, 0:16],
                 reads=["ds", "cIf", "bC"], writes=["bC"])
            acopy(s_q[:, tc * 16:(tc + 1) * 16], pC[:, 0:16], reads=["bC", "sq"], writes=["sq"])
        def emit_xp(h, tc):
            ci = h // 2
            r0 = (h % 2) * 64
            pj = c_pj[0:64, :] if r0 == 0 else c_pj[64:128, :]
            pm = pA if (h * TC + tc) % 2 == 0 else pB
            bn = "bA" if (h * TC + tc) % 2 == 0 else "bB"
            lhsT = a_b3[r0:r0 + 64, ci * TILE + tc * 128: ci * TILE + tc * 128 + 128]
            mm(pm[0:128, 0:M], lhsT, pj, reads=["b3", "pj_lo", "pj_hi", bn], writes=[bn])
            nslot = tc * HEADS + h
            dst = a_big[:, nslot * M:(nslot + 1) * M]
            act(AF.Exp, dst, pm[0:128, 0:M], reads=[bn, "sq"],
                writes=[f"B{nslot}"], bias=s_q[:, tc * 16 + h: tc * 16 + h + 1])
            vred(smc(C_RMQ + tc * 16 + h), pm[0:128, 0:M], ALU.max,
                 reads=[bn, "rmq"], writes=["rmq"] if (h == HEADS - 1 and tc == TC - 1) else [])

        def emit_ctx(h, tc):
            nslot = tc * HEADS + h
            for mc in range(2):
                col = h * 128 + mc * 64
                pcb = pc[col // 512]
                off = col % 512
                mm(pcb[:, off:off + 64],
                   a_big[:, nslot * M + mc * 128: nslot * M + (mc + 1) * 128],
                   a_b4[:, tc * 1024 + h * 64: tc * 1024 + (h + 1) * 64],
                   reads=[f"B{nslot}", "b4", f"bc{col // 512}"],
                   writes=[f"bc{col // 512}"], start=False, stop=(tc == TC - 1))
                mm(pD[:, mc * 16 + h: mc * 16 + h + 1],
                   a_big[:, nslot * M + mc * 128: nslot * M + (mc + 1) * 128],
                   c_oc[:], reads=[f"B{nslot}", "oc", "bD"],
                   writes=["bD"], start=False, stop=(tc == TC - 1))

        hts = [(h, tc) for h in range(HEADS) for tc in range(TC)]
        for i, (h, tc) in enumerate(hts):
            emit_xp(h, tc)
            if i > 0:
                emit_ctx(*hts[i - 1])
        emit_ctx(*hts[-1])
        for cj in range(NCH):
            for tc in range(TC):
                mm(pD[:, 32 + cj:32 + cj + 1],
                   a_b4[:, tc * 1024 + cj * 128: tc * 1024 + (cj + 1) * 128],
                   c_oc[:], reads=["b4", "oc", "bD"],
                   writes=["bD"], start=False, stop=(tc == TC - 1))
        for tc in range(TC):
            tt(smc(C_RM, 16), smc(C_RM, 16), smc(C_RMQ + tc * 16, 16), ALU.max,
               reads=["rm", "rmq"], writes=["rm"])
        for i in range(4):
            tt(s_ctx[:, i * 512:(i + 1) * 512], s_ctx[:, i * 512:(i + 1) * 512],
               pc[i][:, 0:512], ALU.add, reads=["ctxa", f"bc{i}"], writes=["ctxa"])
        tt(smc(C_S0, 32), smc(C_S0, 32), pD[:, 0:32], ALU.add,
           reads=["s0a", "bD"], writes=["s0a"])
        tt(smc(C_VS, 8), smc(C_VS, 8), pD[:, 32:40], ALU.add,
           reads=["vsa", "bD"], writes=["vsa"])
        if t < NT - 1:
            for i in range(4):
                memset(pc[i][:, 0:512], 0.0, reads=["ctxa"], writes=[f"bc{i}"])

    P.marks["1b"] = len(P.ops)
    ld(cc_in[:, 0:2048], s_ctx[:], reads=["ctxa"], writes=["cci0"])
    ld(cc_in[:, 2048:2080], smc(C_S0, 32), reads=["s0a"], writes=["cci1"])
    ld(cc_in[:, 2080:2088], smc(C_VS, 8), reads=["vsa"], writes=["cci2"])
    ts(smc(C_TMP, 16), smc(C_RM, 16), smc(1280, 1), None, ALU.mult,
       reads=["rm", "parc"], writes=["slA"])
    ld(cc_in[:, 2088:2104], smc(C_TMP, 16), reads=["slA"], writes=["cci3"])
    ts(smc(C_TMP, 16), smc(C_RM, 16), smc(1281, 1), None, ALU.mult,
       reads=["rm", "parc", "cci3", "slA"], writes=["slB"])
    ld(cc_in[:, 2104:2120], smc(C_TMP, 16), reads=["slB"], writes=["cci4"])
    P.op("gp", lambda eng: eng.collective_compute(
        "AllReduce", ALU.add, replica_groups=[[0, 1], [2, 3], [4, 5], [6, 7]],
        ins=[cc_in[:]], outs=[cc_out[:]]),
        reads=["cci0", "cci1", "cci2", "cci3", "cci4"], writes=["cco"], cc=True)
    ld(s_ctx[:], cc_out[:, 0:2048], reads=["cco", "ctxa"], writes=["ctxa"])
    ld(smc(C_S0, 32), cc_out[:, 2048:2080], reads=["cco", "s0a"], writes=["s0a"])
    ld(smc(C_VS, 8), cc_out[:, 2080:2088], reads=["cco", "vsa"], writes=["vsa"])
    ld(smc(C_TMP, 16), cc_out[:, 2088:2104], reads=["cco", "slB"], writes=["slA"])
    ld(smc(C_RM, 16), cc_out[:, 2104:2120], reads=["cco", "rm"], writes=["slB2"])
    tt(smc(C_TMP, 16), smc(C_TMP, 16), smc(C_RM, 16), ALU.max,
       reads=["slA", "slB2"], writes=["mx"])
    petr(pD[0:16, 0:128], smc(C_TMP, 16), c_If[:], reads=["mx", "cIf", "bD"], writes=["bD"])
    vred(smc(C_ST)[0:16, :], pD[0:16, 0:128], ALU.max, reads=["bD"], writes=["stab"])
    ts(smc(C_TMP)[0:16, :], smc(C_NS0)[0:16, :], -0.5, None, ALU.mult,
       reads=["ns0", "mx"], writes=["hb"])
    act(AF.Exp, smc(C_GH)[0:16, :], smc(C_ST)[0:16, :], reads=["stab", "hb"],
        writes=["ghc"], bias=smc(C_TMP)[0:16, :], scale=-0.5)
    petr(pD[0:1, 0:16], smc(C_GH)[0:16, :], c_If[0:16, 0:16],
         reads=["ghc", "cIf", "bD"], writes=["bD"])
    acopy(smr(R_G16, 16), pD[0:1, 0:16], reads=["bD"], writes=["ghr"])
    petr(pD[0:8, 0:128], smc(C_VS, 8), c_If[:], reads=["vsa", "cIf", "bD"], writes=["bD"])
    acopy(s_land[0:8, 0:128], pD[0:8, 0:128], reads=["bD"], writes=["vst", "rows8"])
    ld(r_vsl[:], s_land[0:8, 0:128], reads=["vst"], writes=["vsl", "f2t"])
    for i in range(4):
        vcopy(r_gch[:].rearrange("o (h r) -> o h r", r=128),
              smr(R_G16 + 4 * i, 4).unsqueeze(2).broadcast_to([1, 4, 128]),
              reads=["ghr", "gch"], writes=["gch"])
        vcopy(r_vch[:].rearrange("o (h m e) -> o h m e", m=2, e=64),
              r_vsl[0:1, 4 * i * 64:(4 * i + 4) * 64].rearrange("o (h e) -> o h e", e=64)
              .unsqueeze(2).broadcast_to([1, 4, 2, 64]),
              reads=["vsl", "vchb"], writes=["vchb"])
        ts(r_vch[:], r_vch[:], EPS_K, None, ALU.mult, reads=["vchb"], writes=["vchb"])
        mm(pC[:, 0:512], r_one[:], r_gch[:], reads=["gch", "r_1", "bC"], writes=["bC"])
        mm(pD[:, 0:512], r_one[:], r_vch[:], reads=["vchb", "r_1", "bD"], writes=["bD"])
        tt(s_ctx[:, i * 512:(i + 1) * 512], s_ctx[:, i * 512:(i + 1) * 512], pC[:, 0:512],
           ALU.mult, reads=["ctxa", "bC"], writes=["ctxa"])
        tt(s_ctx[:, i * 512:(i + 1) * 512], s_ctx[:, i * 512:(i + 1) * 512], pC[:, 0:512],
           ALU.mult, reads=["ctxa", "bC"], writes=["ctxa"])
        tt(s_ctx[:, i * 512:(i + 1) * 512], s_ctx[:, i * 512:(i + 1) * 512], pD[:, 0:512],
           ALU.add, reads=["ctxa", "bD"], writes=["ctxa"])
    P.merge("f2t", ["vsl", "gch", "vchb"])
    tt(smr(R_G2, 16), smr(R_G16, 16), smr(R_G16, 16), ALU.mult, reads=["ghr"], writes=["g2r"])
    vcopy(smr(C_G2E, 32).rearrange("o (m h) -> o m h", m=2),
          smr(R_G2, 16).unsqueeze(1).broadcast_to([1, 2, 16]), reads=["g2r"], writes=["g2e"])
    mm(pC[:, 0:32], r_one[:], smr(C_G2E, 32), reads=["g2e", "r_1", "bC"], writes=["bC"])
    tt(smc(C_KS, 32), smc(C_S0, 32), pC[:, 0:32], ALU.mult, reads=["s0a", "bC"], writes=["ks"])
    ts(smc(C_KS, 32), smc(C_KS, 32), EPS_K * 2 * TOK, None, ALU.add, reads=["ks"], writes=["ks"])
    vcopy(s_caug[:, :].rearrange("p (g c) -> p g c", c=65)[:, :, 0:64],
          s_ctx[:, 0:2048].rearrange("p (g e) -> p g e", e=64),
          reads=["ctxa"], writes=["caug"])
    vcopy(s_caug[:, :].rearrange("p (h m c) -> p h m c", h=16, m=2)[:, :, :, 64:65],
          smc(C_KS, 32).rearrange("p (m h) -> p h m", m=2).unsqueeze(3),
          reads=["ks", "caug"], writes=["caug"])
    for h in range(HEADS):
        for mc in range(2):
            mm(pD[0:65, h:h + 1], s_caug[:, (h * 2 + mc) * 65:(h * 2 + mc + 1) * 65],
               c_oc[:], reads=["caug", "oc", "bD"],
               writes=["bD"], start=(h == 0 and mc == 0), stop=(h == HEADS - 1 and mc == 1))
    act(AF.Copy, s_ecs[:], pD[0:65, 0:16], reads=["bD"], writes=["ecs"], scale=EPS_K)
    petr(pC[0:16, 0:1], s_ecs[64:65, 0:16], c_If[64:65, 64:65],
         reads=["ecs", "cIf", "bC"], writes=["bC"])
    acopy(smc(1600)[0:16, :], pC[0:16, 0:1], reads=["bC"], writes=["e64"])
    ld(r_nb[:], brows[5:6, 0:128], reads=["xt"], writes=["r_nb"], q="gp")
    memset(c_dmask[:], 0.0, reads=["xt"], writes=["dmask"])
    for g in range(32):
        vcopy(c_dmask[:, 16 * g + g // 2: 16 * g + g // 2 + 1],
              s_caug[:, g * 65 + 64: g * 65 + 65], reads=["caug", "dmask"],
              writes=["dmask"] if g == 31 else [])
    ld(c_wq[:].rearrange("p (c o) -> p c o", o=1024), wqT[:].rearrange("(c p) o -> p c o", p=128),
       reads=["wk", "b3"], writes=["wq"], q="gp")
    ld(c_wo[:].rearrange("p (c o) -> p c o", o=1024), woT[:].rearrange("(c p) o -> p c o", p=128),
       reads=["wv", "b4"], writes=["wo"], q="gp")

    P.marks["cc"] = len(P.ops)
    P.merge("b1_1", ["vst", "vsl", "rows8"])
    memset(a_xt[0:1, 0:1], 0.0,
           writes=["xt", "w1p0", "w1p1", "xr0", "xr1", "out0", "out1", "ssb", "nm",
                   "att", "ctxa", "dsl0", "dsl1", "dwp", "dwn", "gld"])
    dsslot = [r_ds0, r_ds1]
    for t in range(NT):
        bi = t % 2
        xnb = a_b1[bi]
        ld(xnb[:, 0:W], xn_scr[:, t * W:(t + 1) * W],
           reads=[f"xnscr{t}"], writes=[f"b1_{bi}"])
        for ci in range(NCH):
            pairs = [(c_wq[:, cj * 1024 + ci * 128: cj * 1024 + ci * 128 + 128],
                      xnb[:, cj * TILE:(cj + 1) * TILE]) for cj in range(NCH)]
            pm = pA if ci % 2 == 0 else pB
            bn = "bA" if ci % 2 == 0 else "bB"
            mmg(pm[:, 0:TILE], pairs, reads=[f"b1_{bi}", "wq", bn], writes=[bn])
            act(AF.Identity, a_b2[:, ci * TILE:(ci + 1) * TILE], pm[:, 0:TILE],
                reads=[bn, "bcols", "b2"], writes=["b2"],
                bias=c_bcols[:, 0 * NCH + ci:0 * NCH + ci + 1])
        tt(a_b4[:, 0:W], a_b2[:, 0:W], a_b2[:, 0:W], ALU.mult,
           reads=["b2", "b4"], writes=["b4"])
        mmg(pC[0:16, 0:TILE],
            [(c_indb[:, ci * 16:(ci + 1) * 16], a_b4[:, ci * TILE:(ci + 1) * TILE])
             for ci in range(NCH)], reads=["b4", "cind", "bC"], writes=["bC"])
        acopy(s_ds[0:16, 0:TILE], pC[0:16, 0:TILE], reads=["bC", "ds"], writes=["ds"])
        for h in range(HEADS):
            ci = h // 2
            r0 = (h % 2) * 64
            pj = c_pj[0:64, :] if r0 == 0 else c_pj[64:128, :]
            for tc in range(TC):
                pm = pA if (h * TC + tc) % 2 == 0 else pB
                bn = "bA" if (h * TC + tc) % 2 == 0 else "bB"
                lhsT = a_b2[r0:r0 + 64, ci * TILE + tc * 128: ci * TILE + tc * 128 + 128]
                mm(pm[0:128, 0:M], lhsT, pj, reads=["b2", "pj_lo", "pj_hi", bn], writes=[bn])
                vred(smc(C_RMQ + tc * 16 + h), pm[0:128, 0:M], ALU.max,
                     reads=[bn, "rmq"], writes=["rmq"] if (h == HEADS - 1 and tc == TC - 1) else [])
        for tc in range(TC):
            petr(pC[0:16, 128:256], smc(C_RMQ + tc * 16, 16), c_If[:],
                 reads=["rmq", "cIf", "bC"], writes=["bC"])
            tt(s_ds[0:16, tc * 128:(tc + 1) * 128], s_ds[0:16, tc * 128:(tc + 1) * 128],
               pC[0:16, 128:256], ALU.add, reads=["ds", "bC"], writes=["ds"])
        for h in range(HEADS):
            ci = h // 2
            r0 = (h % 2) * 64
            pj = c_pj[0:64, :] if r0 == 0 else c_pj[64:128, :]
            sl = f"dsl{h % 2}"
            ld(r_db[h % 2][:], s_ds[h:h + 1, 0:TILE], reads=["ds", sl], writes=[sl],
               q="gp")
            for mc in range(2):
                pm = pA if mc == 0 else pB
                bn = "bA" if mc == 0 else "bB"
                eslot = 2 * (h * 2 + mc)
                mm(pm[:, 0:TILE], pj[:, mc * 128:(mc + 1) * 128],
                   a_b2[r0:r0 + 64, ci * TILE: ci * TILE + TILE],
                   reads=["b2", "pj_lo", "pj_hi", bn], writes=[bn], stop=False)
                mm(pm[:, 0:TILE], r_nb[:], r_db[h % 2][:],
                   reads=["r_nb", sl], writes=[bn], start=False)
                act(AF.Exp, a_big[:, (h * 2 + mc) * TILE:(h * 2 + mc + 1) * TILE],
                    pm[:, 0:TILE], reads=[bn],
                    writes=[f"B{eslot}", f"B{eslot + 1}"])
        memset(pc[0][0:16, 0:TILE], 0.0, reads=["bc0"], writes=["bc0"])
        for g in range(32):
            mm(pc[0][0:16, 0:TILE], c_dmask[:, 16 * g:16 * g + 16],
               a_big[:, g * TILE:(g + 1) * TILE],
               reads=["dmask", f"B{2 * g}", f"B{2 * g + 1}", "bc0"], writes=["bc0"],
               start=False, stop=(g == 31))
        ts(s_ds[0:16, 0:TILE], pc[0][0:16, 0:TILE], smc(1600)[0:16, :], None,
           ALU.add, reads=["e64", "bc0", "ds"], writes=["ds"])
        recipv(s_ds[0:16, 0:TILE], s_ds[0:16, 0:TILE], reads=["ds"], writes=["ds"])
        for h in range(HEADS):
            bn = "bC" if h % 2 == 0 else "bD"
            pm = pC if h % 2 == 0 else pD
            for mc in range(2):
                eslot = 2 * (h * 2 + mc)
                mm(pm[0:65, 0:TILE], s_caug[:, (h * 2 + mc) * 65:(h * 2 + mc + 1) * 65],
                   a_big[:, (h * 2 + mc) * TILE:(h * 2 + mc + 1) * TILE],
                   reads=["caug", f"B{eslot}", f"B{eslot + 1}", bn], writes=[bn],
                   start=(mc == 0), stop=(mc == 1))
            act(AF.Identity, a_nm[:], pm[0:65, 0:TILE], reads=[bn, "ecs"],
                writes=["nm"] + (["ssb"] if h == 0 else []), bias=s_ecs[:, h:h + 1])
            sl = f"dsl{h % 2}"
            ld(dsslot[h % 2][:], s_ds[h:h + 1, 0:TILE], reads=["ds", sl], writes=[sl],
               q="gp")
            pm2 = pD if h % 2 == 0 else pC
            bn2 = "bD" if h % 2 == 0 else "bC"
            mm(pm2[0:64, 0:TILE], r_one[0:1, 0:64], dsslot[h % 2][:],
               reads=["r_1", sl, bn2], writes=[bn2])
            tt(s_att[(h % 2) * 64:(h % 2) * 64 + 64, (h // 2) * TILE:(h // 2 + 1) * TILE],
               a_nm[0:64, :], pm2[0:64, 0:TILE], ALU.mult,
               reads=["nm", bn2, "att"], writes=["att"])
        for ci in range(NCH):
            pairs = [(c_wo[:, cj * 1024 + ci * 128: cj * 1024 + ci * 128 + 128],
                      s_att[:, cj * TILE:(cj + 1) * TILE]) for cj in range(NCH)]
            pm = pA if ci % 2 == 0 else pB
            bn = "bA" if ci % 2 == 0 else "bB"
            mmg(pm[:, 0:TILE], pairs, reads=["att", "wo", bn], writes=[bn])
            stt(a_f1[:, ci * TILE:(ci + 1) * TILE], pm[:, 0:TILE],
                c_bcols[:, 3 * NCH + ci:3 * NCH + ci + 1], xnb[:, ci * TILE:(ci + 1) * TILE],
                ALU.add, ALU.add, reads=[bn, "bcols", f"b1_{bi}", "f1t", "w2pb0", "w2pb1"],
                writes=["f1t"])
        acopy(a_big[:, 0:W], a_f1[:, 0:W], reads=["f1t"] + BS(0, W), writes=BS(0, W))
        tt(a_big[:, W:2 * W], a_big[:, 0:W], a_big[:, 0:W], ALU.mult,
           reads=BS(0, W) + BS(W, 2 * W), writes=BS(W, 2 * W))
        mmg(pC[0:1, 0:TILE], [(c_oc[:], a_big[:, ci * TILE:(ci + 1) * TILE]) for ci in range(NCH)],
            reads=BS(0, W) + ["oc", "bC"], writes=["bC"])
        mmg(pD[0:1, 0:TILE], [(c_oc[:], a_big[:, W + ci * TILE:W + (ci + 1) * TILE])
                                for ci in range(NCH)],
            reads=BS(W, 2 * W) + ["bD"], writes=["bD"])

        def ln_smalls():
            mu, var = smr(SM_MU, TILE), smr(SM_VAR, TILE)
            ts(mu, pC[0:1, 0:TILE], 1.0 / D, None, ALU.mult, reads=["bC"], writes=["mu", "dact", "s0t", "d2pr", "d2nr", "sctmp", "stab0", "ghr", "g2r"])
            ts(var, pD[0:1, 0:TILE], 1.0 / D, None, ALU.mult, reads=["bD"], writes=["var", "s0t", "wbm", "bm", "sc", "wsc"])
            tt(r_lnA[:], mu, mu, ALU.mult, reads=["mu", "lnA"], writes=["lnA"])
            tt(var, var, r_lnA[:], ALU.subtract, reads=["var", "lnA"], writes=["var"])
            ts(var, var, LN_EPS, None, ALU.add, reads=["var"], writes=["var"])
            newton_rsqrt(r_lnA[:], var, r_lnB[0:1, :], ["var", "lnB0"], "lnA")
            tt(r_lnB[0:1, :], mu, r_lnA[:], ALU.mult, reads=["mu", "lnA", "lnB0"], writes=["lnB0"])
            ts(r_lnB[0:1, :], r_lnB[0:1, :], -1.0, None, ALU.mult, reads=["lnB0"], writes=["lnB0"])
        ln_smalls()
        mm(pC[:, 0:TILE], r_one[:], r_lnA[:], reads=["lnA", "r_1", "bC"], writes=["bC"])
        mm(pD[:, 0:TILE], r_one[:], r_lnB[0:1, :], reads=["lnB0", "r_1", "bD"], writes=["bD"])
        for ci in range(NCH):
            tt(a_f2[:, ci * TILE:(ci + 1) * TILE], a_f1[:, ci * TILE:(ci + 1) * TILE],
               pC[:, 0:TILE], ALU.mult, reads=["f1t", "bC", "f2t"], writes=["f2t"])
            tt(a_f2[:, ci * TILE:(ci + 1) * TILE], a_f2[:, ci * TILE:(ci + 1) * TILE],
               pD[:, 0:TILE], ALU.add, reads=["bD", "f2t"], writes=["f2t"])
            ts(a_f2[:, ci * TILE:(ci + 1) * TILE], a_f2[:, ci * TILE:(ci + 1) * TILE],
               c_gcols[:, ci:ci + 1], c_gcols[:, 8 + ci:9 + ci], ALU.mult, ALU.add,
               reads=["gcols", "f2t"], writes=["f2t"], q="gp")
        acopy(a_b3[:, 0:W], a_f2[:, 0:W], reads=["f2t", "b3"], writes=["b3"])
        for fo in range(32):
            pb = fo % 2
            ld(a_w1p[pb][:], w1_scr[:, fo * D:(fo + 1) * D],
               reads=["w1s"], writes=[f"w1p{pb}"])
            pairs = [(a_w1p[pb][:, cj * 128:(cj + 1) * 128],
                      a_b3[:, cj * TILE:(cj + 1) * TILE]) for cj in range(NCH)]
            pm = pA if fo % 2 == 0 else pB
            bn = "bA" if fo % 2 == 0 else "bB"
            mmg(pm[:, 0:TILE], pairs, reads=[f"w1p{pb}", "b3", bn], writes=[bn])
            act(AF.Relu, a_big[:, fo * TILE:(fo + 1) * TILE], pm[:, 0:TILE],
                reads=[bn, "bcols"] + BS(fo * TILE, (fo + 1) * TILE),
                writes=BS(fo * TILE, (fo + 1) * TILE),
                bias=c_bcols[:, 6 * NCH + fo:6 * NCH + fo + 1])
        for ci in range(NCH):
            pb = ci % 2
            ld(a_w2p[pb][:], w2_scr[:, ci * DFF:(ci + 1) * DFF],
               reads=["w2s"], writes=[f"w2pb{pb}", "f1t"])
            pairs = [(a_w2p[pb][:, fo * 128:(fo + 1) * 128],
                      a_big[:, fo * TILE:(fo + 1) * TILE]) for fo in range(32)]
            pm = pA if ci % 2 == 0 else pB
            bn = "bA" if ci % 2 == 0 else "bB"
            mmg(pm[:, 0:TILE], pairs,
                reads=[f"w2pb{pb}", bn] + BS(0, 16384), writes=[bn])
            stt(a_f3[:, ci * TILE:(ci + 1) * TILE], pm[:, 0:TILE],
                c_bcols[:, 4 * NCH + ci:4 * NCH + ci + 1], a_f2[:, ci * TILE:(ci + 1) * TILE],
                ALU.add, ALU.add, reads=[bn, "bcols", "f2t", "f3t"], writes=["f3t"])
        acopy(a_big[:, 0:W], a_f3[:, 0:W], reads=["f3t"] + BS(0, W), writes=BS(0, W))
        tt(a_big[:, W:2 * W], a_big[:, 0:W], a_big[:, 0:W], ALU.mult,
           reads=BS(0, W) + BS(W, 2 * W), writes=BS(W, 2 * W))
        mmg(pC[0:1, 0:TILE], [(c_oc[:], a_big[:, ci * TILE:(ci + 1) * TILE]) for ci in range(NCH)],
            reads=BS(0, W) + ["oc", "bC"], writes=["bC"])
        mmg(pD[0:1, 0:TILE], [(c_oc[:], a_big[:, W + ci * TILE:W + (ci + 1) * TILE])
                                for ci in range(NCH)],
            reads=BS(W, 2 * W) + ["bD"], writes=["bD"])
        ln_smalls()
        mm(pC[:, 0:TILE], r_one[:], r_lnA[:], reads=["lnA", "r_1", "bC"], writes=["bC"])
        mm(pD[:, 0:TILE], r_one[:], r_lnB[0:1, :], reads=["lnB0", "r_1", "bD"], writes=["bD"])
        ld(a_g[:], g_scr[:, t * W:t * W + 2048], reads=[f"gscr{t}"],
           writes=["att"])
        for ci in range(NCH):
            if ci == 4:
                ld(a_g[:], g_scr[:, t * W + 2048:t * W + 4096],
                   reads=[f"gscr{t}"], writes=["att"])
            ob = 0
            ld(a_xres[ob][:], xT[ci * 128:(ci + 1) * 128, t * TILE:(t + 1) * TILE],
               reads=[f"xr{ob}"], writes=[f"xr{ob}"])
            tt(a_ss[:], a_f3[:, ci * TILE:(ci + 1) * TILE], pC[:, 0:TILE], ALU.mult,
               reads=["f3t", "bC"], writes=["ssb"] + (["nm"] if ci == 0 else []))
            tt(a_ss[:], a_ss[:], pD[:, 0:TILE], ALU.add, reads=["ssb", "bD"], writes=["ssb"])
            ts(a_ss[:], a_ss[:], c_gcols[:, 16 + ci:17 + ci], c_gcols[:, 24 + ci:25 + ci],
               ALU.mult, ALU.add, reads=["gcols", "ssb"], writes=["ssb"], q="gp")
            tt(a_out[ob][:], a_ss[:], a_g[:, (ci % 4) * 512:(ci % 4 + 1) * 512], ALU.mult,
               reads=["ssb", "att", f"out{ob}"], writes=[f"out{ob}"])
            tt(a_out[ob][:], a_out[ob][:], a_xres[ob][:], ALU.add,
               reads=[f"out{ob}", f"xr{ob}"], writes=[f"out{ob}"])
            ld(outT[ci * 128:(ci + 1) * 128, t * TILE:(t + 1) * TILE], a_out[ob][:],
               reads=[f"out{ob}"], writes=[f"ost{ci % 2}"])

    P.marks["end"] = len(P.ops)
    if cut is not None:
        P.ops = P.ops[:cut]
        cnt = {s: 0 for s in P.count}
        for (_q, _e, _w, _tick, _s) in P.ops:
            cnt[_tick[0]] = max(cnt.get(_tick[0], 0), _tick[1])
        P.count = cnt
        P.bufs = {}
        ld(outT[0:128, 0:TILE], a_f1[:, 0:TILE], writes=["fin"])
        P.op("dve", lambda eng: eng.engine_nop(), reads=["fin"])
    else:
        P.op("dve", lambda eng: eng.engine_nop(), reads=["ost0", "ost1"])
    globals()["LAST_PROG"] = P

    from contextlib import ExitStack
    with ExitStack() as _ctx:
        sems = {s: _ctx.enter_context(nc.semaphore("sem_" + s)) for s in P.count}
        blk = _ctx.enter_context(nc.Block())
        P.emit(blk, sems)
    return nc


def _host_layout(inputs, TOK=4096):
    c = np.ascontiguousarray
    f = np.float32
    video = np.asarray(inputs["L0_attn_video"], f)
    B, N, _ = video.shape
    base = {
        "wqT": c(np.asarray(inputs["L0_attn_Wq"], f).T),
        "wkT": c(np.asarray(inputs["L0_attn_Wk"], f).T),
        "wvT": c(np.asarray(inputs["L0_attn_Wv"], f).T),
        "woT": c(np.asarray(inputs["L0_attn_Wo"], f).T),
        "w1T": c(np.asarray(inputs["L0_attn_W1"], f).T),
        "w2T": c(np.asarray(inputs["L0_attn_W2"], f).T),
        "wfn": c(np.asarray(inputs["L0_ffn_weight"], f)),
        "projTd": c(np.asarray(inputs["L0_attn_proj"], f).T),
        "dact": np.asarray(inputs["L0_ffn_delta_act"], f).reshape(1, 1),
        "cI": np.eye(128, dtype=f),
    }
    bc = np.zeros((128, 80), f)
    for j, k in enumerate(["L0_attn_bq", "L0_attn_bk", "L0_attn_bv", "L0_attn_bo",
                           "L0_attn_b2", "L0_ffn_bias"]):
        bc[:, j * 8:(j + 1) * 8] = np.asarray(inputs[k], f).reshape(8, 128).T
    bc[:, 48:80] = np.asarray(inputs["L0_attn_b1"], f).reshape(32, 128).T
    base["bcols"] = bc
    br = np.zeros((8, 1024), f)
    br[0] = np.asarray(inputs["L0_attn_bv"], f)
    br[1] = np.asarray(inputs["L0_attn_g2"], f)
    br[2] = np.asarray(inputs["L0_attn_be2"], f)
    br[3] = np.asarray(inputs["L0_attn_g3"], f)
    br[4] = np.asarray(inputs["L0_attn_be3"], f)
    br[5, 0:128] = -1.0
    br[6, 0:128] = 1.0
    br[7, :] = 1.0
    base["brows"] = br
    gc = np.zeros((128, 32), f)
    for j, k in enumerate(["L0_attn_g2", "L0_attn_be2", "L0_attn_g3", "L0_attn_be3"]):
        gc[:, j * 8:(j + 1) * 8] = np.asarray(inputs[k], f).reshape(8, 128).T
    base["gcols"] = gc
    dw = np.asarray(inputs["L0_ffn_delta_w"], f)
    base["dwb"] = c(dw.reshape(8, 128, 32).transpose(1, 0, 2).reshape(128, 256))
    ind = np.zeros((128, 128), f)
    for ci in range(8):
        for r in range(128):
            ind[r, ci * 16 + 2 * ci + r // 64] = 0.0
    ind[:] = 0.0
    for ci in range(8):
        for r in range(128):
            h = 2 * ci + r // 64
            ind[r, ci * 16 + h] = 0.0625
    base["cind"] = ind

    maps = []
    for core in range(8):
        b, half = core // 2, core % 2
        im = dict(base)
        im["xT"] = c(video[b, half * TOK:(half + 1) * TOK, :].T)
        im["par"] = np.array([[1.0, 0.0]] if half == 0 else [[0.0, 1.0]], f)
        maps.append(im)
    return maps


def kernel(**inputs):
    TOK = 4096
    nc = build_program(TOK=TOK, TILE=512)
    in_maps = _host_layout(inputs, TOK)
    res = run_bass_kernel_spmd(nc, in_maps, core_ids=list(range(8)))
    video = np.asarray(inputs["L0_attn_video"])
    B, N, Dm = video.shape
    out = np.empty((B, N, Dm), np.float32)
    for core in range(8):
        b, half = core // 2, core % 2
        out[b, half * TOK:(half + 1) * TOK, :] = res.results[core]["outT"].T
    return out


# version 08
# speedup vs baseline: 1.0160x; 1.0090x over previous
import numpy as np

import concourse.bass as bass
import concourse.mybir as mybir
from concourse.bass_utils import run_bass_kernel_spmd

F32 = mybir.dt.float32
BF16 = mybir.dt.bfloat16
I32 = mybir.dt.int32
AF = mybir.ActivationFunctionType
ALU = mybir.AluOpType
AX = mybir.AxisListType

D = 1024
HEADS = 16
DH = 64
M = 256
DFF = 4096
NCH = 8
EPS_K = 1e-4
LN_EPS = 1e-5
NORM = DH ** -0.25
LN2 = float(np.log(2.0))

QUEUES = ("sd", "gp", "pe", "act", "dve")
ENG_OF = {"sd": "sync", "gp": "gpsimd", "pe": "tensor", "act": "scalar", "dve": "vector"}


SEMS = ("sd", "gp", "gpd", "cc", "pe", "act", "dve")


class Prog:
    def __init__(self):
        self.ops = []
        self.count = {s: 0 for s in SEMS}
        self.bufs = {}

    def buf(self, name):
        if name not in self.bufs:
            self.bufs[name] = {"w": [], "r": []}
        return self.bufs[name]

    def op(self, q, emit, reads=(), writes=(), extra=(), dma=False, cc=False):
        waits = [w for w in extra if w is not None]
        for b in reads:
            waits += self.buf(b)["w"]
        for b in writes:
            waits += self.buf(b)["r"] + self.buf(b)["w"]
        if cc:
            sem, step = "cc", 1
        elif dma:
            sem, step = "D_" + (writes[0] if writes else "misc"), 16
        else:
            sem, step = q, 1
        if sem not in self.count:
            self.count[sem] = 0
        self.count[sem] += step
        tick = (sem, self.count[sem])
        uniq = {}
        for (ws, wv) in waits:
            if ws == sem and wv >= tick[1]:
                raise RuntimeError(f"self-deadlock {sem} {wv} {tick}")
            uniq[ws] = max(uniq.get(ws, 0), wv)
        self.ops.append((q, emit, [(k, v) for k, v in uniq.items()], tick, step))
        for b in reads:
            self.buf(b)["r"].append(tick)
        for b in writes:
            self.bufs[b] = {"w": [tick], "r": []}
        return tick

    def merge(self, dst, srcs):
        d = self.buf(dst)
        for s in srcs:
            b = self.buf(s)
            d["w"] = d["w"] + b["w"]
            d["r"] = d["r"] + b["r"]

    def emit(self, blk, sems):
        needed = set()
        for _, _, waits, _, _ in self.ops:
            needed.update(waits)
        per_q = {q: [] for q in QUEUES}
        semq = {}
        for (q, emit_fn, waits, tick, step) in self.ops:
            per_q[q].append((emit_fn, waits, tick, step))
            assert semq.setdefault(tick[0], q) == q, f"sem {tick[0]} on two queues"
        last_inc = {s: 0 for s in self.count}

        def make_body(q):
            ops = per_q[q]

            def body(eng):
                waited = {s: 0 for s in self.count}
                for (emit_fn, waits, tick, step) in ops:
                    for (ws, wv) in waits:
                        if wv > waited[ws]:
                            eng.wait_ge(sems[ws], wv)
                            waited[ws] = wv
                    inst = emit_fn(eng)
                    if tick in needed or step == 16 or tick[0] == "cc":
                        assert inst is not None, f"ticketed op returned None on {q}"
                        inst.then_inc(sems[tick[0]], tick[1] - last_inc[tick[0]])
                        last_inc[tick[0]] = tick[1]
            return body

        for q in QUEUES:
            if per_q[q]:
                getattr(blk, ENG_OF[q])(make_body(q))


LAST_PROG = None


def build_program(TOK=4096, TILE=512, cut=None):
    NT = TOK // TILE
    TC = TILE // 128
    W = NCH * TILE
    NB = W // 32
    nc = bass.Bass("TRN2", target_bir_lowering=False, debug=False)

    def din(name, shape, dtype=F32):
        return nc.dram_tensor(name, list(shape), dtype, kind="ExternalInput").ap()

    xT = din("xT", [D, TOK])
    wqT = din("wqT", [D, D]); wkT = din("wkT", [D, D]); wvT = din("wvT", [D, D])
    woT = din("woT", [D, D])
    w1T = din("w1T", [D, DFF]); w2T = din("w2T", [DFF, D])
    wfn = din("wfn", [D, D])
    projTd = din("projTd", [DH, M])
    bcols = din("bcols", [128, 80])
    brows = din("brows", [8, 1024])
    dwb = din("dwb", [128, 256])
    dact = din("dact", [1, 1])
    cI = din("cI", [128, 128])
    cind = din("cind", [128, 128])
    par = din("par", [1, 2])
    gcols = din("gcols", [128, 32])

    outT = nc.dram_tensor("outT", [D, TOK], F32, kind="ExternalOutput").ap()

    xn_scr = nc.dram_tensor("xn_scr", [128, NT * W], BF16, kind="Internal").ap()
    g_scr = nc.dram_tensor("g_scr", [128, NT * W], F32, kind="Internal").ap()
    w1_scr = nc.dram_tensor("w1_scr", [128, 32 * D], BF16, kind="Internal").ap()
    w2_scr = nc.dram_tensor("w2_scr", [128, NCH * DFF], BF16, kind="Internal").ap()
    CCW = 2120
    cc_in = nc.dram_tensor("cc_in", [128, CCW], F32, kind="Internal").ap()
    cc_out = nc.dram_tensor("cc_out", [128, CCW], F32, kind="Internal").ap()

    B0 = 16512

    def S(name, shape, dtype, off):
        assert B0 + off + int(np.prod(shape[1:])) * mybir.dt.size(dtype) <= 229376
        return nc.alloc_sbuf_tensor_at(name, list(shape), dtype, offset=B0 + off)

    a_f1 = S("a_f1", [128, 4096], F32, 0)
    a_w2p = [S(f"a_w2p{i}", [128, 4096], BF16, i * 8192) for i in range(2)]
    a_f2 = S("a_f2", [128, 4096], F32, 16384)
    r_vsl = S("r_vsl", [1, 1024], F32, 16384)
    r_gch = S("r_gch", [1, 512], F32, 20480)
    r_vch = S("r_vch", [1, 512], F32, 22528)
    a_f3 = S("a_f3", [128, 4096], F32, 32768)
    a_xt = S("a_xt", [128, 4096], F32, 49152)
    a_w1p = [S(f"a_w1p{i}", [128, 1024], BF16, 49152 + i * 2048) for i in range(2)]
    a_out = [S("a_out0", [128, 512], F32, 53248)]
    a_xres = [S("a_xres0", [128, 512], F32, 55296)]
    a_ss = S("a_ss", [128, 512], F32, 57344)
    a_nm = S("a_nm", [65, 512], F32, 57344)
    r_ds0 = S("r_ds0", [1, 512], F32, 59392)
    a_b1 = [S(f"a_b1{i}", [128, 4096], BF16, 65536 + i * 8192) for i in range(2)]
    c_dwp = S("c_dwp", [128, 256], F32, 73728)
    c_dwn = S("c_dwn", [128, 256], F32, 74752)
    s_land = S("s_land", [8, 1024], F32, 75776)
    a_b2 = S("a_b2", [128, 4096], BF16, 81920)
    a_b3 = S("a_b3", [128, 4096], BF16, 90112)
    a_b4 = S("a_b4", [128, 4096], BF16, 98304)
    a_i1 = S("a_i1", [128, 2048], I32, 98304)
    a_big = S("a_big", [128, 16384], BF16, 106496)
    a_wqh = S("a_wqh", [128, 4096], F32, 106496)
    c_wqt = S("c_wqt", [128, 8192], BF16, 122880)
    c_wk = S("c_wk", [128, 8192], BF16, 139264)
    c_wq = S("c_wq", [128, 8192], BF16, 139264)
    c_wv = S("c_wv", [128, 8192], BF16, 155648)
    c_wo = S("c_wo", [128, 8192], BF16, 155648)
    s_ctx = S("s_ctx", [128, 2048], F32, 172032)
    s_xst = S("s_xst", [128, 4096], BF16, 172032)
    s_att = S("s_att", [128, 4096], BF16, 172032)
    a_g = S("a_g", [128, 2048], F32, 172032)
    s_caug = S("s_caug", [128, 2080], BF16, 180224)
    s_ds = S("s_ds", [16, 512], F32, 184384)
    s_sm = S("s_sm", [128, 1664], F32, 186432)
    r_ds1 = S("r_ds1", [1, 512], F32, 61440)
    r_one = S("r_one", [1, 128], F32, 193088)
    r_neg1 = S("r_neg1", [1, 128], F32, 193600)
    r_bv = S("r_bv", [1, 1024], F32, 194112)
    r_g2b2 = S("r_g2b2", [2, 1024], F32, 198208)
    r_g3b3 = S("r_g3b3", [2, 1024], F32, 202304)
    r_lnA = S("r_lnA", [1, 512], F32, 206400)
    r_lnB = S("r_lnB", [2, 512], F32, 208448)
    c_pj = S("c_pj", [128, 256], BF16, 210496)
    c_Ib = S("c_Ib", [128, 128], BF16, 211008)
    c_If = S("c_If", [128, 128], F32, 211264)
    c_indb = S("c_indb", [128, 128], BF16, 211776)
    c_bcols = S("c_bcols", [128, 80], F32, 212032)
    s_ecs = S("s_ecs", [65, 16], F32, 212352)
    c_oc = S("c_oc", [128, 1], BF16, 212416)
    c_ocf = S("c_ocf", [128, 1], F32, 212448)
    c_gcols = S("c_gcols", [128, 32], F32, 212480)
    c_dmask = S("c_dmask", [128, 512], BF16, 63488)
    r_db = [S(f"r_db{i}", [1, 512], BF16, 59392 + i * 2048) for i in range(2)]
    r_nb = S("r_nb", [1, 128], BF16, 64512)
    r_bvb = S("r_bvb", [1, 1024], BF16, 194112)
    r_1bv = S("r_1bv", [1, 128], BF16, 196160)

    SM_MU, SM_VAR = 0, 512
    C_RMQ, C_RM, C_S0, C_VS = 1024, 1088, 1104, 1136
    C_D2P, C_D2N, C_NS0, C_GH, C_ST = 1144, 1145, 1146, 1147, 1148
    C_TMP = 1152
    C_KS = 1184
    C_PAR = 1536
    C_G2E = 1540
    R_G16, R_G2, R_S0B = 0, 16, 40
    s_q = s_sm[:, 1216:1280]

    nc.bump_sbuf(B0 + 212608 - nc.sbuf_base)

    psum = {n: nc.alloc_psum_tensor(n, [128, 512], F32)
            for n in ("pA", "pB", "pC", "pD", "p0", "p1", "p2", "p3")}
    pA, pB, pC, pD = psum["pA"], psum["pB"], psum["pC"], psum["pD"]
    pc = [psum[f"p{i}"] for i in range(4)]

    P = Prog()
    P.marks = {}

    def ld(dst, src, reads=(), writes=(), extra=(), q="sd"):
        return P.op(q, lambda eng, d=dst, s=src: eng.dma_start(out=d, in_=s),
                    reads=reads, writes=writes, extra=extra, dma=True)

    def mm(out_ap, l, r, reads=(), writes=(), extra=(), start=True, stop=True, tp=None):
        def emit(eng):
            kw = {"tile_position": tp} if tp is not None else {}
            return eng.matmul(out_ap, l, r, start=start, stop=stop,
                              skip_group_check=True, **kw)
        return P.op("pe", emit, reads=reads, writes=writes, extra=extra)

    def mmg(out_ap, pairs, reads=(), writes=(), extra=(), start=True, stop=True, tp=None):
        def emit(eng):
            inst = None
            n = len(pairs)
            kw = {"tile_position": tp} if tp is not None else {}
            for i, (l, r) in enumerate(pairs):
                inst = eng.matmul(out_ap, l, r, start=(start and i == 0),
                                  stop=(stop and i == n - 1), skip_group_check=True, **kw)
            return inst
        return P.op("pe", emit, reads=reads, writes=writes, extra=extra)

    def rl(*aps):
        return [f"rl_{ap.tensor.name}" for ap in aps
                if type(ap.tensor).__name__ == "PSumTensorHandle"]

    def act(fn, out, in_, reads=(), writes=(), extra=(), **kw):
        return P.op("act", lambda eng: eng.activation(out, in_, fn, **kw),
                    reads=reads, writes=list(writes) + rl(in_), extra=extra)

    def acopy(out, in_, reads=(), writes=(), extra=(), scale=None):
        if scale is None:
            return P.op("act", lambda eng: eng.copy(out, in_),
                        reads=reads, writes=list(writes) + rl(in_), extra=extra)
        return P.op("act", lambda eng: eng.activation(out, in_, AF.Copy, scale=scale),
                    reads=reads, writes=list(writes) + rl(in_), extra=extra)

    def ts(out, in_, s1, s2, op0, op1=None, reads=(), writes=(), extra=(), q="dve"):
        if op1 is None:
            return P.op(q, lambda eng: eng.tensor_scalar(out, in_, s1, None, op0=op0),
                        reads=reads, writes=list(writes) + rl(in_), extra=extra)
        return P.op(q, lambda eng: eng.tensor_scalar(out, in_, s1, s2, op0=op0, op1=op1),
                    reads=reads, writes=list(writes) + rl(in_), extra=extra)

    def tt(out, a, b, alu, reads=(), writes=(), extra=(), q="dve"):
        return P.op(q, lambda eng: eng.tensor_tensor(out, a, b, op=alu),
                    reads=reads, writes=list(writes) + rl(a, b), extra=extra)

    def stt(out, in0, scalar, in1, op0, op1, reads=(), writes=(), extra=(), q="dve"):
        return P.op(q, lambda eng: eng.scalar_tensor_tensor(out, in0, scalar, in1,
                                                            op0=op0, op1=op1),
                    reads=reads, writes=list(writes) + rl(in0, in1), extra=extra)

    def vcopy(out, in_, reads=(), writes=(), extra=(), q="dve"):
        return P.op(q, lambda eng: eng.tensor_copy(out, in_),
                    reads=reads, writes=list(writes) + rl(in_), extra=extra)

    def vred(out, in_, alu, reads=(), writes=(), extra=(), absval=None):
        return P.op("dve", lambda eng: eng.tensor_reduce(out, in_, axis=AX.X, op=alu,
                                                         apply_absolute_value=absval),
                    reads=reads, writes=list(writes) + rl(in_), extra=extra)

    def vtrans(out, in_, reads=(), writes=(), extra=()):
        return P.op("dve", lambda eng: eng.transpose(out, in_),
                    reads=reads, writes=list(writes) + rl(in_), extra=extra)

    def memset(ap, val, reads=(), writes=(), extra=(), q="dve"):
        return P.op(q, lambda eng: eng.memset(ap, val), reads=reads, writes=writes, extra=extra)

    def recipv(out, in_, reads=(), writes=(), extra=()):
        return P.op("dve", lambda eng: eng.reciprocal(out, in_),
                    reads=reads, writes=list(writes) + rl(in_), extra=extra)

    def petr(out, in_, ident, reads=(), writes=(), extra=()):
        return P.op("pe", lambda eng: eng.transpose(out, in_, ident),
                    reads=reads, writes=writes, extra=extra)

    smc = lambda c0, n=1: s_sm[:, c0:c0 + n]
    smr = lambda c0, n: s_sm[0:1, c0:c0 + n]

    def newton_rsqrt(dst, src, tmp, rbufs, wbuf):
        ts(dst.bitcast(I32), src.bitcast(I32), 0x7F800000, 23,
           ALU.bitwise_and, ALU.logical_shift_right, reads=rbufs, writes=[wbuf])
        ts(dst.bitcast(I32), dst.bitcast(I32), -1, 380, ALU.mult, ALU.add,
           reads=[wbuf], writes=[wbuf])
        ts(dst.bitcast(I32), dst.bitcast(I32), 1, 23,
           ALU.logical_shift_right, ALU.logical_shift_left, reads=[wbuf], writes=[wbuf])
        for _ in range(6):
            tt(tmp, dst, dst, ALU.mult, reads=[wbuf], writes=["rsq_tmp"])
            tt(tmp, tmp, src, ALU.mult, reads=["rsq_tmp"] + rbufs, writes=["rsq_tmp"])
            ts(tmp, tmp, -0.5, 1.5, ALU.mult, ALU.add, reads=["rsq_tmp"], writes=["rsq_tmp"])
            tt(dst, dst, tmp, ALU.mult, reads=[wbuf, "rsq_tmp"], writes=[wbuf])

    def fp4_round(tin, iout, tmpf, rbuf, ibuf, obuf):
        ts(iout, tin.bitcast(I32), 0x7F800000, 23,
           ALU.bitwise_and, ALU.logical_shift_right, reads=[rbuf], writes=[ibuf])
        ts(iout, iout, 127, 0, ALU.subtract, ALU.max, reads=[ibuf], writes=[ibuf])
        ts(iout, iout, 2, None, ALU.min, reads=[ibuf], writes=[ibuf])
        ts(iout, iout, 149, None, ALU.add, reads=[ibuf], writes=[ibuf])
        ts(iout, iout, 23, 0x400000, ALU.logical_shift_left, ALU.bitwise_or,
           reads=[ibuf], writes=[ibuf])
        tt(tmpf, tin, iout.bitcast(F32), ALU.add, reads=[rbuf, ibuf], writes=[obuf])
        tt(tmpf, tmpf, iout.bitcast(F32), ALU.subtract, reads=[obuf, ibuf], writes=[obuf])

    def tanh_2pow(dst_p, dst_n, src, t1, t2, t3, rbufs, wp, wn, tb):
        ts(t1, src, 0.5, None, ALU.mult, reads=rbufs, writes=[tb])
        tt(t2, t1, t1, ALU.mult, reads=[tb], writes=[tb])
        ts(t3, t2, -17.0 / 315.0, 2.0 / 15.0, ALU.mult, ALU.add, reads=[tb], writes=[tb])
        tt(t3, t3, t2, ALU.mult, reads=[tb], writes=[tb])
        ts(t3, t3, -1.0 / 3.0, None, ALU.add, reads=[tb], writes=[tb])
        tt(t3, t3, t2, ALU.mult, reads=[tb], writes=[tb])
        ts(t3, t3, 1.0, None, ALU.add, reads=[tb], writes=[tb])
        tt(dst_p, t3, t1, ALU.mult, reads=[tb], writes=[wp])
        for _ in range(3):
            tt(t2, dst_p, dst_p, ALU.mult, reads=[wp], writes=[tb])
            ts(t2, t2, 1.0, None, ALU.add, reads=[tb], writes=[tb])
            recipv(t3, t2, reads=[tb], writes=[tb])
            tt(dst_p, dst_p, t3, ALU.mult, reads=[wp, tb], writes=[wp])
            ts(dst_p, dst_p, 2.0, None, ALU.mult, reads=[wp], writes=[wp])
        ts(t1, dst_p, 0.5 * LN2, None, ALU.mult, reads=[wp], writes=[tb])

        def expw(dst, sgn, wd):
            ts(dst, t1, sgn / 9.0, 1.0, ALU.mult, ALU.add, reads=[tb], writes=[wd])
            for kk in range(8, 0, -1):
                tt(t2, dst, t1, ALU.mult, reads=[wd, tb], writes=[tb + "x"])
                ts(dst, t2, sgn / kk, 1.0, ALU.mult, ALU.add, reads=[tb + "x"], writes=[wd])
        expw(dst_p, 1.0, wp)
        expw(dst_n, -1.0, wn)
    def BS(c0, c1):
        return [f"B{i}" for i in range(c0 // 256, (c1 + 255) // 256)]

    P.op("act", lambda eng: eng.preload_act_table(AF.Silu))
    ld(c_If[:], cI[:], writes=["cIf"])
    ld(c_Ib[:], cI[:], writes=["cIb"], q="gp")
    ld(c_indb[:], cind[:], writes=["cind"], q="gp")
    ld(c_bcols[:], bcols[:], writes=["bcols"])
    ld(c_gcols[:], gcols[:], writes=["gcols"])
    ld(c_dwp[:], dwb[:], writes=["dwraw"])
    ld(smr(512, 1), dact[:], writes=["dact"])
    ld(smr(C_PAR, 2), par[:], writes=["par"])
    ld(s_sm[0:64, 0:256], projTd[:], writes=["pjf"])
    ld(s_land[:], brows[:], writes=["rows8"])
    ld(r_bvb[:], s_land[0:1, :], reads=["rows8"], writes=["r_bvb"], q="gp")
    ld(r_1bv[:], s_land[6:7, 0:128], reads=["rows8"], writes=["r_1bv"], q="gp")
    ld(r_g2b2[0:1, :], s_land[1:2, :], reads=["rows8"], writes=["r_g2"])
    ld(r_g2b2[1:2, :], s_land[2:3, :], reads=["rows8"], writes=["r_b2"])
    ld(r_g3b3[0:1, :], s_land[3:4, :], reads=["rows8"], writes=["r_g3"])
    ld(r_g3b3[1:2, :], s_land[4:5, :], reads=["rows8"], writes=["r_b3"])
    ld(r_neg1[:], s_land[5:6, 0:128], reads=["rows8"], writes=["r_n1"])
    ld(r_one[:], s_land[6:7, 0:128], reads=["rows8"], writes=["r_1"])
    memset(c_oc[:], 1.0, writes=["oc"])
    memset(c_ocf[:], 1.0, writes=["ocf"])
    ld(r_lnB[1:2, :], brows[7:8, 0:512], writes=["lnB1"])

    ts(s_sm[0:64, 0:256], s_sm[0:64, 0:256], NORM, None, ALU.mult,
       reads=["pjf"], writes=["pjf"])
    acopy(c_pj[0:64, :], s_sm[0:64, 0:256], reads=["pjf"], writes=["pj_lo"])
    ld(c_pj[64:128, :], c_pj[0:64, :], reads=["pj_lo"], writes=["pj_hi"])
    tt(s_sm[0:64, 0:256], c_pj[0:64, :], c_pj[0:64, :], ALU.mult,
       reads=["pj_lo", "pjf"], writes=["pjsq"])
    mm(pD[0:1, 0:256], c_ocf[0:64, :], s_sm[0:64, 0:256],
       reads=["pjsq", "ocf", "bD"], writes=["bD"])
    ts(smr(520, 256), pD[0:1, 0:256], 0.5, None, ALU.mult, reads=["bD"], writes=["s0t"])
    vred(smr(R_S0B, 1), smr(520, 256), ALU.max, reads=["s0t"], writes=["stab0"])
    mm(pD[0:16, 0:1], r_one[0:1, 0:16], smr(R_S0B, 1),
       reads=["stab0", "r_1", "bD"], writes=["bD"])
    acopy(smc(C_NS0)[0:16, :], pD[0:16, 0:1], reads=["bD"], writes=["ns0"], scale=-1.0)

    P.merge("sctmp", ["s0t"])
    tanh_2pow(smr(530, 1), smr(531, 1), smr(512, 1), smr(532, 1), smr(533, 1), smr(534, 1),
              ["dact"], "d2pr", "d2nr", "sctmp")
    mm(pD[:, 0:2], r_one[:], smr(530, 2), reads=["d2pr", "d2nr", "r_1", "bD"], writes=["bD"])
    acopy(smc(C_D2P, 2), pD[:, 0:2], reads=["bD"], writes=["d2c"])

    mm(pD[:, 2:4], r_one[:], smr(C_PAR, 2), reads=["par", "r_1", "bD"], writes=["bD"])
    acopy(smc(1280, 2), pD[:, 2:4], reads=["bD"], writes=["parc"])

    tanh_2pow(c_dwp[:], c_dwn[:], c_dwp[:],
              a_f1[:, 0:256], a_f1[:, 256:512], a_f1[:, 512:768],
              ["dwraw"], "dwp", "dwn", "f1t")

    ld(c_wk[:].rearrange("p (c o) -> p c o", o=1024), wkT[:].rearrange("(c p) o -> p c o", p=128), writes=["wk"], q="gp")
    ld(c_wv[:].rearrange("p (c o) -> p c o", o=1024), wvT[:].rearrange("(c p) o -> p c o", p=128), writes=["wv"], q="gp")

    for half in range(2):
        ld(a_wqh[:].rearrange("p (c o) -> p c o", o=1024),
           wfn[:].rearrange("(c p) o -> p c o", p=128)[:, half * 4:(half + 1) * 4, :],
           writes=BS(0, 8192), q="gp")
        vred(s_sm[:, 768:896], a_wqh[:].rearrange("p (a b) -> p a b", b=32), ALU.max,
             absval=True, reads=BS(0, 8192), writes=["wbm"])
        ts(a_i1[:, 0:128], smc(768, 128).bitcast(I32), 0x7F800000, 23,
           ALU.bitwise_and, ALU.logical_shift_right, reads=["wbm", "i1t"], writes=["we8"])
        ts(a_f1[:, 768:896], smc(768, 128), 0.0, None, ALU.is_equal,
           reads=["wbm", "f1t"], writes=["wiz"])
        ts(a_i1[:, 128:256], a_i1[:, 0:128], 2, None, ALU.subtract, reads=["we8"], writes=["wsb"])
        ts(a_i1[:, 128:256], a_i1[:, 128:256], 23, None, ALU.logical_shift_left,
           reads=["wsb"], writes=["wsb"])
        ts(smc(896, 128), a_f1[:, 768:896], -1.0, 1.0, ALU.mult, ALU.add,
           reads=["wiz"], writes=["wsc"])
        tt(smc(896, 128), smc(896, 128), a_i1[:, 128:256].bitcast(F32), ALU.mult,
           reads=["wsc", "wsb"], writes=["wsc"])
        tt(smc(896, 128), smc(896, 128), a_f1[:, 768:896], ALU.add,
           reads=["wsc", "wiz"], writes=["wsc"])
        tt(smc(896, 128), smc(896, 128), c_dwp[:, half * 128:half * 128 + 128],
           ALU.mult, reads=["wsc", "dwp"], writes=["wsc"])
        ts(a_i1[:, 128:256], a_i1[:, 0:128], -1, 256, ALU.mult, ALU.add,
           reads=["we8", "wsc"], writes=["wsb"])
        ts(a_i1[:, 128:256], a_i1[:, 128:256], 23, None, ALU.logical_shift_left,
           reads=["wsb"], writes=["wsb"])
        ts(a_f1[:, 896:1024], a_f1[:, 768:896], -1.0, 1.0, ALU.mult, ALU.add,
           reads=["wiz"], writes=["wiv"])
        tt(a_f1[:, 896:1024], a_f1[:, 896:1024], a_i1[:, 128:256].bitcast(F32), ALU.mult,
           reads=["wiv", "wsb"], writes=["wiv"])
        tt(a_f1[:, 896:1024], a_f1[:, 896:1024], a_f1[:, 768:896], ALU.add,
           reads=["wiv", "wiz"], writes=["wiv"])
        tt(a_f1[:, 896:1024], a_f1[:, 896:1024],
           c_dwn[:, half * 128:half * 128 + 128], ALU.mult,
           reads=["wiv", "dwn"], writes=["wiv"])
        for ch in range(2):
            sl = slice(ch * 2048, (ch + 1) * 2048)
            bsl = slice(ch * 64, (ch + 1) * 64)
            tt(a_f2[:, sl].rearrange("p (a b) -> p a b", b=32),
               a_wqh[:, sl].rearrange("p (a b) -> p a b", b=32),
               a_f1[:, 896:1024][:, bsl].unsqueeze(2).broadcast_to([128, 64, 32]),
               ALU.mult, reads=BS(ch * 4096, (ch + 1) * 4096) + ["wiv", "f2t"], writes=["wxs"])
            ts(a_f2[:, sl], a_f2[:, sl], 6.0, -6.0, ALU.min, ALU.max,
               reads=["wxs"], writes=["wxs"])
            fp4_round(a_f2[:, sl], a_i1[:, 0:2048], a_f3[:, sl], "wxs", "wi1", "wr4")
            tt(a_b2[:, sl].rearrange("p (a b) -> p a b", b=32),
               a_f3[:, sl].rearrange("p (a b) -> p a b", b=32),
               smc(896, 128)[:, bsl].unsqueeze(2).broadcast_to([128, 64, 32]),
               ALU.mult, reads=["wr4", "wsc", "wqn"], writes=["wqn"])
        for ocl in range(4):
            oc = half * 4 + ocl
            for icn in range(NCH):
                w_in = a_b2[:, ocl * 1024 + icn * 128: ocl * 1024 + icn * 128 + 128]
                petr(pC[:, 0:64].bitcast(BF16), w_in, c_Ib[:],
                     reads=["wqn", "cIb", "bC"], writes=["bC"])
                dstc = icn * 1024 + oc * 128
                acopy(c_wqt[:, dstc:dstc + 128], pC[:, 0:64].bitcast(BF16),
                      reads=["bC"], writes=BS(8192 + dstc, 8192 + dstc + 128))
    WQT = BS(8192, 16384)

    for cj in range(NCH):
        ld(a_b3[:, 0:4096], w1T[cj * 128:(cj + 1) * 128, :], writes=["b3"], q="gp")
        ld(w1_scr[:].rearrange("p (f c o) -> p f c o", f=32, c=NCH)[:, :, cj, :],
           a_b3[:, 0:4096].rearrange("p (f o) -> p f o", o=128),
           reads=["b3"], writes=["w1s"] if cj == NCH - 1 else [])
    P.merge("b4", ["we8", "wsb", "wi1"])
    for fc in range(32):
        ld(a_b4[:, 0:1024], w2T[fc * 128:(fc + 1) * 128, :], writes=["b4"], q="gp")
        ld(w2_scr[:].rearrange("p (c f o) -> p c f o", c=NCH, f=32)[:, :, fc, :],
           a_b4[:, 0:1024].rearrange("p (c o) -> p c o", o=128),
           reads=["b4"], writes=["w2s"] if fc == 31 else [])
    P.marks["ph0"] = len(P.ops)
    for t in range(NT):
        ld(a_xt[:].rearrange("p (c n) -> p c n", c=NCH),
           xT[:, t * TILE:(t + 1) * TILE].rearrange("(c p) n -> p c n", p=128),
           writes=["xt"])
        acopy(a_b3[:, 0:W], a_xt[:], reads=["xt"], writes=["b3"])
        tt(a_b4[:, 0:W], a_b3[:, 0:W], a_b3[:, 0:W], ALU.mult,
           reads=["b3", "i1t"], writes=["b4"])
        mmg(pC[0:1, 0:TILE], [(c_oc[:], a_b3[:, ci * TILE:(ci + 1) * TILE]) for ci in range(NCH)],
            reads=["b3", "oc", "bC"], writes=["bC"])
        mmg(pD[0:1, 0:TILE], [(c_oc[:], a_b4[:, ci * TILE:(ci + 1) * TILE]) for ci in range(NCH)],
            reads=["b4", "bD"], writes=["bD"])
        mu, var = smr(SM_MU, TILE), smr(SM_VAR, TILE)
        ts(mu, pC[0:1, 0:TILE], 1.0 / D, None, ALU.mult, reads=["bC"], writes=["mu", "dact", "s0t", "d2pr", "d2nr", "sctmp", "stab0", "ghr", "g2r"])
        ts(var, pD[0:1, 0:TILE], 1.0 / D, None, ALU.mult, reads=["bD"], writes=["var", "s0t", "wbm", "bm", "sc", "wsc"])
        tt(r_lnA[:], mu, mu, ALU.mult, reads=["mu", "lnA"], writes=["lnA"])
        tt(var, var, r_lnA[:], ALU.subtract, reads=["var", "lnA"], writes=["var"])
        ts(var, var, LN_EPS, None, ALU.add, reads=["var"], writes=["var"])
        newton_rsqrt(r_lnA[:], var, r_lnB[0:1, :], ["var", "lnB0"], "lnA")
        tt(r_lnB[0:1, :], mu, r_lnA[:], ALU.mult, reads=["mu", "lnA", "lnB0"], writes=["lnB0"])
        ts(r_lnB[0:1, :], r_lnB[0:1, :], -1.0, None, ALU.mult, reads=["lnB0"], writes=["lnB0"])
        mm(pC[:, 0:TILE], r_one[:], r_lnA[:], reads=["lnA", "r_1", "bC"], writes=["bC"])
        mm(pD[:, 0:TILE], r_one[:], r_lnB[0:1, :], reads=["lnB0", "r_1", "bD"], writes=["bD"])
        for ci in range(NCH):
            tt(a_f1[:, ci * TILE:(ci + 1) * TILE], a_xt[:, ci * TILE:(ci + 1) * TILE],
               pC[:, 0:TILE], ALU.mult, reads=["xt", "bC", "f1t"], writes=["f1t"])
            tt(a_f1[:, ci * TILE:(ci + 1) * TILE], a_f1[:, ci * TILE:(ci + 1) * TILE],
               pD[:, 0:TILE], ALU.add, reads=["bD", "f1t"], writes=["f1t"])
        acopy(s_xst[:, 0:W], a_f1[:, 0:W], reads=["f1t", "xstg"], writes=["xstg"])
        ld(xn_scr[:, t * W:(t + 1) * W], s_xst[:, 0:W], reads=["xstg"], writes=[f"xnscr{t}"])
        for ci in range(NCH):
            vtrans(a_f2[:, ci * TILE:(ci + 1) * TILE], a_f1[:, ci * TILE:(ci + 1) * TILE],
                   reads=["f1t", "f2t"], writes=["f2t"])
        vred(s_sm[:, 768:768 + NB], a_f2[:, 0:W].rearrange("p (a b) -> p a b", b=32),
             ALU.max, absval=True, reads=["f2t", "wbm"], writes=["bm"])
        ts(a_i1[:, 0:NB], smc(768, NB).bitcast(I32), 0x7F800000, 23,
           ALU.bitwise_and, ALU.logical_shift_right, reads=["bm"], writes=["e8", "b4"])
        ts(a_f3[:, 0:NB], smc(768, NB), 0.0, None, ALU.is_equal,
           reads=["bm"], writes=["iz", "f3t"])
        ts(a_i1[:, NB:2 * NB], a_i1[:, 0:NB], 2, None, ALU.subtract, reads=["e8"], writes=["sb"])
        ts(a_i1[:, NB:2 * NB], a_i1[:, NB:2 * NB], 23, None, ALU.logical_shift_left,
           reads=["sb"], writes=["sb"])
        ts(smc(896, NB), a_f3[:, 0:NB], -1.0, 1.0, ALU.mult, ALU.add,
           reads=["iz", "wsc"], writes=["sc"])
        tt(smc(896, NB), smc(896, NB), a_i1[:, NB:2 * NB].bitcast(F32), ALU.mult,
           reads=["sc", "sb"], writes=["sc"])
        tt(smc(896, NB), smc(896, NB), a_f3[:, 0:NB], ALU.add, reads=["sc", "iz"], writes=["sc"])
        tt(smc(896, NB), smc(896, NB), smc(C_D2P).broadcast_to([128, NB]), ALU.mult,
           reads=["sc", "d2c"], writes=["sc"])
        ts(a_i1[:, NB:2 * NB], a_i1[:, 0:NB], -1, 256, ALU.mult, ALU.add,
           reads=["e8", "sc"], writes=["sb"])
        ts(a_i1[:, NB:2 * NB], a_i1[:, NB:2 * NB], 23, None, ALU.logical_shift_left,
           reads=["sb"], writes=["sb"])
        ts(a_f3[:, NB:2 * NB], a_f3[:, 0:NB], -1.0, 1.0, ALU.mult, ALU.add,
           reads=["iz"], writes=["iv"])
        tt(a_f3[:, NB:2 * NB], a_f3[:, NB:2 * NB], a_i1[:, NB:2 * NB].bitcast(F32), ALU.mult,
           reads=["iv", "sb"], writes=["iv"])
        tt(a_f3[:, NB:2 * NB], a_f3[:, NB:2 * NB], a_f3[:, 0:NB], ALU.add,
           reads=["iv", "iz"], writes=["iv"])
        tt(a_f3[:, NB:2 * NB], a_f3[:, NB:2 * NB], smc(C_D2N).broadcast_to([128, NB]),
           ALU.mult, reads=["iv", "d2c"], writes=["iv"])
        vcopy(a_f2[:, 0:W].rearrange("p (a b) -> p a b", b=32),
              a_f3[:, NB:2 * NB].unsqueeze(2).broadcast_to([128, NB, 32]),
              reads=["iv", "f2t"], writes=["f2t"])
        for ci in range(NCH):
            vtrans(a_f3[:, ci * TILE:(ci + 1) * TILE], a_f2[:, ci * TILE:(ci + 1) * TILE],
                   reads=["f2t", "iv", "iz", "f3t"], writes=["f3t"])
        tt(a_f2[:, 0:W], a_f1[:, 0:W], a_f3[:, 0:W], ALU.mult,
           reads=["f1t", "f3t", "f2t"], writes=["f2t"])
        ts(a_f2[:, 0:W], a_f2[:, 0:W], 6.0, -6.0, ALU.min, ALU.max,
           reads=["f2t"], writes=["f2t"])
        vcopy(a_f3[:, 0:W].rearrange("p (a b) -> p a b", b=32),
              smc(896, NB).unsqueeze(2).broadcast_to([128, NB, 32]),
              reads=["sc", "f3t"], writes=["f3t"])
        for ci in range(NCH):
            vtrans(a_f1[:, ci * TILE:(ci + 1) * TILE], a_f3[:, ci * TILE:(ci + 1) * TILE],
                   reads=["f3t", "f1t"], writes=["f1t"])
        for ch in range(2):
            sl = slice(ch * 2048, (ch + 1) * 2048)
            fp4_round(a_f2[:, sl], a_i1[:, 0:2048], a_f3[:, sl], "f2t", "i1t", "f3t")
        tt(a_b2[:, 0:W], a_f3[:, 0:W], a_f1[:, 0:W], ALU.mult,
           reads=["f3t", "f1t", "b2"], writes=["b2"])
        P.merge("b4", ["e8", "sb", "i1t"])
        for ci in range(NCH):
            pairs = [(c_wqt[:, cj * 1024 + ci * 128: cj * 1024 + ci * 128 + 128],
                      a_b2[:, cj * TILE:(cj + 1) * TILE]) for cj in range(NCH)]
            pm = pA if ci % 2 == 0 else pB
            bn = "bA" if ci % 2 == 0 else "bB"
            mmg(pm[:, 0:TILE], pairs, reads=["b2", bn] + WQT, writes=[bn])
            act(AF.Silu, a_f3[:, ci * TILE:(ci + 1) * TILE], pm[:, 0:TILE],
                reads=[bn, "bcols", "f3t"], writes=["f3t"],
                bias=c_bcols[:, 5 * NCH + ci:5 * NCH + ci + 1])
        ld(g_scr[:, t * W:(t + 1) * W], a_f3[:, 0:W], reads=["f3t"], writes=[f"gscr{t}"])

    P.marks["1a"] = len(P.ops)
    P.op("act", lambda eng: eng.preload_act_table(AF.Exp))
    memset(s_ctx[:], 0.0, writes=["ctxa", "xstg"])
    P.merge("b1_1", ["dwp", "dwn", "dwraw", "rows8", "vst"])
    memset(smc(C_RM, 16), -1e30, writes=["rm"])
    memset(smc(C_S0, 32), 0.0, writes=["s0a"])
    memset(smc(C_VS, 8), 0.0, writes=["vsa"])
    for i in range(4):
        memset(pc[i][:, 0:512], 0.0, writes=[f"bc{i}"])
    for t in range(NT):
        bi = t % 2
        xnb = a_b1[bi]
        ld(xnb[:, 0:W], xn_scr[:, t * W:(t + 1) * W],
           reads=[f"xnscr{t}"], writes=[f"b1_{bi}"])
        for ci in range(NCH):
            pairs = [(c_wk[:, cj * 1024 + ci * 128: cj * 1024 + ci * 128 + 128],
                      xnb[:, cj * TILE:(cj + 1) * TILE]) for cj in range(NCH)]
            pm = pA if ci % 2 == 0 else pB
            bn = "bA" if ci % 2 == 0 else "bB"
            mmg(pm[:, 0:TILE], pairs, reads=[f"b1_{bi}", "wk", bn], writes=[bn])
            act(AF.Identity, a_b3[:, ci * TILE:(ci + 1) * TILE], pm[:, 0:TILE],
                reads=[bn, "bcols", "b3"], writes=["b3"],
                bias=c_bcols[:, 1 * NCH + ci:1 * NCH + ci + 1])
        for tc in range(TC):
            for half in range(2):
                pairs = [(xnb[:, cj * TILE + tc * 128: cj * TILE + tc * 128 + 128],
                          c_wv[:, cj * 1024 + half * 512: cj * 1024 + (half + 1) * 512])
                         for cj in range(NCH)]
                pm = pA if (tc * 2 + half) % 2 == 0 else pB
                bn = "bA" if (tc * 2 + half) % 2 == 0 else "bB"
                mmg(pm[:, 0:512], pairs, reads=[f"b1_{bi}", "wv", bn], writes=[bn], stop=False)
                mm(pm[:, 0:512], r_1bv[:], r_bvb[0:1, half * 512:(half + 1) * 512],
                   reads=["r_1bv", "r_bvb"], writes=[bn], start=False)
                acopy(a_b4[:, tc * 1024 + half * 512: tc * 1024 + (half + 1) * 512],
                      pm[:, 0:512], reads=[bn, "b4"], writes=["b4"])
        tt(a_b2[:, 0:W], a_b3[:, 0:W], a_b3[:, 0:W], ALU.mult,
           reads=["b3", "b2"], writes=["b2"])
        mmg(pD[0:16, 0:TILE],
            [(c_indb[:, ci * 16:(ci + 1) * 16], a_b2[:, ci * TILE:(ci + 1) * TILE])
             for ci in range(NCH)], reads=["b2", "cind", "bD"], writes=["bD"])
        act(AF.Identity, s_ds[0:16, 0:TILE], pD[0:16, 0:TILE], reads=["bD", "ns0", "ds"],
            writes=["ds"], scale=-1.0, bias=smc(C_NS0)[0:16, :])
        memset(pD[:, 0:40], 0.0, reads=["bD"], writes=["bD"])
        for tc in range(TC):
            petr(pC[:, 0:16], s_ds[0:16, tc * 128:(tc + 1) * 128], c_If[0:16, 0:16],
                 reads=["ds", "cIf", "bC"], writes=["bC"])
            acopy(s_q[:, tc * 16:(tc + 1) * 16], pC[:, 0:16], reads=["bC", "sq"], writes=["sq"])
        def emit_xp(h, tc):
            ci = h // 2
            r0 = (h % 2) * 64
            pj = c_pj[0:64, :] if r0 == 0 else c_pj[64:128, :]
            pm = pA if (h * TC + tc) % 2 == 0 else pB
            bn = "bA" if (h * TC + tc) % 2 == 0 else "bB"
            lhsT = a_b3[r0:r0 + 64, ci * TILE + tc * 128: ci * TILE + tc * 128 + 128]
            mm(pm[0:128, 0:M], lhsT, pj, reads=["b3", "pj_lo", "pj_hi", bn], writes=[bn])
            nslot = tc * HEADS + h
            dst = a_big[:, nslot * M:(nslot + 1) * M]
            act(AF.Exp, dst, pm[0:128, 0:M], reads=[bn, "sq"],
                writes=[f"B{nslot}"], bias=s_q[:, tc * 16 + h: tc * 16 + h + 1])
            vred(smc(C_RMQ + tc * 16 + h), pm[0:128, 0:M], ALU.max,
                 reads=[bn, "rmq"], writes=["rmq"] if (h == HEADS - 1 and tc == TC - 1) else [])

        def emit_ctx(h, tc):
            nslot = tc * HEADS + h
            for mc in range(2):
                col = h * 128 + mc * 64
                pcb = pc[col // 512]
                off = col % 512
                mm(pcb[:, off:off + 64],
                   a_big[:, nslot * M + mc * 128: nslot * M + (mc + 1) * 128],
                   a_b4[:, tc * 1024 + h * 64: tc * 1024 + (h + 1) * 64],
                   reads=[f"B{nslot}", "b4", f"bc{col // 512}"],
                   writes=[f"bc{col // 512}"], start=False, stop=(tc == TC - 1))
                mm(pD[:, mc * 16 + h: mc * 16 + h + 1],
                   a_big[:, nslot * M + mc * 128: nslot * M + (mc + 1) * 128],
                   c_oc[:], reads=[f"B{nslot}", "oc", "bD"],
                   writes=["bD"], start=False, stop=(tc == TC - 1))

        hts = [(h, tc) for h in range(HEADS) for tc in range(TC)]
        for i, (h, tc) in enumerate(hts):
            emit_xp(h, tc)
            if i > 0:
                emit_ctx(*hts[i - 1])
        emit_ctx(*hts[-1])
        for cj in range(NCH):
            for tc in range(TC):
                mm(pD[:, 32 + cj:32 + cj + 1],
                   a_b4[:, tc * 1024 + cj * 128: tc * 1024 + (cj + 1) * 128],
                   c_oc[:], reads=["b4", "oc", "bD"],
                   writes=["bD"], start=False, stop=(tc == TC - 1))
        for tc in range(TC):
            tt(smc(C_RM, 16), smc(C_RM, 16), smc(C_RMQ + tc * 16, 16), ALU.max,
               reads=["rm", "rmq"], writes=["rm"])
        for i in range(4):
            tt(s_ctx[:, i * 512:(i + 1) * 512], s_ctx[:, i * 512:(i + 1) * 512],
               pc[i][:, 0:512], ALU.add, reads=["ctxa", f"bc{i}"], writes=["ctxa"])
        tt(smc(C_S0, 32), smc(C_S0, 32), pD[:, 0:32], ALU.add,
           reads=["s0a", "bD"], writes=["s0a"])
        tt(smc(C_VS, 8), smc(C_VS, 8), pD[:, 32:40], ALU.add,
           reads=["vsa", "bD"], writes=["vsa"])
        if t < NT - 1:
            for i in range(4):
                memset(pc[i][:, 0:512], 0.0, reads=["ctxa"], writes=[f"bc{i}"])

    P.marks["1b"] = len(P.ops)
    ld(cc_in[:, 0:2048], s_ctx[:], reads=["ctxa"], writes=["cci0"])
    ld(cc_in[:, 2048:2080], smc(C_S0, 32), reads=["s0a"], writes=["cci1"])
    ld(cc_in[:, 2080:2088], smc(C_VS, 8), reads=["vsa"], writes=["cci2"])
    ts(smc(C_TMP, 16), smc(C_RM, 16), smc(1280, 1), None, ALU.mult,
       reads=["rm", "parc"], writes=["slA"])
    ld(cc_in[:, 2088:2104], smc(C_TMP, 16), reads=["slA"], writes=["cci3"])
    ts(smc(C_TMP, 16), smc(C_RM, 16), smc(1281, 1), None, ALU.mult,
       reads=["rm", "parc", "cci3", "slA"], writes=["slB"])
    ld(cc_in[:, 2104:2120], smc(C_TMP, 16), reads=["slB"], writes=["cci4"])
    P.op("gp", lambda eng: eng.collective_compute(
        "AllReduce", ALU.add, replica_groups=[[0, 1], [2, 3], [4, 5], [6, 7]],
        ins=[cc_in[:]], outs=[cc_out[:]]),
        reads=["cci0", "cci1", "cci2", "cci3", "cci4"], writes=["cco"], cc=True)
    ld(s_ctx[:], cc_out[:, 0:2048], reads=["cco", "ctxa"], writes=["ctxa"])
    ld(smc(C_S0, 32), cc_out[:, 2048:2080], reads=["cco", "s0a"], writes=["s0a"])
    ld(smc(C_VS, 8), cc_out[:, 2080:2088], reads=["cco", "vsa"], writes=["vsa"])
    ld(smc(C_TMP, 16), cc_out[:, 2088:2104], reads=["cco", "slB"], writes=["slA"])
    ld(smc(C_RM, 16), cc_out[:, 2104:2120], reads=["cco", "rm"], writes=["slB2"])
    tt(smc(C_TMP, 16), smc(C_TMP, 16), smc(C_RM, 16), ALU.max,
       reads=["slA", "slB2"], writes=["mx"])
    petr(pD[0:16, 0:128], smc(C_TMP, 16), c_If[:], reads=["mx", "cIf", "bD"], writes=["bD"])
    vred(smc(C_ST)[0:16, :], pD[0:16, 0:128], ALU.max, reads=["bD"], writes=["stab"])
    ts(smc(C_TMP)[0:16, :], smc(C_NS0)[0:16, :], -0.5, None, ALU.mult,
       reads=["ns0", "mx"], writes=["hb"])
    act(AF.Exp, smc(C_GH)[0:16, :], smc(C_ST)[0:16, :], reads=["stab", "hb"],
        writes=["ghc"], bias=smc(C_TMP)[0:16, :], scale=-0.5)
    petr(pD[0:1, 0:16], smc(C_GH)[0:16, :], c_If[0:16, 0:16],
         reads=["ghc", "cIf", "bD"], writes=["bD"])
    acopy(smr(R_G16, 16), pD[0:1, 0:16], reads=["bD"], writes=["ghr"])
    petr(pD[0:8, 0:128], smc(C_VS, 8), c_If[:], reads=["vsa", "cIf", "bD"], writes=["bD"])
    acopy(s_land[0:8, 0:128], pD[0:8, 0:128], reads=["bD"], writes=["vst", "rows8"])
    ld(r_vsl[:], s_land[0:8, 0:128], reads=["vst"], writes=["vsl", "f2t"])
    for i in range(4):
        vcopy(r_gch[:].rearrange("o (h r) -> o h r", r=128),
              smr(R_G16 + 4 * i, 4).unsqueeze(2).broadcast_to([1, 4, 128]),
              reads=["ghr", "gch"], writes=["gch"])
        vcopy(r_vch[:].rearrange("o (h m e) -> o h m e", m=2, e=64),
              r_vsl[0:1, 4 * i * 64:(4 * i + 4) * 64].rearrange("o (h e) -> o h e", e=64)
              .unsqueeze(2).broadcast_to([1, 4, 2, 64]),
              reads=["vsl", "vchb"], writes=["vchb"])
        ts(r_vch[:], r_vch[:], EPS_K, None, ALU.mult, reads=["vchb"], writes=["vchb"])
        mm(pC[:, 0:512], r_one[:], r_gch[:], reads=["gch", "r_1", "bC"], writes=["bC"])
        mm(pD[:, 0:512], r_one[:], r_vch[:], reads=["vchb", "r_1", "bD"], writes=["bD"])
        tt(s_ctx[:, i * 512:(i + 1) * 512], s_ctx[:, i * 512:(i + 1) * 512], pC[:, 0:512],
           ALU.mult, reads=["ctxa", "bC"], writes=["ctxa"])
        tt(s_ctx[:, i * 512:(i + 1) * 512], s_ctx[:, i * 512:(i + 1) * 512], pC[:, 0:512],
           ALU.mult, reads=["ctxa", "bC"], writes=["ctxa"])
        tt(s_ctx[:, i * 512:(i + 1) * 512], s_ctx[:, i * 512:(i + 1) * 512], pD[:, 0:512],
           ALU.add, reads=["ctxa", "bD"], writes=["ctxa"])
    P.merge("f2t", ["vsl", "gch", "vchb"])
    tt(smr(R_G2, 16), smr(R_G16, 16), smr(R_G16, 16), ALU.mult, reads=["ghr"], writes=["g2r"])
    vcopy(smr(C_G2E, 32).rearrange("o (m h) -> o m h", m=2),
          smr(R_G2, 16).unsqueeze(1).broadcast_to([1, 2, 16]), reads=["g2r"], writes=["g2e"])
    mm(pC[:, 0:32], r_one[:], smr(C_G2E, 32), reads=["g2e", "r_1", "bC"], writes=["bC"])
    tt(smc(C_KS, 32), smc(C_S0, 32), pC[:, 0:32], ALU.mult, reads=["s0a", "bC"], writes=["ks"])
    ts(smc(C_KS, 32), smc(C_KS, 32), EPS_K * 2 * TOK, None, ALU.add, reads=["ks"], writes=["ks"])
    vcopy(s_caug[:, :].rearrange("p (g c) -> p g c", c=65)[:, :, 0:64],
          s_ctx[:, 0:2048].rearrange("p (g e) -> p g e", e=64),
          reads=["ctxa"], writes=["caug"])
    vcopy(s_caug[:, :].rearrange("p (h m c) -> p h m c", h=16, m=2)[:, :, :, 64:65],
          smc(C_KS, 32).rearrange("p (m h) -> p h m", m=2).unsqueeze(3),
          reads=["ks", "caug"], writes=["caug"])
    for h in range(HEADS):
        for mc in range(2):
            mm(pD[0:65, h:h + 1], s_caug[:, (h * 2 + mc) * 65:(h * 2 + mc + 1) * 65],
               c_oc[:], reads=["caug", "oc", "bD"],
               writes=["bD"], start=(h == 0 and mc == 0), stop=(h == HEADS - 1 and mc == 1))
    act(AF.Copy, s_ecs[:], pD[0:65, 0:16], reads=["bD"], writes=["ecs"], scale=EPS_K)
    petr(pC[0:16, 0:1], s_ecs[64:65, 0:16], c_If[64:65, 64:65],
         reads=["ecs", "cIf", "bC"], writes=["bC"])
    acopy(smc(1600)[0:16, :], pC[0:16, 0:1], reads=["bC"], writes=["e64"])
    ld(r_nb[:], brows[5:6, 0:128], reads=["xt"], writes=["r_nb"], q="gp")
    memset(c_dmask[:], 0.0, reads=["xt"], writes=["dmask"])
    for g in range(32):
        vcopy(c_dmask[:, 16 * g + g // 2: 16 * g + g // 2 + 1],
              s_caug[:, g * 65 + 64: g * 65 + 65], reads=["caug", "dmask"],
              writes=["dmask"] if g == 31 else [])
    ld(c_wq[:].rearrange("p (c o) -> p c o", o=1024), wqT[:].rearrange("(c p) o -> p c o", p=128),
       reads=["wk", "b3"], writes=["wq"], q="gp")
    ld(c_wo[:].rearrange("p (c o) -> p c o", o=1024), woT[:].rearrange("(c p) o -> p c o", p=128),
       reads=["wv", "b4"], writes=["wo"], q="gp")

    P.marks["cc"] = len(P.ops)
    P.merge("b1_1", ["vst", "vsl", "rows8"])
    memset(a_xt[0:1, 0:1], 0.0,
           writes=["xt", "w1p0", "w1p1", "xr0", "xr1", "out0", "out1", "ssb", "nm",
                   "att", "ctxa", "dsl0", "dsl1", "dwp", "dwn", "gld"])
    dsslot = [r_ds0, r_ds1]
    for t in range(NT):
        bi = t % 2
        xnb = a_b1[bi]
        ld(xnb[:, 0:W], xn_scr[:, t * W:(t + 1) * W],
           reads=[f"xnscr{t}"], writes=[f"b1_{bi}"])
        for ci in range(NCH):
            pairs = [(c_wq[:, cj * 1024 + ci * 128: cj * 1024 + ci * 128 + 128],
                      xnb[:, cj * TILE:(cj + 1) * TILE]) for cj in range(NCH)]
            pm = pA if ci % 2 == 0 else pB
            bn = "bA" if ci % 2 == 0 else "bB"
            mmg(pm[:, 0:TILE], pairs, reads=[f"b1_{bi}", "wq", bn], writes=[bn])
            act(AF.Identity, a_b2[:, ci * TILE:(ci + 1) * TILE], pm[:, 0:TILE],
                reads=[bn, "bcols", "b2"], writes=["b2"],
                bias=c_bcols[:, 0 * NCH + ci:0 * NCH + ci + 1])
        tt(a_b4[:, 0:W], a_b2[:, 0:W], a_b2[:, 0:W], ALU.mult,
           reads=["b2", "b4"], writes=["b4"])
        mmg(pC[0:16, 0:TILE],
            [(c_indb[:, ci * 16:(ci + 1) * 16], a_b4[:, ci * TILE:(ci + 1) * TILE])
             for ci in range(NCH)], reads=["b4", "cind", "bC"], writes=["bC"])
        acopy(s_ds[0:16, 0:TILE], pC[0:16, 0:TILE], reads=["bC", "ds"], writes=["ds"])
        for h in range(HEADS):
            ci = h // 2
            r0 = (h % 2) * 64
            pj = c_pj[0:64, :] if r0 == 0 else c_pj[64:128, :]
            for tc in range(TC):
                pm = pA if (h * TC + tc) % 2 == 0 else pB
                bn = "bA" if (h * TC + tc) % 2 == 0 else "bB"
                lhsT = a_b2[r0:r0 + 64, ci * TILE + tc * 128: ci * TILE + tc * 128 + 128]
                mm(pm[0:128, 0:M], lhsT, pj, reads=["b2", "pj_lo", "pj_hi", bn], writes=[bn])
                vred(smc(C_RMQ + tc * 16 + h), pm[0:128, 0:M], ALU.max,
                     reads=[bn, "rmq"], writes=["rmq"] if (h == HEADS - 1 and tc == TC - 1) else [])
        for tc in range(TC):
            petr(pC[0:16, 128:256], smc(C_RMQ + tc * 16, 16), c_If[:],
                 reads=["rmq", "cIf", "bC"], writes=["bC"])
            tt(s_ds[0:16, tc * 128:(tc + 1) * 128], s_ds[0:16, tc * 128:(tc + 1) * 128],
               pC[0:16, 128:256], ALU.add, reads=["ds", "bC"], writes=["ds"])
        for h in range(HEADS):
            ci = h // 2
            r0 = (h % 2) * 64
            pj = c_pj[0:64, :] if r0 == 0 else c_pj[64:128, :]
            sl = f"dsl{h % 2}"
            ld(r_db[h % 2][:], s_ds[h:h + 1, 0:TILE], reads=["ds", sl], writes=[sl],
               q="gp")
            for mc in range(2):
                pm = pA if mc == 0 else pB
                bn = "bA" if mc == 0 else "bB"
                eslot = 2 * (h * 2 + mc)
                mm(pm[:, 0:TILE], pj[:, mc * 128:(mc + 1) * 128],
                   a_b2[r0:r0 + 64, ci * TILE: ci * TILE + TILE],
                   reads=["b2", "pj_lo", "pj_hi", bn], writes=[bn], stop=False)
                mm(pm[:, 0:TILE], r_nb[:], r_db[h % 2][:],
                   reads=["r_nb", sl], writes=[bn], start=False)
                act(AF.Exp, a_big[:, (h * 2 + mc) * TILE:(h * 2 + mc + 1) * TILE],
                    pm[:, 0:TILE], reads=[bn],
                    writes=[f"B{eslot}", f"B{eslot + 1}"])
        memset(pc[0][0:16, 0:TILE], 0.0, reads=["bc0"], writes=["bc0"])
        for g in range(32):
            mm(pc[0][0:16, 0:TILE], c_dmask[:, 16 * g:16 * g + 16],
               a_big[:, g * TILE:(g + 1) * TILE],
               reads=["dmask", f"B{2 * g}", f"B{2 * g + 1}", "bc0"], writes=["bc0"],
               start=False, stop=(g == 31))
        ts(s_ds[0:16, 0:TILE], pc[0][0:16, 0:TILE], smc(1600)[0:16, :], None,
           ALU.add, reads=["e64", "bc0", "ds"], writes=["ds"])
        recipv(s_ds[0:16, 0:TILE], s_ds[0:16, 0:TILE], reads=["ds"], writes=["ds"])
        for h in range(HEADS):
            bn = "bC" if h % 2 == 0 else "bD"
            pm = pC if h % 2 == 0 else pD
            for mc in range(2):
                eslot = 2 * (h * 2 + mc)
                mm(pm[0:65, 0:TILE], s_caug[:, (h * 2 + mc) * 65:(h * 2 + mc + 1) * 65],
                   a_big[:, (h * 2 + mc) * TILE:(h * 2 + mc + 1) * TILE],
                   reads=["caug", f"B{eslot}", f"B{eslot + 1}", bn], writes=[bn],
                   start=(mc == 0), stop=(mc == 1))
            act(AF.Identity, a_nm[:], pm[0:65, 0:TILE], reads=[bn, "ecs"],
                writes=["nm"] + (["ssb"] if h == 0 else []), bias=s_ecs[:, h:h + 1])
            sl = f"dsl{h % 2}"
            ld(dsslot[h % 2][:], s_ds[h:h + 1, 0:TILE], reads=["ds", sl], writes=[sl],
               q="gp")
            pm2 = pD if h % 2 == 0 else pC
            bn2 = "bD" if h % 2 == 0 else "bC"
            mm(pm2[0:64, 0:TILE], r_one[0:1, 0:64], dsslot[h % 2][:],
               reads=["r_1", sl, bn2], writes=[bn2])
            tt(s_att[(h % 2) * 64:(h % 2) * 64 + 64, (h // 2) * TILE:(h // 2 + 1) * TILE],
               a_nm[0:64, :], pm2[0:64, 0:TILE], ALU.mult,
               reads=["nm", bn2, "att"], writes=["att"])
        for ci in range(NCH):
            pairs = [(c_wo[:, cj * 1024 + ci * 128: cj * 1024 + ci * 128 + 128],
                      s_att[:, cj * TILE:(cj + 1) * TILE]) for cj in range(NCH)]
            pm = pA if ci % 2 == 0 else pB
            bn = "bA" if ci % 2 == 0 else "bB"
            mmg(pm[:, 0:TILE], pairs, reads=["att", "wo", bn], writes=[bn])
            stt(a_f1[:, ci * TILE:(ci + 1) * TILE], pm[:, 0:TILE],
                c_bcols[:, 3 * NCH + ci:3 * NCH + ci + 1], xnb[:, ci * TILE:(ci + 1) * TILE],
                ALU.add, ALU.add, reads=[bn, "bcols", f"b1_{bi}", "f1t", "w2pb0", "w2pb1"],
                writes=["f1t"])
        acopy(a_big[:, 0:W], a_f1[:, 0:W], reads=["f1t"] + BS(0, W), writes=BS(0, W))
        act(AF.Square, a_big[:, W:2 * W], a_f1[:, 0:W],
            reads=["f1t"] + BS(W, 2 * W), writes=BS(W, 2 * W))
        mmg(pC[0:1, 0:TILE], [(c_oc[:], a_big[:, ci * TILE:(ci + 1) * TILE]) for ci in range(NCH)],
            reads=BS(0, W) + ["oc", "bC"], writes=["bC"])
        mmg(pD[0:1, 0:TILE], [(c_oc[:], a_big[:, W + ci * TILE:W + (ci + 1) * TILE])
                                for ci in range(NCH)],
            reads=BS(W, 2 * W) + ["bD"], writes=["bD"])

        def ln_smalls():
            mu, var = smr(SM_MU, TILE), smr(SM_VAR, TILE)
            ts(mu, pC[0:1, 0:TILE], 1.0 / D, None, ALU.mult, reads=["bC"], writes=["mu", "dact", "s0t", "d2pr", "d2nr", "sctmp", "stab0", "ghr", "g2r"])
            ts(var, pD[0:1, 0:TILE], 1.0 / D, None, ALU.mult, reads=["bD"], writes=["var", "s0t", "wbm", "bm", "sc", "wsc"])
            tt(r_lnA[:], mu, mu, ALU.mult, reads=["mu", "lnA"], writes=["lnA"])
            tt(var, var, r_lnA[:], ALU.subtract, reads=["var", "lnA"], writes=["var"])
            ts(var, var, LN_EPS, None, ALU.add, reads=["var"], writes=["var"])
            newton_rsqrt(r_lnA[:], var, r_lnB[0:1, :], ["var", "lnB0"], "lnA")
            tt(r_lnB[0:1, :], mu, r_lnA[:], ALU.mult, reads=["mu", "lnA", "lnB0"], writes=["lnB0"])
            ts(r_lnB[0:1, :], r_lnB[0:1, :], -1.0, None, ALU.mult, reads=["lnB0"], writes=["lnB0"])
        ln_smalls()
        mm(pC[:, 0:TILE], r_one[:], r_lnA[:], reads=["lnA", "r_1", "bC"], writes=["bC"])
        mm(pD[:, 0:TILE], r_one[:], r_lnB[0:1, :], reads=["lnB0", "r_1", "bD"], writes=["bD"])
        for ci in range(NCH):
            tt(a_f2[:, ci * TILE:(ci + 1) * TILE], a_f1[:, ci * TILE:(ci + 1) * TILE],
               pC[:, 0:TILE], ALU.mult, reads=["f1t", "bC", "f2t"], writes=["f2t"])
            tt(a_f2[:, ci * TILE:(ci + 1) * TILE], a_f2[:, ci * TILE:(ci + 1) * TILE],
               pD[:, 0:TILE], ALU.add, reads=["bD", "f2t"], writes=["f2t"])
            ts(a_f2[:, ci * TILE:(ci + 1) * TILE], a_f2[:, ci * TILE:(ci + 1) * TILE],
               c_gcols[:, ci:ci + 1], c_gcols[:, 8 + ci:9 + ci], ALU.mult, ALU.add,
               reads=["gcols", "f2t"], writes=["f2t"], q="gp")
        acopy(a_b3[:, 0:W], a_f2[:, 0:W], reads=["f2t", "b3"], writes=["b3"])
        for fo in range(32):
            pb = fo % 2
            ld(a_w1p[pb][:], w1_scr[:, fo * D:(fo + 1) * D],
               reads=["w1s"], writes=[f"w1p{pb}"])
            pairs = [(a_w1p[pb][:, cj * 128:(cj + 1) * 128],
                      a_b3[:, cj * TILE:(cj + 1) * TILE]) for cj in range(NCH)]
            pm = pA if fo % 2 == 0 else pB
            bn = "bA" if fo % 2 == 0 else "bB"
            mmg(pm[:, 0:TILE], pairs, reads=[f"w1p{pb}", "b3", bn], writes=[bn])
            act(AF.Relu, a_big[:, fo * TILE:(fo + 1) * TILE], pm[:, 0:TILE],
                reads=[bn, "bcols"] + BS(fo * TILE, (fo + 1) * TILE),
                writes=BS(fo * TILE, (fo + 1) * TILE),
                bias=c_bcols[:, 6 * NCH + fo:6 * NCH + fo + 1])
        for ci in range(NCH):
            pb = ci % 2
            ld(a_w2p[pb][:], w2_scr[:, ci * DFF:(ci + 1) * DFF],
               reads=["w2s"], writes=[f"w2pb{pb}", "f1t"])
            pairs = [(a_w2p[pb][:, fo * 128:(fo + 1) * 128],
                      a_big[:, fo * TILE:(fo + 1) * TILE]) for fo in range(32)]
            pm = pA if ci % 2 == 0 else pB
            bn = "bA" if ci % 2 == 0 else "bB"
            mmg(pm[:, 0:TILE], pairs,
                reads=[f"w2pb{pb}", bn] + BS(0, 16384), writes=[bn])
            stt(a_f3[:, ci * TILE:(ci + 1) * TILE], pm[:, 0:TILE],
                c_bcols[:, 4 * NCH + ci:4 * NCH + ci + 1], a_f2[:, ci * TILE:(ci + 1) * TILE],
                ALU.add, ALU.add, reads=[bn, "bcols", "f2t", "f3t"], writes=["f3t"])
        acopy(a_big[:, 0:W], a_f3[:, 0:W], reads=["f3t"] + BS(0, W), writes=BS(0, W))
        act(AF.Square, a_big[:, W:2 * W], a_f3[:, 0:W],
            reads=["f3t"] + BS(W, 2 * W), writes=BS(W, 2 * W))
        mmg(pC[0:1, 0:TILE], [(c_oc[:], a_big[:, ci * TILE:(ci + 1) * TILE]) for ci in range(NCH)],
            reads=BS(0, W) + ["oc", "bC"], writes=["bC"])
        mmg(pD[0:1, 0:TILE], [(c_oc[:], a_big[:, W + ci * TILE:W + (ci + 1) * TILE])
                                for ci in range(NCH)],
            reads=BS(W, 2 * W) + ["bD"], writes=["bD"])
        ln_smalls()
        mm(pC[:, 0:TILE], r_one[:], r_lnA[:], reads=["lnA", "r_1", "bC"], writes=["bC"])
        mm(pD[:, 0:TILE], r_one[:], r_lnB[0:1, :], reads=["lnB0", "r_1", "bD"], writes=["bD"])
        ld(a_g[:], g_scr[:, t * W:t * W + 2048], reads=[f"gscr{t}"],
           writes=["att"])
        for ci in range(NCH):
            if ci == 4:
                ld(a_g[:], g_scr[:, t * W + 2048:t * W + 4096],
                   reads=[f"gscr{t}"], writes=["att"])
            ob = 0
            ld(a_xres[ob][:], xT[ci * 128:(ci + 1) * 128, t * TILE:(t + 1) * TILE],
               reads=[f"xr{ob}"], writes=[f"xr{ob}"])
            tt(a_ss[:], a_f3[:, ci * TILE:(ci + 1) * TILE], pC[:, 0:TILE], ALU.mult,
               reads=["f3t", "bC"], writes=["ssb"] + (["nm"] if ci == 0 else []))
            tt(a_ss[:], a_ss[:], pD[:, 0:TILE], ALU.add, reads=["ssb", "bD"], writes=["ssb"])
            ts(a_ss[:], a_ss[:], c_gcols[:, 16 + ci:17 + ci], c_gcols[:, 24 + ci:25 + ci],
               ALU.mult, ALU.add, reads=["gcols", "ssb"], writes=["ssb"], q="gp")
            tt(a_out[ob][:], a_ss[:], a_g[:, (ci % 4) * 512:(ci % 4 + 1) * 512], ALU.mult,
               reads=["ssb", "att", f"out{ob}"], writes=[f"out{ob}"])
            tt(a_out[ob][:], a_out[ob][:], a_xres[ob][:], ALU.add,
               reads=[f"out{ob}", f"xr{ob}"], writes=[f"out{ob}"])
            ld(outT[ci * 128:(ci + 1) * 128, t * TILE:(t + 1) * TILE], a_out[ob][:],
               reads=[f"out{ob}"], writes=[f"ost{ci % 2}"])

    P.marks["end"] = len(P.ops)
    if cut is not None:
        P.ops = P.ops[:cut]
        cnt = {s: 0 for s in P.count}
        for (_q, _e, _w, _tick, _s) in P.ops:
            cnt[_tick[0]] = max(cnt.get(_tick[0], 0), _tick[1])
        P.count = cnt
        P.bufs = {}
        ld(outT[0:128, 0:TILE], a_f1[:, 0:TILE], writes=["fin"])
        P.op("dve", lambda eng: eng.engine_nop(), reads=["fin"])
    else:
        P.op("dve", lambda eng: eng.engine_nop(), reads=["ost0", "ost1"])
    globals()["LAST_PROG"] = P

    from contextlib import ExitStack
    with ExitStack() as _ctx:
        sems = {s: _ctx.enter_context(nc.semaphore("sem_" + s)) for s in P.count}
        blk = _ctx.enter_context(nc.Block())
        P.emit(blk, sems)
    return nc


def _host_layout(inputs, TOK=4096):
    c = np.ascontiguousarray
    f = np.float32
    video = np.asarray(inputs["L0_attn_video"], f)
    B, N, _ = video.shape
    base = {
        "wqT": c(np.asarray(inputs["L0_attn_Wq"], f).T),
        "wkT": c(np.asarray(inputs["L0_attn_Wk"], f).T),
        "wvT": c(np.asarray(inputs["L0_attn_Wv"], f).T),
        "woT": c(np.asarray(inputs["L0_attn_Wo"], f).T),
        "w1T": c(np.asarray(inputs["L0_attn_W1"], f).T),
        "w2T": c(np.asarray(inputs["L0_attn_W2"], f).T),
        "wfn": c(np.asarray(inputs["L0_ffn_weight"], f)),
        "projTd": c(np.asarray(inputs["L0_attn_proj"], f).T),
        "dact": np.asarray(inputs["L0_ffn_delta_act"], f).reshape(1, 1),
        "cI": np.eye(128, dtype=f),
    }
    bc = np.zeros((128, 80), f)
    for j, k in enumerate(["L0_attn_bq", "L0_attn_bk", "L0_attn_bv", "L0_attn_bo",
                           "L0_attn_b2", "L0_ffn_bias"]):
        bc[:, j * 8:(j + 1) * 8] = np.asarray(inputs[k], f).reshape(8, 128).T
    bc[:, 48:80] = np.asarray(inputs["L0_attn_b1"], f).reshape(32, 128).T
    base["bcols"] = bc
    br = np.zeros((8, 1024), f)
    br[0] = np.asarray(inputs["L0_attn_bv"], f)
    br[1] = np.asarray(inputs["L0_attn_g2"], f)
    br[2] = np.asarray(inputs["L0_attn_be2"], f)
    br[3] = np.asarray(inputs["L0_attn_g3"], f)
    br[4] = np.asarray(inputs["L0_attn_be3"], f)
    br[5, 0:128] = -1.0
    br[6, 0:128] = 1.0
    br[7, :] = 1.0
    base["brows"] = br
    gc = np.zeros((128, 32), f)
    for j, k in enumerate(["L0_attn_g2", "L0_attn_be2", "L0_attn_g3", "L0_attn_be3"]):
        gc[:, j * 8:(j + 1) * 8] = np.asarray(inputs[k], f).reshape(8, 128).T
    base["gcols"] = gc
    dw = np.asarray(inputs["L0_ffn_delta_w"], f)
    base["dwb"] = c(dw.reshape(8, 128, 32).transpose(1, 0, 2).reshape(128, 256))
    ind = np.zeros((128, 128), f)
    for ci in range(8):
        for r in range(128):
            ind[r, ci * 16 + 2 * ci + r // 64] = 0.0
    ind[:] = 0.0
    for ci in range(8):
        for r in range(128):
            h = 2 * ci + r // 64
            ind[r, ci * 16 + h] = 0.0625
    base["cind"] = ind

    maps = []
    for core in range(8):
        b, half = core // 2, core % 2
        im = dict(base)
        im["xT"] = c(video[b, half * TOK:(half + 1) * TOK, :].T)
        im["par"] = np.array([[1.0, 0.0]] if half == 0 else [[0.0, 1.0]], f)
        maps.append(im)
    return maps


def kernel(**inputs):
    TOK = 4096
    nc = build_program(TOK=TOK, TILE=512)
    in_maps = _host_layout(inputs, TOK)
    res = run_bass_kernel_spmd(nc, in_maps, core_ids=list(range(8)))
    video = np.asarray(inputs["L0_attn_video"])
    B, N, Dm = video.shape
    out = np.empty((B, N, Dm), np.float32)
    for core in range(8):
        b, half = core // 2, core % 2
        out[b, half * TOK:(half + 1) * TOK, :] = res.results[core]["outT"].T
    return out
